# version 13
# baseline (speedup 1.0000x reference)
import contextlib

import numpy as np
import ml_dtypes

import concourse.bass as bass
import concourse.mybir as mybir
from concourse.bass_utils import run_bass_kernel_spmd

F32 = mybir.dt.float32
BF16 = mybir.dt.bfloat16
I32 = mybir.dt.int32
U32 = mybir.dt.uint32
AF = mybir.ActivationFunctionType
ALU = mybir.AluOpType
AX = mybir.AxisListType
BF = ml_dtypes.bfloat16

NCORES = 8
D = 4096
RPC = 1024
HALF = 512
FQ = 2049
FQP = 2176
KCF = 32
KCQ = 17
NB_FWD = [(0, 512), (512, 512), (1024, 512), (1536, 512), (2048, 128)]
H = 11008
HPAD = 11264
NHC = 88
HC_REAL = 86
HPC = 11
EPS = 1e-5

NF4_CODE = np.array([-1.0, -0.6961928009986877, -0.5250730514526367,
                     -0.39491748809814453, -0.28444138169288635,
                     -0.18477343022823334, -0.09105003625154495, 0.0,
                     0.07958029955625534, 0.16093020141124725,
                     0.24611230194568634, 0.33791524171829224,
                     0.44070982933044434, 0.5626170039176941,
                     0.7229568362236023, 1.0], dtype=np.float32)
NF4_BOUND = ((NF4_CODE[:-1] + NF4_CODE[1:]) * 0.5).astype(np.float32)
NF4_E = ((NF4_CODE[1:] - NF4_CODE[:-1]) * 0.5).astype(np.float32)

SLAB_W = 128 * 4096
PIECE_W = 128 * 512
DQ1_ELEMS = HPC * SLAB_W
DQ3_ELEMS = NHC * PIECE_W
PHASE_SCALE = float(2 * np.pi / 4096)
TWO_PI = float(2 * np.pi)


def _host_consts():
    t = np.arange(D, dtype=np.float64)
    k = np.arange(FQP, dtype=np.float64)
    ang = 2 * np.pi * np.outer(t, k) / D
    C = np.cos(ang)
    S = np.sin(ang)
    C[:, FQ:] = 0.0
    S[:, FQ:] = 0.0
    w = np.full(FQP, 2.0)
    w[0] = 1.0
    w[2048] = 1.0
    w[FQ:] = 0.0
    angi = 2 * np.pi * np.outer(k, t) / D
    CiC = (w[:, None] * np.cos(angi)) / D
    CiS = (-w[:, None] * np.sin(angi)) / D
    Cc = np.ascontiguousarray(C.reshape(KCF, 128, FQP)).astype(BF)
    Sc = np.ascontiguousarray(S.reshape(KCF, 128, FQP)).astype(BF)
    CiCc = np.ascontiguousarray(CiC.reshape(KCQ, 128, D)).astype(BF)
    CiSc = np.ascontiguousarray(CiS.reshape(KCQ, 128, D)).astype(BF)
    ident = np.eye(128, dtype=BF)
    bcol = np.zeros((128, 16), np.float32)
    bcol[:, :15] = -NF4_BOUND[None, :]
    return Cc, Sc, CiCc, CiSc, ident, bcol


class KB:

    ENGS = ("sync", "scalar", "vector", "tensor", "gpsimd")
    DMA_POOL = 10

    def __init__(self, nc, es):
        self.nc = nc
        self.sem = {}
        self.cnt = {}
        names = ["sy", "sc", "ve", "te", "gp", "cc1", "cc2", "cc3"]
        for q in ("dA", "dB", "dG"):
            names += [f"{q}{i}" for i in range(self.DMA_POOL)]
        for name in names:
            self.sem[name] = es.enter_context(nc.semaphore("sem_" + name))
            self.cnt[name] = 0
        self.engsem = {"sync": "sy", "scalar": "sc", "vector": "ve",
                       "tensor": "te", "gpsimd": "gp"}
        self.dmaq = {"sync": "dA", "scalar": "dB", "gpsimd": "dG"}
        self.dseq = {"dA": 0, "dB": 0, "dG": 0}
        self.fake = {e: 0 for e in self.ENGS}
        self.thunks = {e: [] for e in self.ENGS}
        self.clk = {}
        self.semtop = {s: 0 for s in self.sem}
        self.lastw = {}
        self.readers = {}
        self.tagrec = {}
        self.fence_dep = {}

    @staticmethod
    def _join(a, b):
        if not b:
            return a
        out = dict(a)
        for k, v in b.items():
            if out.get(k, 0) < v:
                out[k] = v
        return out

    def _clock_of(self, s, v):
        c = self.clk.get((s, v))
        if c is not None:
            return c
        base = {s: v}
        for e, sn in self.engsem.items():
            if sn == s:
                base["F:" + e] = self.fake[e]
        return base

    def _emit(self, eng, fn, deps, inc_sem, inc_amt):
        cov = {}
        waits = []
        best = {}
        for s, v in deps:
            if best.get(s, 0) < v:
                best[s] = v
        order = sorted(best.items(), key=lambda kv: (kv[0][0] == "d", kv[0]))
        for s, v in order:
            if cov.get(s, 0) >= v:
                continue
            waits.append((s, v))
            cov = self._join(cov, self._clock_of(s, v))
        self.fake[eng] += 1
        cov["F:" + eng] = self.fake[eng]
        if inc_sem is not None:
            self.cnt[inc_sem] += inc_amt
            val = self.cnt[inc_sem]
            cov[inc_sem] = val
            prev = self.clk.get((inc_sem, self.semtop[inc_sem]), {})
            self.clk[(inc_sem, val)] = self._join(cov, prev)
            self.semtop[inc_sem] = val
            self.thunks[eng].append((waits, fn, inc_sem, inc_amt))
        else:
            self.thunks[eng].append((waits, fn, None, 0))

    def _record(self, rec, reads, writes, tag):
        for w in writes:
            self.lastw[w] = rec
            self.readers[w] = []
        for r in reads:
            self.readers.setdefault(r, []).append(rec)
        if tag is not None:
            tr = self.tagrec.setdefault(tag, {})
            s, v = rec
            if tr.get(s, 0) < v:
                tr[s] = v

    def op(self, eng, fn, reads=(), writes=(), dma=False, tag=None):
        deps = []
        fd = self.fence_dep.get((tag, eng))
        if fd is not None:
            deps.append(fd)
        for r in reads:
            lw = self.lastw.get(r)
            if lw:
                deps.append(lw)
        for w in writes:
            lw = self.lastw.get(w)
            if lw:
                deps.append(lw)
            deps.extend(self.readers.get(w, ()))
        if dma:
            q = self.dmaq[eng]
            n = self.dseq[q]
            self.dseq[q] = n + 1
            s = f"{q}{n % self.DMA_POOL}"
            if n >= self.DMA_POOL:
                deps.append((s, 16 * (n // self.DMA_POOL)))
            self._emit(eng, fn, deps, s, 16)
        else:
            s = self.engsem[eng]
            self._emit(eng, fn, deps, s, 1)
        self._record((s, self.cnt[s]), reads, writes, tag)

    def cc(self, fn, sem, reads=(), writes=()):
        deps = []
        for r in reads:
            lw = self.lastw.get(r)
            if lw:
                deps.append(lw)
        for w in writes:
            lw = self.lastw.get(w)
            if lw:
                deps.append(lw)
            deps.extend(self.readers.get(w, ()))
        self._emit("gpsimd", fn, deps, sem, 1)
        self._record((sem, self.cnt[sem]), reads, writes, None)

    def fence(self, tag):
        tr = dict(self.tagrec.get(tag, {}))
        for e in self.ENGS:
            deps = list(tr.items())
            fd = self.fence_dep.get((tag, e))
            if fd is not None:
                deps.append(fd)
            if e in ("vector", "gpsimd"):
                fn = (lambda en: en.engine_nop())
            else:
                fn = (lambda en: en.nop())
            s = self.engsem[e]
            self._emit(e, fn, deps, s, 1)
            self.fence_dep[(tag, e)] = (s, self.cnt[s])
            tr2 = self.tagrec.setdefault(tag, {})
            if tr2.get(s, 0) < self.cnt[s]:
                tr2[s] = self.cnt[s]

    def barrier(self):
        snap = [(s, v) for s, v in self.cnt.items() if v > 0]
        for e in self.ENGS:
            if e in ("vector", "gpsimd"):
                fn = (lambda en: en.engine_nop())
            else:
                fn = (lambda en: en.nop())
            s = self.engsem[e]
            self._emit(e, fn, snap, s, 1)
            rec = (s, self.cnt[s])
            for tag in set(t for (t, _) in self.fence_dep) | {"A", "D", None}:
                self.fence_dep[(tag, e)] = rec
            tr = self.tagrec.setdefault("A", {})

    def build(self, block):
        sems = self.sem
        for eng in self.ENGS:
            def make_body(thunks=self.thunks[eng]):
                def body(e):
                    for waits, fn, s, amt in thunks:
                        for ws, wv in waits:
                            e.wait_ge(sems[ws], wv)
                        if fn is not None:
                            ins = fn(e)
                            if s is not None:
                                ins.then_inc(sems[s], amt)
                return body
            getattr(block, eng)(make_body())


def build_program():
    nc = bass.Bass("TRN2", target_bir_lowering=False, debug=False)

    x_in = nc.dram_tensor("x_in", [RPC, D], F32, kind="ExternalInput")
    w1s = nc.dram_tensor("w1s", [HPC * 128, D], F32, kind="ExternalInput")
    w2s = nc.dram_tensor("w2s", [HPC * 128, D], F32, kind="ExternalInput")
    w3s = nc.dram_tensor("w3s", [512, HPAD], F32, kind="ExternalInput")
    Cc_d = nc.dram_tensor("Cc", [KCF, 128, FQP], BF16, kind="ExternalInput")
    Sc_d = nc.dram_tensor("Sc", [KCF, 128, FQP], BF16, kind="ExternalInput")
    CiC_d = nc.dram_tensor("CiC", [KCQ, 128, D], BF16, kind="ExternalInput")
    CiS_d = nc.dram_tensor("CiS", [KCQ, 128, D], BF16, kind="ExternalInput")
    idn_d = nc.dram_tensor("idn", [128, 128], BF16, kind="ExternalInput")
    bcol_d = nc.dram_tensor("bcol", [128, 16], F32, kind="ExternalInput")
    out_d = nc.dram_tensor("out", [RPC, D], F32, kind="ExternalOutput")

    out1_d = nc.dram_tensor("out1_s", [RPC, D], F32, kind="Internal")
    part_d = nc.dram_tensor("part_s", [RPC, D], F32, kind="Internal")
    dqm1_d = nc.dram_tensor("dqm1", [704, 8192], BF16, kind="Internal")
    dqm2_d = nc.dram_tensor("dqm2", [704, 8192], BF16, kind="Internal")
    dqm3_d = nc.dram_tensor("dqm3", [704, 8192], BF16, kind="Internal")
    dqa1_d = nc.dram_tensor("dqa1", [NCORES * 704, 8192], BF16,
                            kind="Internal", addr_space="Shared")
    dqa2_d = nc.dram_tensor("dqa2", [NCORES * 704, 8192], BF16,
                            kind="Internal", addr_space="Shared")
    dqa3_d = nc.dram_tensor("dqa3", [NCORES * 704, 8192], BF16,
                            kind="Internal", addr_space="Shared")

    def dram_ap(handle, offset, pattern):
        return bass.AP(handle, offset, pattern)

    _uid = [0]

    def _sb(name, shape, dt):
        _uid[0] += 1
        return nc.sbuf_tensor(f"{name}_{_uid[0]}", shape, dt)

    def _pt(name, shape, dt):
        _uid[0] += 1
        return nc.psum_tensor(f"{name}_{_uid[0]}", shape, dt)

    es_top = contextlib.ExitStack()
    with es_top:
        kb = KB(nc, es_top)

        idn = es_top.enter_context(_sb("idn_s", [128, 128], BF16))
        bcol = es_top.enter_context(_sb("bcol_s", [128, 16], F32))
        stats = es_top.enter_context(_sb("stats", [128, 96], F32))
        es_tp = contextlib.ExitStack()
        tpp = [es_tp.enter_context(_pt(f"tpp{i}", [128, 128], BF16))
               for i in range(2)]
        tpc = [0]

        kb.op("sync", lambda e: e.dma_start(out=idn[:, :], in_=idn_d.ap()),
              writes=[("idn",)], dma=True)
        kb.op("sync", lambda e: e.dma_start(out=bcol[:, :], in_=bcol_d.ap()),
              writes=[("bcol",)], dma=True)

        def transpose128(src_ap, dst_ap, reads, writes, tag=None):
            p = tpc[0] % 2
            tpc[0] += 1
            kb.op("tensor", lambda e, p=p, s=src_ap: e.transpose(
                tpp[p][:, :], s, idn[:, :]),
                reads=list(reads) + [("idn",)], writes=[("tpp", p)], tag=tag)
            kb.op("vector", lambda e, p=p, d=dst_ap: e.tensor_copy(
                d, tpp[p][:, :]),
                reads=[("tpp", p)], writes=list(writes), tag=tag)

        es_d = contextlib.ExitStack()
        wb2 = [es_d.enter_context(_sb(f"d_w{i}", [128, D], F32))
               for i in range(2)]
        sbf = [es_d.enter_context(_sb(f"d_s{i}", [128, 2048], BF16))
               for i in range(2)]
        accb = es_d.enter_context(_sb("d_acc", [128, 2048], F32))
        amc1 = es_d.enter_context(_sb("d_amc", [128, 64], F32))
        rcb = es_d.enter_context(_sb("d_rc", [128, 64], F32))
        dq2 = [es_d.enter_context(_sb(f"d_dq{i}", [128, D], BF16))
               for i in range(2)]
        slab1 = es_d.enter_context(_sb("d_slab", [128, D], BF16))
        tpd = [es_tp.enter_context(_pt(f"tpd{i}", [128, 128], BF16))
               for i in range(2)]
        tdc = [0]

        def transposeD(src_ap, dst_ap, reads, writes):
            p = tdc[0] % 2
            tdc[0] += 1
            kb.op("tensor", lambda e, p=p, s=src_ap: e.transpose(
                tpd[p][:, :], s, idn[:, :]),
                reads=list(reads) + [("idn",)], writes=[("tpd", p)], tag="D")
            kb.op("vector", lambda e, p=p, d=dst_ap: e.tensor_copy(
                d, tpd[p][:, :]),
                reads=[("tpd", p)], writes=list(writes), tag="D")

        def dq_math(src_ap, F, pm):
            nb = F // 64
            wbuf = wb2[pm]
            amc = amc1
            dqb = dq2[pm]
            kb.op("gpsimd", lambda e, a=src_ap, F=F, wbuf=wbuf:
                  e.dma_start(out=wbuf[:, 0:F], in_=a),
                  writes=[("wbuf", pm)], dma=True, tag="D")
            yield
            kb.op("vector", lambda e, F=F, nb=nb, wbuf=wbuf, amc=amc:
                  e.reduce_max(
                      amc[:, 0:nb],
                      wbuf[:, 0:F].rearrange("p (s k) -> p s k", k=64),
                      axis=AX.X, apply_absolute_value=True),
                  reads=[("wbuf", pm)], writes=[("am",)], tag="D")
            kb.op("vector", lambda e, nb=nb, amc=amc: e.tensor_scalar_max(
                amc[:, 0:nb], amc[:, 0:nb], 1e-30),
                reads=[("am",)], writes=[("amc", pm)], tag="D")
            kb.op("vector", lambda e, nb=nb, amc=amc: e.reciprocal(
                rcb[:, 0:nb], amc[:, 0:nb]),
                reads=[("amc", pm)], writes=[("rc",)], tag="D")
            kb.op("vector", lambda e, F=F, nb=nb, wbuf=wbuf: e.tensor_tensor(
                wbuf[:, 0:F].rearrange("p (s k) -> p s k", k=64),
                wbuf[:, 0:F].rearrange("p (s k) -> p s k", k=64),
                rcb[:, 0:nb].rearrange("p (s o) -> p s o", o=1)
                    .broadcast_to((128, nb, 64)),
                ALU.mult),
                reads=[("wbuf", pm), ("rc",)],
                writes=[("wbuf", pm), ("u", pm)],
                tag="D")
            yield
            for c0 in range(0, F, 2048):
                cw = min(2048, F - c0)
                cb = cw // 64
                for j in range(15):
                    p = j % 2
                    kb.op("scalar", lambda e, c0=c0, cw=cw, j=j, p=p,
                          wbuf=wbuf:
                          e.activation(sbf[p][:, 0:cw],
                                       wbuf[:, c0:c0 + cw], AF.Sign,
                                       bias=bcol[:, j:j + 1]),
                          reads=[("u", pm), ("wbuf", pm), ("bcol",)],
                          writes=[("sb", p)],
                          tag="D")
                    if j == 0:
                        kb.op("vector", lambda e, cw=cw, p=p:
                              e.tensor_scalar(accb[:, 0:cw],
                                              sbf[p][:, 0:cw],
                                              float(NF4_E[0]), None,
                                              ALU.mult),
                              reads=[("sb", p)], writes=[("acc",)],
                              tag="D")
                    else:
                        kb.op("vector", lambda e, cw=cw, j=j, p=p:
                              e.scalar_tensor_tensor(
                                  accb[:, 0:cw], sbf[p][:, 0:cw],
                                  float(NF4_E[j]), accb[:, 0:cw],
                                  ALU.mult, ALU.add),
                              reads=[("sb", p), ("acc",)],
                              writes=[("acc",)], tag="D")
                kb.op("vector", lambda e, c0=c0, cw=cw, cb=cb, amc=amc,
                      dqb=dqb: e.tensor_tensor(
                          dqb[:, c0:c0 + cw].rearrange(
                              "p (s k) -> p s k", k=64),
                          accb[:, 0:cw].rearrange(
                              "p (s k) -> p s k", k=64),
                          amc[:, c0 // 64:c0 // 64 + cb].rearrange(
                              "p (s o) -> p s o", o=1)
                              .broadcast_to((128, cb, 64)),
                          ALU.mult),
                      reads=[("acc",), ("amc", pm)],
                      writes=[("dq", pm, c0 // 2048)], tag="D")
                yield

        def d_gen():
            pm = 0
            for mat, wsrc, dqm in ((1, w1s, dqm1_d), (2, w2s, dqm2_d)):
                for j in range(HPC):
                    src = dram_ap(wsrc, j * 128 * D, [[D, 128], [1, D]])
                    yield from dq_math(src, D, pm)
                    dqb = dq2[pm]
                    slab = slab1
                    for kc in range(KCF):
                        transposeD(dqb[:, kc * 128:(kc + 1) * 128],
                                   slab[:, kc * 128:(kc + 1) * 128],
                                   [("dq", pm, (kc * 128) // 2048)],
                                   [("slab", kc)])
                        if kc in (7, 15, 23):
                            yield
                    res = (f"dqm{mat}", j)
                    kb.op("sync", lambda e, j=j, dqm=dqm, slab=slab:
                          e.dma_start(
                              out=dram_ap(dqm, j * SLAB_W,
                                          [[D, 128], [1, D]]),
                              in_=slab[:, :]),
                          reads=[("slab", kc) for kc in range(KCF)],
                          writes=[res], dma=True, tag="D")
                    pm = 1 - pm
                    yield
                sem = "cc1" if mat == 1 else "cc2"
                dqm_l, dqa_l = (dqm1_d, dqa1_d) if mat == 1 else (dqm2_d, dqa2_d)
                kb.cc(lambda e, dqm_l=dqm_l, dqa_l=dqa_l: e.collective_compute(
                    "AllGather", ALU.bypass,
                    replica_groups=[list(range(NCORES))],
                    ins=[dqm_l.ap()], outs=[dqa_l.ap()]),
                    sem=sem,
                    reads=[(f"dqm{mat}", j) for j in range(HPC)],
                    writes=[(f"dqa{mat}",)])
                yield
            for hcg in range(22):
                slab = slab1
                for dc in range(4):
                    src = dram_ap(w3s, dc * 128 * HPAD + hcg * 512,
                                  [[HPAD, 128], [1, 512]])
                    yield from dq_math(src, 512, pm)
                    dqb = dq2[pm]
                    for j in range(4):
                        col = (j * 512 + dc * 128) // 128
                        transposeD(dqb[:, j * 128:(j + 1) * 128],
                                   slab[:, j * 512 + dc * 128:
                                        j * 512 + dc * 128 + 128],
                                   [("dq", pm, 0)], [("slab", col)])
                    pm = 1 - pm
                for j in range(4):
                    hc = hcg * 4 + j
                    res = ("dqm3", hc)
                    kb.op("sync", lambda e, hc=hc, j=j, slab=slab:
                          e.dma_start(
                              out=dram_ap(dqm3_d, hc * PIECE_W,
                                          [[512, 128], [1, 512]]),
                              in_=slab[:, j * 512:(j + 1) * 512]),
                          reads=[("slab", (j * 512 + dc * 128) // 128)
                                 for dc in range(4)],
                          writes=[res], dma=True, tag="D")
                yield
            kb.cc(lambda e: e.collective_compute(
                "AllGather", ALU.bypass,
                replica_groups=[list(range(NCORES))],
                ins=[dqm3_d.ap()], outs=[dqa3_d.ap()]),
                sem="cc3",
                reads=[("dqm3", hc) for hc in range(NHC)],
                writes=[("dqa3",)])

        dgen = d_gen()
        dq_done = [False]

        PUMP_INTERLEAVE = True

        def pump(k=1):
            if not PUMP_INTERLEAVE:
                return
            for _ in range(k):
                try:
                    next(dgen)
                except StopIteration:
                    dq_done[0] = True

        with contextlib.ExitStack() as pa:
            iota_i = pa.enter_context(_sb("iota_i", [128, FQP], I32))
            c1024 = pa.enter_context(_sb("c1024", [128, 1], I32))
            kb.op("gpsimd", lambda e: e.iota(
                iota_i[:, :], [[1, FQP]], channel_multiplier=0,
                allow_small_or_imprecise_dtypes=True),
                writes=[("iota",)], tag="A")
            kb.op("gpsimd", lambda e: e.memset(c1024[:, :], 1024),
                  writes=[("c1024",)], tag="A")

            for hf in range(2):
                row0 = hf * HALF
                with contextlib.ExitStack() as phh:
                    xrb = [phh.enter_context(_sb(f"a_xr{r}", [128, FQP], BF16))
                           for r in range(4)]
                    psb = [phh.enter_context(_sb(f"a_ps{r}", [128, FQP], BF16))
                           for r in range(4)]
                    ytr = [phh.enter_context(
                        _sb(f"a_ytr{r}", [128, KCQ, 128], BF16))
                        for r in range(4)]
                    yti = [phh.enter_context(
                        _sb(f"a_yti{r}", [128, KCQ, 128], BF16))
                        for r in range(4)]

                    with contextlib.ExitStack() as ps:
                        xht = ps.enter_context(
                            _sb("a_xht", [128, KCF, HALF], BF16))
                        with contextlib.ExitStack() as ps2:
                            xt = ps2.enter_context(_sb("a_xt", [128, D], F32))
                            sqc = ps2.enter_context(
                                _sb("a_sqc", [128, 512], F32))
                            sqp = ps2.enter_context(
                                _sb("a_sqp", [128, 8], F32))
                            xhb = ps2.enter_context(
                                _sb("a_xhb", [128, D], BF16))
                            for r in range(4):
                                rb = (row0 + r * 128) * D
                                kb.op("sync", lambda e, rb=rb: e.dma_start(
                                    out=xt[:, :],
                                    in_=dram_ap(x_in, rb, [[D, 128], [1, D]])),
                                    writes=[("xt",)], dma=True, tag="A")
                                cb = 8 * (hf * 4 + r)
                                kb.op("vector", lambda e, cb=cb: e.reduce_sum(
                                    stats[:, cb:cb + 1], xt[:, :], axis=AX.X),
                                    reads=[("xt",)], writes=[("st", cb)],
                                    tag="A")
                                for q8 in range(8):
                                    kb.op("scalar", lambda e, q8=q8:
                                          e.activation(
                                              sqc[:, :],
                                              xt[:, q8 * 512:(q8 + 1) * 512],
                                              AF.Square),
                                          reads=[("xt",)], writes=[("sqc",)],
                                          tag="A")
                                    kb.op("vector", lambda e, q8=q8:
                                          e.reduce_sum(
                                              sqp[:, q8:q8 + 1], sqc[:, :],
                                              axis=AX.X),
                                          reads=[("sqc",)],
                                          writes=[("sqp", q8)], tag="A")
                                kb.op("vector", lambda e, cb=cb: e.reduce_sum(
                                    stats[:, cb + 1:cb + 2], sqp[:, :],
                                    axis=AX.X),
                                    reads=[("sqp", q) for q in range(8)],
                                    writes=[("st", cb + 1)], tag="A")
                                kb.op("vector", lambda e, cb=cb:
                                      e.tensor_scalar(
                                          stats[:, cb + 2:cb + 3],
                                          stats[:, cb:cb + 1], 1.0 / D, None,
                                          ALU.mult),
                                      reads=[("st", cb)],
                                      writes=[("st", cb + 2)], tag="A")
                                kb.op("vector", lambda e, cb=cb:
                                      e.tensor_scalar(
                                          stats[:, cb + 3:cb + 4],
                                          stats[:, cb + 1:cb + 2], 1.0 / D,
                                          None, ALU.mult),
                                      reads=[("st", cb + 1)],
                                      writes=[("st", cb + 3)], tag="A")
                                kb.op("vector", lambda e, cb=cb:
                                      e.tensor_tensor(
                                          stats[:, cb + 4:cb + 5],
                                          stats[:, cb + 2:cb + 3],
                                          stats[:, cb + 2:cb + 3], ALU.mult),
                                      reads=[("st", cb + 2)],
                                      writes=[("st", cb + 4)], tag="A")
                                kb.op("vector", lambda e, cb=cb:
                                      e.tensor_tensor(
                                          stats[:, cb + 5:cb + 6],
                                          stats[:, cb + 3:cb + 4],
                                          stats[:, cb + 4:cb + 5],
                                          ALU.subtract),
                                      reads=[("st", cb + 3), ("st", cb + 4)],
                                      writes=[("st", cb + 5)], tag="A")
                                kb.op("vector", lambda e, cb=cb:
                                      e.tensor_scalar(
                                          stats[:, cb + 5:cb + 6],
                                          stats[:, cb + 5:cb + 6], EPS, None,
                                          ALU.add),
                                      reads=[("st", cb + 5)],
                                      writes=[("st", cb + 5)], tag="A")
                                kb.op("scalar", lambda e, cb=cb: e.activation(
                                    stats[:, cb + 6:cb + 7],
                                    stats[:, cb + 5:cb + 6], AF.Ln),
                                    reads=[("st", cb + 5)],
                                    writes=[("st", cb + 6)], tag="A")
                                kb.op("scalar", lambda e, cb=cb: e.activation(
                                    stats[:, cb + 7:cb + 8],
                                    stats[:, cb + 6:cb + 7], AF.Exp,
                                    scale=-0.5),
                                    reads=[("st", cb + 6)],
                                    writes=[("st", cb + 7)], tag="A")
                                kb.op("vector", lambda e, cb=cb:
                                      e.scalar_tensor_tensor(
                                          stats[:, cb + 4:cb + 5],
                                          stats[:, cb + 2:cb + 3], -1.0,
                                          stats[:, cb + 7:cb + 8],
                                          ALU.mult, ALU.mult),
                                      reads=[("st", cb + 2), ("st", cb + 7)],
                                      writes=[("st", cb + 4)], tag="A")
                                kb.op("scalar", lambda e, cb=cb: e.activation(
                                    xhb[:, :], xt[:, :], AF.Identity,
                                    bias=stats[:, cb + 4:cb + 5],
                                    scale=stats[:, cb + 7:cb + 8]),
                                    reads=[("xt",), ("st", cb + 4),
                                           ("st", cb + 7)],
                                    writes=[("xhb",)], tag="A")
                                for kc in range(KCF):
                                    transpose128(
                                        xhb[:, kc * 128:(kc + 1) * 128],
                                        xht[:, kc, r * 128:(r + 1) * 128],
                                        [("xhb",)], [("xht", kc, r)], tag="A")
                                pump(3)
                        kb.fence("A")

                        with contextlib.ExitStack() as pf:
                            ccb = [pf.enter_context(
                                _sb(f"a_cc{i}", [128, 512], BF16))
                                for i in range(2)]
                            scb = [pf.enter_context(
                                _sb(f"a_sc{i}", [128, 512], BF16))
                                for i in range(2)]
                            pxr = [pf.enter_context(
                                _pt(f"a_pxr{i}", [128, 512], F32))
                                for i in range(2)]
                            pps = [pf.enter_context(
                                _pt(f"a_pps{i}", [128, 512], F32))
                                for i in range(2)]
                            for nbi, (nb0, nbw) in enumerate(NB_FWD):
                                for rg in range(2):
                                    for kc in range(KCF):
                                        p = kc % 2
                                        kb.op("scalar", lambda e, p=p, kc=kc,
                                              nb0=nb0, nbw=nbw: e.dma_start(
                                                  out=ccb[p][:, 0:nbw],
                                                  in_=dram_ap(
                                                      Cc_d,
                                                      kc * 128 * FQP + nb0,
                                                      [[FQP, 128], [1, nbw]])),
                                              writes=[("ccb", p)], dma=True,
                                              tag="A")
                                        kb.op("scalar", lambda e, p=p, kc=kc,
                                              nb0=nb0, nbw=nbw: e.dma_start(
                                                  out=scb[p][:, 0:nbw],
                                                  in_=dram_ap(
                                                      Sc_d,
                                                      kc * 128 * FQP + nb0,
                                                      [[FQP, 128], [1, nbw]])),
                                              writes=[("scb", p)], dma=True,
                                              tag="A")
                                        for ri in range(2):
                                            r = rg * 2 + ri
                                            st_ = (kc == 0)
                                            sp_ = (kc == KCF - 1)
                                            kb.op("tensor",
                                                  lambda e, r=r, ri=ri, kc=kc,
                                                  p=p, nbw=nbw, st_=st_,
                                                  sp_=sp_: e.matmul(
                                                      pxr[ri][:, 0:nbw],
                                                      xht[:, kc,
                                                          r * 128:(r + 1) * 128],
                                                      ccb[p][:, 0:nbw],
                                                      start=st_, stop=sp_),
                                                  reads=[("xht", kc, r),
                                                         ("ccb", p)],
                                                  writes=[("pxr", ri)] if (st_ or sp_) else (),
                                                  tag="A")
                                            kb.op("tensor",
                                                  lambda e, r=r, ri=ri, kc=kc,
                                                  p=p, nbw=nbw, st_=st_,
                                                  sp_=sp_: e.matmul(
                                                      pps[ri][:, 0:nbw],
                                                      xht[:, kc,
                                                          r * 128:(r + 1) * 128],
                                                      scb[p][:, 0:nbw],
                                                      start=st_, stop=sp_),
                                                  reads=[("xht", kc, r),
                                                         ("scb", p)],
                                                  writes=[("pps", ri)] if (st_ or sp_) else (),
                                                  tag="A")
                                    for ri in range(2):
                                        r = rg * 2 + ri
                                        kb.op("vector", lambda e, r=r, ri=ri,
                                              nb0=nb0, nbw=nbw: e.tensor_copy(
                                                  xrb[r][:, nb0:nb0 + nbw],
                                                  pxr[ri][:, 0:nbw]),
                                              reads=[("pxr", ri)],
                                              writes=[("xr", r)], tag="A")
                                        kb.op("vector", lambda e, r=r, ri=ri,
                                              nb0=nb0, nbw=nbw: e.tensor_copy(
                                                  psb[r][:, nb0:nb0 + nbw],
                                                  pps[ri][:, 0:nbw]),
                                              reads=[("pps", ri)],
                                              writes=[("ps", r)], tag="A")
                                    pump(3)
                            kb.fence("A")
                    kb.fence("A")

                    d08 = phh.enter_context(_sb("a_d08", [128, 4], I32))
                    cof8 = phh.enter_context(_sb("a_cof8", [128, 4], F32))
                    with contextlib.ExitStack() as ps:
                        corb = [ps.enter_context(
                            _sb(f"a_cor{r}", [128, D], BF16))
                            for r in range(4)]
                        with contextlib.ExitStack() as psp:
                            ptb = [psp.enter_context(
                                _sb(f"a_pt{r}", [128, KCQ, 128], BF16))
                                for r in range(4)]
                            with contextlib.ExitStack() as ps2:
                                tA = ps2.enter_context(
                                    _sb("a_tA", [128, 1088], BF16))
                                tB = ps2.enter_context(
                                    _sb("a_tB", [128, 1088], BF16))
                                pb = ps2.enter_context(
                                    _sb("a_pb", [128, FQP], BF16))
                                for r in range(4):
                                    for ch in range(2):
                                        c0 = ch * 1088
                                        kb.op("scalar", lambda e, r=r, c0=c0:
                                              e.activation(
                                                  tA[:, :],
                                                  xrb[r][:, c0:c0 + 1088],
                                                  AF.Square),
                                              reads=[("xr", r)],
                                              writes=[("tA",)], tag="A")
                                        kb.op("vector", lambda e, r=r, c0=c0:
                                              e.tensor_tensor(
                                                  tB[:, :],
                                                  psb[r][:, c0:c0 + 1088],
                                                  psb[r][:, c0:c0 + 1088],
                                                  ALU.mult),
                                              reads=[("ps", r)],
                                              writes=[("tB",)], tag="A")
                                        kb.op("vector", lambda e, c0=c0:
                                              e.tensor_tensor(
                                                  pb[:, c0:c0 + 1088],
                                                  tA[:, :], tB[:, :],
                                                  ALU.add),
                                              reads=[("tA",), ("tB",)],
                                              writes=[("pb", c0)], tag="A")
                                    for kq in range(KCQ):
                                        transpose128(
                                            pb[:, kq * 128:(kq + 1) * 128],
                                            ptb[r][:, kq, :],
                                            [("pb", 0), ("pb", 1088)],
                                            [("ptb", r, kq)], tag="A")
                                    pump(3)
                                kb.fence("A")

                            with contextlib.ExitStack() as ps2:
                                cib = [ps2.enter_context(
                                    _sb(f"a_ci{i}", [128, 512], BF16))
                                    for i in range(2)]
                                pc = [ps2.enter_context(
                                    _pt(f"a_pc{i}", [128, 512], F32))
                                    for i in range(4)]
                                for n in range(8):
                                    for rg in range(2):
                                        par = (n * 2 + rg) % 2
                                        for kq in range(KCQ):
                                            p = kq % 2
                                            kb.op("scalar", lambda e, p=p,
                                                  kq=kq, n=n: e.dma_start(
                                                      out=cib[p][:, :],
                                                      in_=dram_ap(
                                                          CiC_d,
                                                          kq * 128 * D
                                                          + n * 512,
                                                          [[D, 128],
                                                           [1, 512]])),
                                                  writes=[("cib", p)],
                                                  dma=True, tag="A")
                                            for ri in range(2):
                                                r = rg * 2 + ri
                                                bank = par * 2 + ri
                                                st_ = (kq == 0)
                                                sp_ = (kq == KCQ - 1)
                                                kb.op("tensor",
                                                      lambda e, r=r, kq=kq,
                                                      p=p, bank=bank,
                                                      st_=st_, sp_=sp_:
                                                      e.matmul(
                                                          pc[bank][:, :],
                                                          ptb[r][:, kq, :],
                                                          cib[p][:, :],
                                                          start=st_,
                                                          stop=sp_),
                                                      reads=[("ptb", r, kq),
                                                             ("cib", p)],
                                                      writes=[("pc", bank)] if (st_ or sp_) else (),
                                                      tag="A")
                                        for ri in range(2):
                                            r = rg * 2 + ri
                                            bank = par * 2 + ri
                                            kb.op("vector", lambda e, r=r,
                                                  n=n, bank=bank:
                                                  e.tensor_copy(
                                                      corb[r][:, n * 512:
                                                              (n + 1) * 512],
                                                      pc[bank][:, :]),
                                                  reads=[("pc", bank)],
                                                  writes=[("cor", r, n)],
                                                  tag="A")
                                        pump(3)
                                kb.fence("A")

                        with contextlib.ExitStack() as ps2:
                            t8 = ps2.enter_context(_sb("a_t8", [128, 8], BF16))
                            i8 = ps2.enter_context(_sb("a_i8", [128, 8], U32))
                            dvb = ps2.enter_context(_sb("a_dv", [128, 8], F32))
                            exb = ps2.enter_context(_sb("a_ex", [128, 8], F32))
                            den = ps2.enter_context(_sb("a_den", [128, 1], F32))
                            for r in range(4):
                                corres = [("cor", r, n) for n in range(8)]
                                kb.op("vector", lambda e, r=r: e.max(
                                    t8[:, :], corb[r][:, :]),
                                    reads=corres, writes=[("t8",)], tag="A")
                                kb.op("vector", lambda e, r=r: e.max_index(
                                    i8[:, :], t8[:, :], corb[r][:, :]),
                                    reads=corres + [("t8",)],
                                    writes=[("i8",)], tag="A")
                                kb.op("vector", lambda e: e.tensor_tensor(
                                    dvb[:, :], t8[:, :],
                                    t8[:, 0:1].broadcast_to((128, 8)),
                                    ALU.subtract),
                                    reads=[("t8",)], writes=[("dv",)],
                                    tag="A")
                                kb.op("scalar", lambda e: e.activation(
                                    exb[:, :], dvb[:, :], AF.Exp,
                                    accum_out=den[:, :]),
                                    reads=[("dv",)],
                                    writes=[("den",), ("exv",)], tag="A")
                                kb.op("vector", lambda e, r=r: e.reciprocal(
                                    cof8[:, r:r + 1], den[:, :]),
                                    reads=[("den",)], writes=[("cof", r)],
                                    tag="A")
                                kb.op("vector", lambda e, r=r: e.tensor_copy(
                                    d08[:, r:r + 1], i8[:, 0:1]),
                                    reads=[("i8",)], writes=[("d0", r)],
                                    tag="A")
                                pump(1)
                        kb.fence("A")

                    with contextlib.ExitStack() as ps2:
                        mi1 = ps2.enter_context(_sb("a_mi1", [128, 1088], I32))
                        mi2 = ps2.enter_context(_sb("a_mi2", [128, 1088], I32))
                        tf1 = ps2.enter_context(_sb("a_tf1", [128, 1088], F32))
                        tf2 = ps2.enter_context(_sb("a_tf2", [128, 1088], F32))
                        csb = ps2.enter_context(_sb("a_cs", [128, FQP], BF16))
                        snb = ps2.enter_context(_sb("a_sn", [128, FQP], BF16))
                        y1 = ps2.enter_context(_sb("a_y1", [128, 1088], BF16))
                        y2 = ps2.enter_context(_sb("a_y2", [128, 1088], BF16))
                        y3 = ps2.enter_context(_sb("a_y3", [128, FQP], BF16))

                        for r in range(4):
                            for ch in range(2):
                                c0 = ch * 1088
                                kb.op("vector", lambda e, c0=c0, r=r:
                                      e.tensor_tensor(
                                          mi1[:, :],
                                          iota_i[:, c0:c0 + 1088],
                                          d08[:, r:r + 1].broadcast_to(
                                              (128, 1088)), ALU.mult),
                                      reads=[("iota",), ("d0", r)],
                                      writes=[("mi1",)], tag="A")
                                kb.op("vector", lambda e: e.tensor_tensor(
                                    mi2[:, :], mi1[:, :],
                                    c1024[:, :].broadcast_to((128, 1088)),
                                    ALU.add),
                                    reads=[("mi1",), ("c1024",)],
                                    writes=[("mi2",)], tag="A")
                                for mi, tf, outb, mk, tk, ok_, sk in (
                                        (mi1, tf1, snb, ("mi1",), ("tf1",),
                                         ("tf2",), ("snb",)),
                                        (mi2, tf2, csb, ("mi2",), ("tf2",),
                                         ("tf1",), ("csb",))):
                                    oth = tf2 if tf is tf1 else tf1
                                    kb.op("vector", lambda e, mi=mi, mk=mk:
                                          e.tensor_single_scalar(
                                              mi[:, :], mi[:, :], 4095,
                                              ALU.bitwise_and),
                                          reads=[mk], writes=[mk], tag="A")
                                    kb.op("vector", lambda e, mi=mi, tf=tf:
                                          e.tensor_copy(tf[:, :], mi[:, :]),
                                          reads=[mk], writes=[tk], tag="A")
                                    kb.op("vector", lambda e, tf=tf:
                                          e.tensor_scalar(
                                              tf[:, :], tf[:, :],
                                              PHASE_SCALE, None, ALU.mult),
                                          reads=[tk], writes=[tk], tag="A")
                                    kb.op("vector", lambda e, tf=tf, oth=oth:
                                          e.tensor_scalar(
                                              oth[:, :], tf[:, :],
                                              float(np.pi), TWO_PI,
                                              ALU.is_gt, ALU.mult),
                                          reads=[tk], writes=[ok_], tag="A")
                                    kb.op("vector", lambda e, tf=tf, oth=oth:
                                          e.tensor_tensor(
                                              tf[:, :], tf[:, :], oth[:, :],
                                              ALU.subtract),
                                          reads=[tk, ok_], writes=[tk],
                                          tag="A")
                                    kb.op("scalar", lambda e, tf=tf,
                                          outb=outb, c0=c0: e.activation(
                                              outb[:, c0:c0 + 1088],
                                              tf[:, :], AF.Sin),
                                          reads=[tk], writes=[sk], tag="A")
                                pump(2)
                            for which in range(2):
                                a_op = ALU.add if which == 0 else ALU.subtract
                                m1 = csb if which == 0 else snb
                                m2 = snb if which == 0 else csb
                                yt = ytr if which == 0 else yti
                                ykey = "ytr" if which == 0 else "yti"
                                for ch in range(2):
                                    c0 = ch * 1088
                                    kb.op("vector", lambda e, r=r, c0=c0,
                                          m1=m1: e.tensor_tensor(
                                              y1[:, :],
                                              xrb[r][:, c0:c0 + 1088],
                                              m1[:, c0:c0 + 1088], ALU.mult),
                                          reads=[("xr", r), ("csb",),
                                                 ("snb",)],
                                          writes=[("y1",)], tag="A")
                                    kb.op("vector", lambda e, r=r, c0=c0,
                                          m2=m2: e.tensor_tensor(
                                              y2[:, :],
                                              psb[r][:, c0:c0 + 1088],
                                              m2[:, c0:c0 + 1088], ALU.mult),
                                          reads=[("ps", r), ("csb",),
                                                 ("snb",)],
                                          writes=[("y2",)], tag="A")
                                    kb.op("vector", lambda e, c0=c0,
                                          a_op=a_op: e.tensor_tensor(
                                              y3[:, c0:c0 + 1088], y1[:, :],
                                              y2[:, :], a_op),
                                          reads=[("y1",), ("y2",)],
                                          writes=[("y3", c0)], tag="A")
                                    kb.op("vector", lambda e, c0=c0, r=r:
                                          e.tensor_scalar(
                                              y3[:, c0:c0 + 1088],
                                              y3[:, c0:c0 + 1088],
                                              cof8[:, r:r + 1], None,
                                              ALU.mult),
                                          reads=[("y3", c0), ("cof", r)],
                                          writes=[("y3", c0)], tag="A")
                                for kq in range(KCQ):
                                    transpose128(
                                        y3[:, kq * 128:(kq + 1) * 128],
                                        yt[r][:, kq, :],
                                        [("y3", 0), ("y3", 1088)],
                                        [(ykey, r, kq)], tag="A")
                                pump(2)
                        kb.fence("A")

                    with contextlib.ExitStack() as ps:
                        cb2 = [ps.enter_context(
                            _sb(f"a_cj{i}", [128, 512], BF16))
                            for i in range(2)]
                        sb2 = [ps.enter_context(
                            _sb(f"a_sj{i}", [128, 512], BF16))
                            for i in range(2)]
                        xsl = [ps.enter_context(
                            _sb(f"a_xs{i}", [128, 512], F32))
                            for i in range(2)]
                        o1b = [ps.enter_context(
                            _sb(f"a_o1{i}", [128, 512], F32))
                            for i in range(2)]
                        pr = [ps.enter_context(
                            _pt(f"a_pr{i}", [128, 512], F32))
                            for i in range(4)]
                        for n in range(8):
                            for rg in range(2):
                                par = (n * 2 + rg) % 2
                                for kq in range(KCQ):
                                    p = kq % 2
                                    kb.op("scalar", lambda e, p=p, kq=kq, n=n:
                                          e.dma_start(
                                              out=cb2[p][:, :],
                                              in_=dram_ap(
                                                  CiC_d,
                                                  kq * 128 * D + n * 512,
                                                  [[D, 128], [1, 512]])),
                                          writes=[("cj", p)], dma=True,
                                          tag="A")
                                    kb.op("scalar", lambda e, p=p, kq=kq, n=n:
                                          e.dma_start(
                                              out=sb2[p][:, :],
                                              in_=dram_ap(
                                                  CiS_d,
                                                  kq * 128 * D + n * 512,
                                                  [[D, 128], [1, 512]])),
                                          writes=[("sj", p)], dma=True,
                                          tag="A")
                                    for ri in range(2):
                                        r = rg * 2 + ri
                                        bank = par * 2 + ri
                                        kb.op("tensor", lambda e, r=r, kq=kq,
                                              p=p, bank=bank, st_=(kq == 0):
                                              e.matmul(pr[bank][:, :],
                                                       ytr[r][:, kq, :],
                                                       cb2[p][:, :],
                                                       start=st_, stop=False),
                                              reads=[("ytr", r, kq),
                                                     ("cj", p)],
                                              writes=[("pr", bank)] if kq == 0 else (),
                                              tag="A")
                                        kb.op("tensor", lambda e, r=r, kq=kq,
                                              p=p, bank=bank,
                                              sp_=(kq == KCQ - 1):
                                              e.matmul(pr[bank][:, :],
                                                       yti[r][:, kq, :],
                                                       sb2[p][:, :],
                                                       start=False, stop=sp_),
                                              reads=[("yti", r, kq),
                                                     ("sj", p)],
                                              writes=[("pr", bank)] if kq == KCQ - 1 else (),
                                              tag="A")
                                for ri in range(2):
                                    r = rg * 2 + ri
                                    bank = par * 2 + ri
                                    bp = (n * 4 + r) % 2
                                    off = (row0 + r * 128) * D + n * 512
                                    kb.op("sync", lambda e, bp=bp, off=off:
                                          e.dma_start(
                                              out=xsl[bp][:, :],
                                              in_=dram_ap(x_in, off,
                                                          [[D, 128],
                                                           [1, 512]])),
                                          writes=[("xsl", bp)], dma=True,
                                          tag="A")
                                    kb.op("vector", lambda e, bank=bank,
                                          bp=bp: e.tensor_tensor(
                                              o1b[bp][:, :], pr[bank][:, :],
                                              xsl[bp][:, :], ALU.add),
                                          reads=[("pr", bank), ("xsl", bp)],
                                          writes=[("o1b", bp)], tag="A")
                                    kb.op("sync", lambda e, bp=bp, off=off:
                                          e.dma_start(
                                              out=dram_ap(out1_d, off,
                                                          [[D, 128],
                                                           [1, 512]]),
                                              in_=o1b[bp][:, :]),
                                          reads=[("o1b", bp)],
                                          writes=[("out1", off)], dma=True,
                                          tag="A")
                                pump(3)
                        kb.fence("A")
                    kb.fence("A")

            while not dq_done[0]:
                try:
                    next(dgen)
                except StopIteration:
                    dq_done[0] = True
        es_d.close()
        kb.barrier()

        out1_res = [("out1", (r * 128) * D + n * 512)
                    for r in range(8) for n in range(8)]

        with contextlib.ExitStack() as pf:
            x2t = pf.enter_context(_sb("f_x2t", [128, KCF, RPC], BF16))

            with contextlib.ExitStack() as ps:
                ot = ps.enter_context(_sb("f_ot", [128, D], F32))
                fsqc = ps.enter_context(_sb("f_sqc", [128, 512], F32))
                fsqp = ps.enter_context(_sb("f_sqp", [128, 8], F32))
                fxhb = ps.enter_context(_sb("f_xhb", [128, D], BF16))
                for rt in range(8):
                    rb = rt * 128 * D
                    kb.op("sync", lambda e, rb=rb: e.dma_start(
                        out=ot[:, :],
                        in_=dram_ap(out1_d, rb, [[D, 128], [1, D]])),
                        reads=out1_res, writes=[("f_ot",)], dma=True)
                    cb = 64 + 8 * (rt % 4)
                    kb.op("vector", lambda e, cb=cb: e.reduce_sum(
                        stats[:, cb:cb + 1], ot[:, :], axis=AX.X),
                        reads=[("f_ot",)], writes=[("st", cb)])
                    for q8 in range(8):
                        kb.op("scalar", lambda e, q8=q8: e.activation(
                            fsqc[:, :], ot[:, q8 * 512:(q8 + 1) * 512],
                            AF.Square),
                            reads=[("f_ot",)], writes=[("sqc",)])
                        kb.op("vector", lambda e, q8=q8: e.reduce_sum(
                            fsqp[:, q8:q8 + 1], fsqc[:, :], axis=AX.X),
                            reads=[("sqc",)], writes=[("sqp", q8)])
                    kb.op("vector", lambda e, cb=cb: e.reduce_sum(
                        stats[:, cb + 1:cb + 2], fsqp[:, :], axis=AX.X),
                        reads=[("sqp", q) for q in range(8)],
                        writes=[("st", cb + 1)])
                    kb.op("vector", lambda e, cb=cb: e.tensor_scalar(
                        stats[:, cb + 2:cb + 3], stats[:, cb:cb + 1],
                        1.0 / D, None, ALU.mult),
                        reads=[("st", cb)], writes=[("st", cb + 2)])
                    kb.op("vector", lambda e, cb=cb: e.tensor_scalar(
                        stats[:, cb + 3:cb + 4], stats[:, cb + 1:cb + 2],
                        1.0 / D, None, ALU.mult),
                        reads=[("st", cb + 1)], writes=[("st", cb + 3)])
                    kb.op("vector", lambda e, cb=cb: e.tensor_tensor(
                        stats[:, cb + 4:cb + 5], stats[:, cb + 2:cb + 3],
                        stats[:, cb + 2:cb + 3], ALU.mult),
                        reads=[("st", cb + 2)], writes=[("st", cb + 4)])
                    kb.op("vector", lambda e, cb=cb: e.tensor_tensor(
                        stats[:, cb + 5:cb + 6], stats[:, cb + 3:cb + 4],
                        stats[:, cb + 4:cb + 5], ALU.subtract),
                        reads=[("st", cb + 3), ("st", cb + 4)],
                        writes=[("st", cb + 5)])
                    kb.op("vector", lambda e, cb=cb: e.tensor_scalar(
                        stats[:, cb + 5:cb + 6], stats[:, cb + 5:cb + 6],
                        EPS, None, ALU.add),
                        reads=[("st", cb + 5)], writes=[("st", cb + 5)])
                    kb.op("scalar", lambda e, cb=cb: e.activation(
                        stats[:, cb + 6:cb + 7], stats[:, cb + 5:cb + 6],
                        AF.Ln),
                        reads=[("st", cb + 5)], writes=[("st", cb + 6)])
                    kb.op("scalar", lambda e, cb=cb: e.activation(
                        stats[:, cb + 7:cb + 8], stats[:, cb + 6:cb + 7],
                        AF.Exp, scale=-0.5),
                        reads=[("st", cb + 6)], writes=[("st", cb + 7)])
                    kb.op("vector", lambda e, cb=cb: e.scalar_tensor_tensor(
                        stats[:, cb + 4:cb + 5], stats[:, cb + 2:cb + 3],
                        -1.0, stats[:, cb + 7:cb + 8], ALU.mult, ALU.mult),
                        reads=[("st", cb + 2), ("st", cb + 7)],
                        writes=[("st", cb + 4)])
                    kb.op("scalar", lambda e, cb=cb: e.activation(
                        fxhb[:, :], ot[:, :], AF.Identity,
                        bias=stats[:, cb + 4:cb + 5],
                        scale=stats[:, cb + 7:cb + 8]),
                        reads=[("f_ot",), ("st", cb + 4), ("st", cb + 7)],
                        writes=[("f_xhb",)])
                    for kc in range(KCF):
                        transpose128(fxhb[:, kc * 128:(kc + 1) * 128],
                                     x2t[:, kc, rt * 128:(rt + 1) * 128],
                                     [("f_xhb",)], [("x2t", kc, rt)])
                kb.barrier()
            es_tp.close()

            x2t_kc = {kc: [("x2t", kc, rt) for rt in range(8)]
                      for kc in range(KCF)}

            ht = pf.enter_context(_sb("f_ht", [128, 43, RPC], BF16))

            def slab_ap(mat, hc):
                r, j = divmod(hc, HPC)
                dqa_l = dqa1_d if mat == 1 else dqa2_d
                base = r * DQ1_ELEMS + j * SLAB_W
                return dram_ap(dqa_l, base, [[4096, 128], [1, 4096]])

            def piece_ap(hc, nq):
                base = nq * DQ3_ELEMS + hc * PIECE_W
                return dram_ap(dqa3_d, base, [[512, 128], [1, 512]])

            for hph in range(2):
                hcs = list(range(hph * 43, hph * 43 + 43))

                with contextlib.ExitStack() as ps:
                    s1b = [ps.enter_context(
                        _sb(f"f_s1{i}", [128, KCF, 128], BF16))
                        for i in range(2)]
                    s2b = [ps.enter_context(
                        _sb(f"f_s2{i}", [128, KCF, 128], BF16))
                        for i in range(2)]
                    sg = ps.enter_context(_sb("f_sg", [128, RPC], F32))
                    pg = [ps.enter_context(_pt(f"f_pg{i}", [128, RPC], F32))
                          for i in range(2)]
                    pu = [ps.enter_context(_pt(f"f_pu{i}", [128, RPC], F32))
                          for i in range(2)]
                    for i, hc in enumerate(hcs):
                        par = i % 2
                        kb.op("sync", lambda e, par=par, hc=hc: e.dma_start(
                            out=s1b[par][:, :, :].rearrange(
                                "p a b -> p (a b)"),
                            in_=slab_ap(1, hc)),
                            reads=[("dqa1",)], writes=[("s1b", par)],
                            dma=True)
                        kb.op("sync", lambda e, par=par, hc=hc: e.dma_start(
                            out=s2b[par][:, :, :].rearrange(
                                "p a b -> p (a b)"),
                            in_=slab_ap(2, hc)),
                            reads=[("dqa2",)], writes=[("s2b", par)],
                            dma=True)
                        for kc in range(KCF):
                            for ns in range(2):
                                st_ = (kc == 0)
                                sp_ = (kc == KCF - 1)
                                kb.op("tensor", lambda e, par=par, kc=kc,
                                      ns=ns, st_=st_, sp_=sp_: e.matmul(
                                          pg[par][:, ns * 512:(ns + 1) * 512],
                                          s1b[par][:, kc, :],
                                          x2t[:, kc, ns * 512:(ns + 1) * 512],
                                          start=st_, stop=sp_),
                                      reads=[("s1b", par)] + x2t_kc[kc],
                                      writes=[("pg", par)] if (st_ or sp_) else ())
                                kb.op("tensor", lambda e, par=par, kc=kc,
                                      ns=ns, st_=st_, sp_=sp_: e.matmul(
                                          pu[par][:, ns * 512:(ns + 1) * 512],
                                          s2b[par][:, kc, :],
                                          x2t[:, kc, ns * 512:(ns + 1) * 512],
                                          start=st_, stop=sp_),
                                      reads=[("s2b", par)] + x2t_kc[kc],
                                      writes=[("pu", par)] if (st_ or sp_) else ())
                        kb.op("scalar", lambda e, par=par: e.activation(
                            sg[:, :], pg[par][:, :], AF.Silu),
                            reads=[("pg", par)], writes=[("f_sg",)])
                        kb.op("vector", lambda e, par=par, i=i:
                              e.tensor_tensor(ht[:, i, :], sg[:, :],
                                              pu[par][:, :], ALU.mult),
                              reads=[("f_sg",), ("pg", par), ("pu", par)],
                              writes=[("ht", i)])
                    kb.barrier()

                with contextlib.ExitStack() as ps:
                    pcb = [ps.enter_context(
                        _sb(f"f_pc{i}", [128, 4, 512], BF16))
                        for i in range(2)]
                    o1c = [ps.enter_context(_sb(f"f_o1c{i}", [128, 512], F32))
                           for i in range(2)]
                    ob = [ps.enter_context(_sb(f"f_ob{i}", [128, 512], F32))
                          for i in range(2)]
                    pd = [ps.enter_context(_pt(f"f_pd{i}", [128, 512], F32))
                          for i in range(8)]
                    nkb = (43 + 3) // 4
                    for n in range(8):
                        for mg in range(2):
                            for kbatch in range(nkb):
                                ks = list(range(kbatch * 4,
                                                min(43, kbatch * 4 + 4)))
                                bp = kbatch % 2
                                for ki, k in enumerate(ks):
                                    kb.op("scalar",
                                          lambda e, bp=bp, ki=ki, k=k, n=n,
                                          hph=hph: e.dma_start(
                                              out=pcb[bp][:, ki, :],
                                              in_=piece_ap(hph * 43 + k, n)),
                                          reads=[("dqa3",)],
                                          writes=[("pcb", bp, ki)], dma=True)
                                for ki, k in enumerate(ks):
                                    for m in range(4):
                                        bank = mg * 4 + m
                                        st_ = (kbatch == 0 and ki == 0)
                                        sp_ = (kbatch == nkb - 1 and
                                               ki == len(ks) - 1)
                                        mslice = (mg * 4 + m) * 128
                                        kb.op("tensor",
                                              lambda e, bp=bp, ki=ki, k=k,
                                              m=m, bank=bank, st_=st_,
                                              sp_=sp_, mslice=mslice:
                                              e.matmul(
                                                  pd[bank][:, :],
                                                  ht[:, k,
                                                     mslice:mslice + 128],
                                                  pcb[bp][:, ki, :],
                                                  start=st_, stop=sp_),
                                              reads=[("ht", k),
                                                     ("pcb", bp, ki)],
                                              writes=[("pd", bank)] if (st_ or sp_) else ())
                            srcd = out1_d if hph == 0 else part_d
                            for m in range(4):
                                bank = mg * 4 + m
                                bp = m % 2
                                off = (mg * 512 + m * 128) * D + n * 512
                                kb.op("sync", lambda e, bp=bp, off=off,
                                      srcd=srcd: e.dma_start(
                                          out=o1c[bp][:, :],
                                          in_=dram_ap(srcd, off,
                                                      [[D, 128], [1, 512]])),
                                      reads=out1_res if hph == 0 else [("part",)],
                                      writes=[("o1c", bp)], dma=True)
                                kb.op("vector", lambda e, bp=bp, bank=bank:
                                      e.tensor_tensor(
                                          ob[bp][:, :], pd[bank][:, :],
                                          o1c[bp][:, :], ALU.add),
                                      reads=[("pd", bank), ("o1c", bp)],
                                      writes=[("ob", bp)])
                                if hph == 0:
                                    kb.op("sync", lambda e, bp=bp, off=off:
                                          e.dma_start(
                                              out=dram_ap(part_d, off,
                                                          [[D, 128],
                                                           [1, 512]]),
                                              in_=ob[bp][:, :]),
                                          reads=[("ob", bp)],
                                          writes=[("part",)], dma=True)
                                else:
                                    kb.op("sync", lambda e, bp=bp, off=off:
                                          e.dma_start(
                                              out=dram_ap(out_d, off,
                                                          [[D, 128],
                                                           [1, 512]]),
                                              in_=ob[bp][:, :]),
                                          reads=[("ob", bp)],
                                          writes=[("outd", off)], dma=True)
                    kb.barrier()

        kb.barrier()

        with nc.Block() as block:
            kb.build(block)

    return nc


_NC_CACHE = {}


def kernel(L0_attn_x, L0_ffn_w1, L0_ffn_w2, L0_ffn_w3):
    x = np.ascontiguousarray(np.asarray(L0_attn_x, np.float32).reshape(8192, D))
    w1 = np.asarray(L0_ffn_w1, np.float32)
    w2 = np.asarray(L0_ffn_w2, np.float32)
    w3 = np.asarray(L0_ffn_w3, np.float32)

    w1p = np.zeros((HPAD, D), np.float32)
    w1p[:H] = w1
    w2p = np.zeros((HPAD, D), np.float32)
    w2p[:H] = w2
    w3p = np.zeros((D, HPAD), np.float32)
    w3p[:, :H] = w3

    Cc, Sc, CiCc, CiSc, ident, bcol = _host_consts()

    if "nc" not in _NC_CACHE:
        _NC_CACHE["nc"] = build_program()
    nc = _NC_CACHE["nc"]

    in_maps = []
    for c in range(NCORES):
        in_maps.append({
            "x_in": np.ascontiguousarray(x[c * RPC:(c + 1) * RPC]),
            "w1s": np.ascontiguousarray(w1p[c * 1408:(c + 1) * 1408]),
            "w2s": np.ascontiguousarray(w2p[c * 1408:(c + 1) * 1408]),
            "w3s": np.ascontiguousarray(w3p[c * 512:(c + 1) * 512]),
            "Cc": Cc, "Sc": Sc, "CiC": CiCc, "CiS": CiSc,
            "idn": ident, "bcol": bcol,
        })
    res = run_bass_kernel_spmd(nc, in_maps, core_ids=list(range(NCORES)))
    out = np.concatenate([np.asarray(r["out"]) for r in res.results], axis=0)
    return out.reshape(4, 2048, D)


# version 14
# speedup vs baseline: 1.2883x; 1.2883x over previous
import contextlib

import numpy as np
import ml_dtypes

import concourse.bass as bass
import concourse.mybir as mybir
from concourse.bass_utils import run_bass_kernel_spmd

F32 = mybir.dt.float32
BF16 = mybir.dt.bfloat16
I32 = mybir.dt.int32
U32 = mybir.dt.uint32
AF = mybir.ActivationFunctionType
ALU = mybir.AluOpType
AX = mybir.AxisListType
BF = ml_dtypes.bfloat16

NCORES = 8
D = 4096
RPC = 1024
HALF = 512
FQ = 2049
FQP = 2176
KCF = 32
KCQ = 17
NB_FWD = [(0, 512), (512, 512), (1024, 512), (1536, 512), (2048, 128)]
H = 11008
HPAD = 11264
NHC = 88
HC_REAL = 86
HPC = 11
EPS = 1e-5

NF4_CODE = np.array([-1.0, -0.6961928009986877, -0.5250730514526367,
                     -0.39491748809814453, -0.28444138169288635,
                     -0.18477343022823334, -0.09105003625154495, 0.0,
                     0.07958029955625534, 0.16093020141124725,
                     0.24611230194568634, 0.33791524171829224,
                     0.44070982933044434, 0.5626170039176941,
                     0.7229568362236023, 1.0], dtype=np.float32)
NF4_BOUND = ((NF4_CODE[:-1] + NF4_CODE[1:]) * 0.5).astype(np.float32)
NF4_E = ((NF4_CODE[1:] - NF4_CODE[:-1]) * 0.5).astype(np.float32)

SLAB_W = 128 * 4096
PIECE_W = 128 * 512
DQ1_ELEMS = HPC * SLAB_W
DQ3_ELEMS = NHC * PIECE_W
PHASE_SCALE = float(2 * np.pi / 4096)
TWO_PI = float(2 * np.pi)


def _host_consts():
    t = np.arange(D, dtype=np.float64)
    k = np.arange(FQP, dtype=np.float64)
    ang = 2 * np.pi * np.outer(t, k) / D
    C = np.cos(ang)
    S = np.sin(ang)
    C[:, FQ:] = 0.0
    S[:, FQ:] = 0.0
    w = np.full(FQP, 2.0)
    w[0] = 1.0
    w[2048] = 1.0
    w[FQ:] = 0.0
    angi = 2 * np.pi * np.outer(k, t) / D
    CiC = (w[:, None] * np.cos(angi)) / D
    CiS = (-w[:, None] * np.sin(angi)) / D
    Cc = np.ascontiguousarray(C.reshape(KCF, 128, FQP)).astype(BF)
    Sc = np.ascontiguousarray(S.reshape(KCF, 128, FQP)).astype(BF)
    CiCc = np.ascontiguousarray(CiC.reshape(KCQ, 128, D)).astype(BF)
    CiSc = np.ascontiguousarray(CiS.reshape(KCQ, 128, D)).astype(BF)
    ident = np.eye(128, dtype=BF)
    bcol = np.zeros((128, 16), np.float32)
    bcol[:, :15] = -NF4_BOUND[None, :]
    return Cc, Sc, CiCc, CiSc, ident, bcol


class KB:

    ENGS = ("sync", "scalar", "vector", "tensor", "gpsimd")
    DMA_POOL = 10

    def __init__(self, nc, es):
        self.nc = nc
        self.sem = {}
        self.cnt = {}
        names = ["sy", "sc", "ve", "te", "gp", "cc1", "cc2", "cc3"]
        for q in ("dA", "dB", "dG"):
            names += [f"{q}{i}" for i in range(self.DMA_POOL)]
        for name in names:
            self.sem[name] = es.enter_context(nc.semaphore("sem_" + name))
            self.cnt[name] = 0
        self.engsem = {"sync": "sy", "scalar": "sc", "vector": "ve",
                       "tensor": "te", "gpsimd": "gp"}
        self.dmaq = {"sync": "dA", "scalar": "dB", "gpsimd": "dG"}
        self.dseq = {"dA": 0, "dB": 0, "dG": 0}
        self.fake = {e: 0 for e in self.ENGS}
        self.thunks = {e: [] for e in self.ENGS}
        self.clk = {}
        self.semtop = {s: 0 for s in self.sem}
        self.lastw = {}
        self.readers = {}
        self.tagrec = {}
        self.fence_dep = {}

    @staticmethod
    def _join(a, b):
        if not b:
            return a
        out = dict(a)
        for k, v in b.items():
            if out.get(k, 0) < v:
                out[k] = v
        return out

    def _clock_of(self, s, v):
        c = self.clk.get((s, v))
        if c is not None:
            return c
        base = {s: v}
        for e, sn in self.engsem.items():
            if sn == s:
                base["F:" + e] = self.fake[e]
        return base

    def _emit(self, eng, fn, deps, inc_sem, inc_amt):
        cov = {}
        waits = []
        best = {}
        for s, v in deps:
            if best.get(s, 0) < v:
                best[s] = v
        order = sorted(best.items(), key=lambda kv: (kv[0][0] == "d", kv[0]))
        for s, v in order:
            if cov.get(s, 0) >= v:
                continue
            waits.append((s, v))
            cov = self._join(cov, self._clock_of(s, v))
        self.fake[eng] += 1
        cov["F:" + eng] = self.fake[eng]
        if inc_sem is not None:
            self.cnt[inc_sem] += inc_amt
            val = self.cnt[inc_sem]
            cov[inc_sem] = val
            prev = self.clk.get((inc_sem, self.semtop[inc_sem]), {})
            self.clk[(inc_sem, val)] = self._join(cov, prev)
            self.semtop[inc_sem] = val
            self.thunks[eng].append((waits, fn, inc_sem, inc_amt))
        else:
            self.thunks[eng].append((waits, fn, None, 0))

    def _record(self, rec, reads, writes, tag):
        for w in writes:
            self.lastw[w] = rec
            self.readers[w] = []
        for r in reads:
            self.readers.setdefault(r, []).append(rec)
        if tag is not None:
            tr = self.tagrec.setdefault(tag, {})
            s, v = rec
            if tr.get(s, 0) < v:
                tr[s] = v

    def op(self, eng, fn, reads=(), writes=(), dma=False, tag=None):
        deps = []
        fd = self.fence_dep.get((tag, eng))
        if fd is not None:
            deps.append(fd)
        for r in reads:
            lw = self.lastw.get(r)
            if lw:
                deps.append(lw)
        for w in writes:
            lw = self.lastw.get(w)
            if lw:
                deps.append(lw)
            deps.extend(self.readers.get(w, ()))
        if dma:
            q = self.dmaq[eng]
            n = self.dseq[q]
            self.dseq[q] = n + 1
            s = f"{q}{n % self.DMA_POOL}"
            if n >= self.DMA_POOL:
                deps.append((s, 16 * (n // self.DMA_POOL)))
            self._emit(eng, fn, deps, s, 16)
        else:
            s = self.engsem[eng]
            self._emit(eng, fn, deps, s, 1)
        self._record((s, self.cnt[s]), reads, writes, tag)

    def cc(self, fn, sem, reads=(), writes=()):
        deps = []
        for r in reads:
            lw = self.lastw.get(r)
            if lw:
                deps.append(lw)
        for w in writes:
            lw = self.lastw.get(w)
            if lw:
                deps.append(lw)
            deps.extend(self.readers.get(w, ()))
        self._emit("gpsimd", fn, deps, sem, 1)
        self._record((sem, self.cnt[sem]), reads, writes, None)

    def fence(self, tag):
        tr = dict(self.tagrec.get(tag, {}))
        for e in self.ENGS:
            deps = list(tr.items())
            fd = self.fence_dep.get((tag, e))
            if fd is not None:
                deps.append(fd)
            if e in ("vector", "gpsimd"):
                fn = (lambda en: en.engine_nop())
            else:
                fn = (lambda en: en.nop())
            s = self.engsem[e]
            self._emit(e, fn, deps, s, 1)
            self.fence_dep[(tag, e)] = (s, self.cnt[s])
            tr2 = self.tagrec.setdefault(tag, {})
            if tr2.get(s, 0) < self.cnt[s]:
                tr2[s] = self.cnt[s]

    def barrier(self):
        snap = [(s, v) for s, v in self.cnt.items() if v > 0]
        for e in self.ENGS:
            if e in ("vector", "gpsimd"):
                fn = (lambda en: en.engine_nop())
            else:
                fn = (lambda en: en.nop())
            s = self.engsem[e]
            self._emit(e, fn, snap, s, 1)
            rec = (s, self.cnt[s])
            for tag in set(t for (t, _) in self.fence_dep) | {"A", "D", None}:
                self.fence_dep[(tag, e)] = rec
            tr = self.tagrec.setdefault("A", {})

    def build(self, block):
        sems = self.sem
        for eng in self.ENGS:
            def make_body(thunks=self.thunks[eng]):
                def body(e):
                    for waits, fn, s, amt in thunks:
                        for ws, wv in waits:
                            e.wait_ge(sems[ws], wv)
                        if fn is not None:
                            ins = fn(e)
                            if s is not None:
                                ins.then_inc(sems[s], amt)
                return body
            getattr(block, eng)(make_body())


def build_program():
    nc = bass.Bass("TRN2", target_bir_lowering=False, debug=False)

    x_in = nc.dram_tensor("x_in", [RPC, D], F32, kind="ExternalInput")
    w1s = nc.dram_tensor("w1s", [HPC * 128, D], F32, kind="ExternalInput")
    w2s = nc.dram_tensor("w2s", [HPC * 128, D], F32, kind="ExternalInput")
    w3s = nc.dram_tensor("w3s", [512, HPAD], F32, kind="ExternalInput")
    Cc_d = nc.dram_tensor("Cc", [KCF, 128, FQP], BF16, kind="ExternalInput")
    Sc_d = nc.dram_tensor("Sc", [KCF, 128, FQP], BF16, kind="ExternalInput")
    CiC_d = nc.dram_tensor("CiC", [KCQ, 128, D], BF16, kind="ExternalInput")
    CiS_d = nc.dram_tensor("CiS", [KCQ, 128, D], BF16, kind="ExternalInput")
    idn_d = nc.dram_tensor("idn", [128, 128], BF16, kind="ExternalInput")
    bcol_d = nc.dram_tensor("bcol", [128, 16], F32, kind="ExternalInput")
    out_d = nc.dram_tensor("out", [RPC, D], F32, kind="ExternalOutput")

    out1_d = nc.dram_tensor("out1_s", [RPC, D], F32, kind="Internal")
    part_d = nc.dram_tensor("part_s", [RPC, D], F32, kind="Internal")
    dqm1_d = nc.dram_tensor("dqm1", [704, 8192], BF16, kind="Internal")
    dqm2_d = nc.dram_tensor("dqm2", [704, 8192], BF16, kind="Internal")
    dqm3_d = nc.dram_tensor("dqm3", [704, 8192], BF16, kind="Internal")
    dqa1_d = nc.dram_tensor("dqa1", [NCORES * 704, 8192], BF16,
                            kind="Internal", addr_space="Shared")
    dqa2_d = nc.dram_tensor("dqa2", [NCORES * 704, 8192], BF16,
                            kind="Internal", addr_space="Shared")
    dqa3_d = nc.dram_tensor("dqa3", [NCORES * 704, 8192], BF16,
                            kind="Internal", addr_space="Shared")

    def dram_ap(handle, offset, pattern):
        return bass.AP(handle, offset, pattern)

    _uid = [0]

    def _sb(name, shape, dt):
        _uid[0] += 1
        return nc.sbuf_tensor(f"{name}_{_uid[0]}", shape, dt)

    def _pt(name, shape, dt):
        _uid[0] += 1
        return nc.psum_tensor(f"{name}_{_uid[0]}", shape, dt)

    es_top = contextlib.ExitStack()
    with es_top:
        kb = KB(nc, es_top)

        idn = es_top.enter_context(_sb("idn_s", [128, 128], BF16))
        bcol = es_top.enter_context(_sb("bcol_s", [128, 16], F32))
        stats = es_top.enter_context(_sb("stats", [128, 96], F32))
        es_tp = contextlib.ExitStack()
        tpp = [es_tp.enter_context(_pt(f"tpp{i}", [128, 128], BF16))
               for i in range(2)]
        tpc = [0]

        kb.op("sync", lambda e: e.dma_start(out=idn[:, :], in_=idn_d.ap()),
              writes=[("idn",)], dma=True)
        kb.op("sync", lambda e: e.dma_start(out=bcol[:, :], in_=bcol_d.ap()),
              writes=[("bcol",)], dma=True)

        def transpose128(src_ap, dst_ap, reads, writes, tag=None):
            p = tpc[0] % 2
            tpc[0] += 1
            kb.op("tensor", lambda e, p=p, s=src_ap: e.transpose(
                tpp[p][:, :], s, idn[:, :]),
                reads=list(reads) + [("idn",)], writes=[("tpp", p)], tag=tag)
            kb.op("vector", lambda e, p=p, d=dst_ap: e.tensor_copy(
                d, tpp[p][:, :]),
                reads=[("tpp", p)], writes=list(writes), tag=tag)

        es_d = contextlib.ExitStack()
        wb2 = [es_d.enter_context(_sb(f"d_w{i}", [128, D], F32))
               for i in range(2)]
        sbf = [es_d.enter_context(_sb(f"d_s{i}", [128, 2048], BF16))
               for i in range(2)]
        accb = es_d.enter_context(_sb("d_acc", [128, 2048], F32))
        amc1 = es_d.enter_context(_sb("d_amc", [128, 64], F32))
        rcb = es_d.enter_context(_sb("d_rc", [128, 64], F32))
        dq2 = [es_d.enter_context(_sb(f"d_dq{i}", [128, D], BF16))
               for i in range(2)]
        slab1 = es_d.enter_context(_sb("d_slab", [128, D], BF16))
        tpd = [es_tp.enter_context(_pt(f"tpd{i}", [128, 128], BF16))
               for i in range(2)]
        tdc = [0]

        def transposeD(src_ap, dst_ap, reads, writes):
            p = tdc[0] % 2
            tdc[0] += 1
            kb.op("tensor", lambda e, p=p, s=src_ap: e.transpose(
                tpd[p][:, :], s, idn[:, :]),
                reads=list(reads) + [("idn",)], writes=[("tpd", p)], tag="D")
            kb.op("vector", lambda e, p=p, d=dst_ap: e.tensor_copy(
                d, tpd[p][:, :]),
                reads=[("tpd", p)], writes=list(writes), tag="D")

        def dq_math(src_ap, F, pm):
            nb = F // 64
            wbuf = wb2[pm]
            amc = amc1
            dqb = dq2[pm]
            kb.op("gpsimd", lambda e, a=src_ap, F=F, wbuf=wbuf:
                  e.dma_start(out=wbuf[:, 0:F], in_=a),
                  writes=[("wbuf", pm)], dma=True, tag="D")
            yield
            kb.op("vector", lambda e, F=F, nb=nb, wbuf=wbuf, amc=amc:
                  e.reduce_max(
                      amc[:, 0:nb],
                      wbuf[:, 0:F].rearrange("p (s k) -> p s k", k=64),
                      axis=AX.X, apply_absolute_value=True),
                  reads=[("wbuf", pm)], writes=[("am",)], tag="D")
            kb.op("vector", lambda e, nb=nb, amc=amc: e.tensor_scalar_max(
                amc[:, 0:nb], amc[:, 0:nb], 1e-30),
                reads=[("am",)], writes=[("amc", pm)], tag="D")
            kb.op("vector", lambda e, nb=nb, amc=amc: e.reciprocal(
                rcb[:, 0:nb], amc[:, 0:nb]),
                reads=[("amc", pm)], writes=[("rc",)], tag="D")
            kb.op("vector", lambda e, F=F, nb=nb, wbuf=wbuf: e.tensor_tensor(
                wbuf[:, 0:F].rearrange("p (s k) -> p s k", k=64),
                wbuf[:, 0:F].rearrange("p (s k) -> p s k", k=64),
                rcb[:, 0:nb].rearrange("p (s o) -> p s o", o=1)
                    .broadcast_to((128, nb, 64)),
                ALU.mult),
                reads=[("wbuf", pm), ("rc",)],
                writes=[("wbuf", pm), ("u", pm)],
                tag="D")
            yield
            for c0 in range(0, F, 2048):
                cw = min(2048, F - c0)
                cb = cw // 64
                for j in range(15):
                    p = j % 2
                    kb.op("scalar", lambda e, c0=c0, cw=cw, j=j, p=p,
                          wbuf=wbuf:
                          e.activation(sbf[p][:, 0:cw],
                                       wbuf[:, c0:c0 + cw], AF.Sign,
                                       bias=bcol[:, j:j + 1]),
                          reads=[("u", pm), ("wbuf", pm), ("bcol",)],
                          writes=[("sb", p)],
                          tag="D")
                    if j == 0:
                        kb.op("vector", lambda e, cw=cw, p=p:
                              e.tensor_scalar(accb[:, 0:cw],
                                              sbf[p][:, 0:cw],
                                              float(NF4_E[0]), None,
                                              ALU.mult),
                              reads=[("sb", p)], writes=[("acc",)],
                              tag="D")
                    else:
                        kb.op("vector", lambda e, cw=cw, j=j, p=p:
                              e.scalar_tensor_tensor(
                                  accb[:, 0:cw], sbf[p][:, 0:cw],
                                  float(NF4_E[j]), accb[:, 0:cw],
                                  ALU.mult, ALU.add),
                              reads=[("sb", p), ("acc",)],
                              writes=[("acc",)], tag="D")
                kb.op("vector", lambda e, c0=c0, cw=cw, cb=cb, amc=amc,
                      dqb=dqb: e.tensor_tensor(
                          dqb[:, c0:c0 + cw].rearrange(
                              "p (s k) -> p s k", k=64),
                          accb[:, 0:cw].rearrange(
                              "p (s k) -> p s k", k=64),
                          amc[:, c0 // 64:c0 // 64 + cb].rearrange(
                              "p (s o) -> p s o", o=1)
                              .broadcast_to((128, cb, 64)),
                          ALU.mult),
                      reads=[("acc",), ("amc", pm)],
                      writes=[("dq", pm, c0 // 2048)], tag="D")
                yield

        def d_gen():
            pm = 0
            for mat, wsrc, dqm in ((1, w1s, dqm1_d), (2, w2s, dqm2_d)):
                for j in range(HPC):
                    src = dram_ap(wsrc, j * 128 * D, [[D, 128], [1, D]])
                    yield from dq_math(src, D, pm)
                    dqb = dq2[pm]
                    slab = slab1
                    for kc in range(KCF):
                        transposeD(dqb[:, kc * 128:(kc + 1) * 128],
                                   slab[:, kc * 128:(kc + 1) * 128],
                                   [("dq", pm, (kc * 128) // 2048)],
                                   [("slab", kc)])
                        if kc in (7, 15, 23):
                            yield
                    res = (f"dqm{mat}", j)
                    kb.op("sync", lambda e, j=j, dqm=dqm, slab=slab:
                          e.dma_start(
                              out=dram_ap(dqm, j * SLAB_W,
                                          [[D, 128], [1, D]]),
                              in_=slab[:, :]),
                          reads=[("slab", kc) for kc in range(KCF)],
                          writes=[res], dma=True, tag="D")
                    pm = 1 - pm
                    yield
                sem = "cc1" if mat == 1 else "cc2"
                dqm_l, dqa_l = (dqm1_d, dqa1_d) if mat == 1 else (dqm2_d, dqa2_d)
                kb.cc(lambda e, dqm_l=dqm_l, dqa_l=dqa_l: e.collective_compute(
                    "AllGather", ALU.bypass,
                    replica_groups=[list(range(NCORES))],
                    ins=[dqm_l.ap()], outs=[dqa_l.ap()]),
                    sem=sem,
                    reads=[(f"dqm{mat}", j) for j in range(HPC)],
                    writes=[(f"dqa{mat}",)])
                yield
            for hcg in range(22):
                slab = slab1
                for dc in range(4):
                    src = dram_ap(w3s, dc * 128 * HPAD + hcg * 512,
                                  [[HPAD, 128], [1, 512]])
                    yield from dq_math(src, 512, pm)
                    dqb = dq2[pm]
                    for j in range(4):
                        col = (j * 512 + dc * 128) // 128
                        transposeD(dqb[:, j * 128:(j + 1) * 128],
                                   slab[:, j * 512 + dc * 128:
                                        j * 512 + dc * 128 + 128],
                                   [("dq", pm, 0)], [("slab", col)])
                    pm = 1 - pm
                for j in range(4):
                    hc = hcg * 4 + j
                    res = ("dqm3", hc)
                    kb.op("sync", lambda e, hc=hc, j=j, slab=slab:
                          e.dma_start(
                              out=dram_ap(dqm3_d, hc * PIECE_W,
                                          [[512, 128], [1, 512]]),
                              in_=slab[:, j * 512:(j + 1) * 512]),
                          reads=[("slab", (j * 512 + dc * 128) // 128)
                                 for dc in range(4)],
                          writes=[res], dma=True, tag="D")
                yield
            kb.cc(lambda e: e.collective_compute(
                "AllGather", ALU.bypass,
                replica_groups=[list(range(NCORES))],
                ins=[dqm3_d.ap()], outs=[dqa3_d.ap()]),
                sem="cc3",
                reads=[("dqm3", hc) for hc in range(NHC)],
                writes=[("dqa3",)])

        dgen = d_gen()
        dq_done = [False]

        PUMP_INTERLEAVE = True

        def pump(k=1):
            if not PUMP_INTERLEAVE:
                return
            for _ in range(k):
                try:
                    next(dgen)
                except StopIteration:
                    dq_done[0] = True

        with contextlib.ExitStack() as pa:
            iota_i = pa.enter_context(_sb("iota_i", [128, FQP], I32))
            c1024 = pa.enter_context(_sb("c1024", [128, 1], I32))
            kb.op("gpsimd", lambda e: e.iota(
                iota_i[:, :], [[1, FQP]], channel_multiplier=0,
                allow_small_or_imprecise_dtypes=True),
                writes=[("iota",)], tag="A")
            kb.op("gpsimd", lambda e: e.memset(c1024[:, :], 1024),
                  writes=[("c1024",)], tag="A")

            for hf in range(2):
                row0 = hf * HALF
                with contextlib.ExitStack() as phh:
                    ytr = [phh.enter_context(
                        _sb(f"a_ytr{r}", [128, KCQ, 128], BF16))
                        for r in range(4)]
                    yti = [phh.enter_context(
                        _sb(f"a_yti{r}", [128, KCQ, 128], BF16))
                        for r in range(4)]
                    d08 = phh.enter_context(_sb("a_d08", [128, 4], I32))
                    cof8 = phh.enter_context(_sb("a_cof8", [128, 4], F32))
                    es_x = contextlib.ExitStack()
                    xrb = [es_x.enter_context(_sb(f"a_xr{r}", [128, FQP], BF16))
                           for r in range(4)]
                    psb = [es_x.enter_context(_sb(f"a_ps{r}", [128, FQP], BF16))
                           for r in range(4)]

                    with contextlib.ExitStack() as ps:
                        xht = ps.enter_context(
                            _sb("a_xht", [128, KCF, HALF], BF16))
                        with contextlib.ExitStack() as ps2:
                            xt = ps2.enter_context(_sb("a_xt", [128, D], F32))
                            sqc = ps2.enter_context(
                                _sb("a_sqc", [128, 512], F32))
                            sqp = ps2.enter_context(
                                _sb("a_sqp", [128, 8], F32))
                            xhb = ps2.enter_context(
                                _sb("a_xhb", [128, D], BF16))
                            for r in range(4):
                                rb = (row0 + r * 128) * D
                                kb.op("sync", lambda e, rb=rb: e.dma_start(
                                    out=xt[:, :],
                                    in_=dram_ap(x_in, rb, [[D, 128], [1, D]])),
                                    writes=[("xt",)], dma=True, tag="A")
                                cb = 8 * (hf * 4 + r)
                                kb.op("vector", lambda e, cb=cb: e.reduce_sum(
                                    stats[:, cb:cb + 1], xt[:, :], axis=AX.X),
                                    reads=[("xt",)], writes=[("st", cb)],
                                    tag="A")
                                for q8 in range(8):
                                    kb.op("scalar", lambda e, q8=q8:
                                          e.activation(
                                              sqc[:, :],
                                              xt[:, q8 * 512:(q8 + 1) * 512],
                                              AF.Square),
                                          reads=[("xt",)], writes=[("sqc",)],
                                          tag="A")
                                    kb.op("vector", lambda e, q8=q8:
                                          e.reduce_sum(
                                              sqp[:, q8:q8 + 1], sqc[:, :],
                                              axis=AX.X),
                                          reads=[("sqc",)],
                                          writes=[("sqp", q8)], tag="A")
                                kb.op("vector", lambda e, cb=cb: e.reduce_sum(
                                    stats[:, cb + 1:cb + 2], sqp[:, :],
                                    axis=AX.X),
                                    reads=[("sqp", q) for q in range(8)],
                                    writes=[("st", cb + 1)], tag="A")
                                kb.op("vector", lambda e, cb=cb:
                                      e.tensor_scalar(
                                          stats[:, cb + 2:cb + 3],
                                          stats[:, cb:cb + 1], 1.0 / D, None,
                                          ALU.mult),
                                      reads=[("st", cb)],
                                      writes=[("st", cb + 2)], tag="A")
                                kb.op("vector", lambda e, cb=cb:
                                      e.tensor_scalar(
                                          stats[:, cb + 3:cb + 4],
                                          stats[:, cb + 1:cb + 2], 1.0 / D,
                                          None, ALU.mult),
                                      reads=[("st", cb + 1)],
                                      writes=[("st", cb + 3)], tag="A")
                                kb.op("vector", lambda e, cb=cb:
                                      e.tensor_tensor(
                                          stats[:, cb + 4:cb + 5],
                                          stats[:, cb + 2:cb + 3],
                                          stats[:, cb + 2:cb + 3], ALU.mult),
                                      reads=[("st", cb + 2)],
                                      writes=[("st", cb + 4)], tag="A")
                                kb.op("vector", lambda e, cb=cb:
                                      e.tensor_tensor(
                                          stats[:, cb + 5:cb + 6],
                                          stats[:, cb + 3:cb + 4],
                                          stats[:, cb + 4:cb + 5],
                                          ALU.subtract),
                                      reads=[("st", cb + 3), ("st", cb + 4)],
                                      writes=[("st", cb + 5)], tag="A")
                                kb.op("vector", lambda e, cb=cb:
                                      e.tensor_scalar(
                                          stats[:, cb + 5:cb + 6],
                                          stats[:, cb + 5:cb + 6], EPS, None,
                                          ALU.add),
                                      reads=[("st", cb + 5)],
                                      writes=[("st", cb + 5)], tag="A")
                                kb.op("scalar", lambda e, cb=cb: e.activation(
                                    stats[:, cb + 6:cb + 7],
                                    stats[:, cb + 5:cb + 6], AF.Ln),
                                    reads=[("st", cb + 5)],
                                    writes=[("st", cb + 6)], tag="A")
                                kb.op("scalar", lambda e, cb=cb: e.activation(
                                    stats[:, cb + 7:cb + 8],
                                    stats[:, cb + 6:cb + 7], AF.Exp,
                                    scale=-0.5),
                                    reads=[("st", cb + 6)],
                                    writes=[("st", cb + 7)], tag="A")
                                kb.op("vector", lambda e, cb=cb:
                                      e.scalar_tensor_tensor(
                                          stats[:, cb + 4:cb + 5],
                                          stats[:, cb + 2:cb + 3], -1.0,
                                          stats[:, cb + 7:cb + 8],
                                          ALU.mult, ALU.mult),
                                      reads=[("st", cb + 2), ("st", cb + 7)],
                                      writes=[("st", cb + 4)], tag="A")
                                kb.op("scalar", lambda e, cb=cb: e.activation(
                                    xhb[:, :], xt[:, :], AF.Identity,
                                    bias=stats[:, cb + 4:cb + 5],
                                    scale=stats[:, cb + 7:cb + 8]),
                                    reads=[("xt",), ("st", cb + 4),
                                           ("st", cb + 7)],
                                    writes=[("xhb",)], tag="A")
                                for kc in range(KCF):
                                    transpose128(
                                        xhb[:, kc * 128:(kc + 1) * 128],
                                        xht[:, kc, r * 128:(r + 1) * 128],
                                        [("xhb",)], [("xht", kc, r)], tag="A")
                                pump(3)
                        kb.fence("A")

                        with contextlib.ExitStack() as pf:
                            KCB = 4
                            ccb = [pf.enter_context(
                                _sb(f"a_cc{i}", [128, KCB * 512], BF16))
                                for i in range(2)]
                            scb = [pf.enter_context(
                                _sb(f"a_sc{i}", [128, KCB * 512], BF16))
                                for i in range(2)]
                            pxr = [pf.enter_context(
                                _pt(f"a_pxr{i}", [128, 512], F32))
                                for i in range(2)]
                            pps = [pf.enter_context(
                                _pt(f"a_pps{i}", [128, 512], F32))
                                for i in range(2)]
                            for nbi, (nb0, nbw) in enumerate(NB_FWD):
                                for rg in range(2):
                                    for kcb in range(KCF // KCB):
                                        p = kcb % 2
                                        kc0 = kcb * KCB
                                        pat = [[FQP, 128],
                                               [128 * FQP, KCB], [1, nbw]]
                                        kb.op("sync", lambda e, p=p, kc0=kc0,
                                              nb0=nb0, nbw=nbw, pat=pat:
                                              e.dma_start(
                                                  out=ccb[p][:, 0:KCB * nbw],
                                                  in_=dram_ap(
                                                      Cc_d,
                                                      kc0 * 128 * FQP + nb0,
                                                      pat)),
                                              writes=[("ccb", p)], dma=True,
                                              tag="A")
                                        kb.op("gpsimd", lambda e, p=p,
                                              kc0=kc0, nb0=nb0, nbw=nbw,
                                              pat=pat: e.dma_start(
                                                  out=scb[p][:, 0:KCB * nbw],
                                                  in_=dram_ap(
                                                      Sc_d,
                                                      kc0 * 128 * FQP + nb0,
                                                      pat)),
                                              writes=[("scb", p)], dma=True,
                                              tag="A")
                                        for kci in range(KCB):
                                            kc = kc0 + kci
                                            st_ = (kc == 0)
                                            sp_ = (kc == KCF - 1)
                                            c0 = kci * nbw
                                            for ri in range(2):
                                                r = rg * 2 + ri
                                                kb.op("tensor",
                                                      lambda e, r=r, ri=ri,
                                                      kc=kc, p=p, nbw=nbw,
                                                      c0=c0, st_=st_,
                                                      sp_=sp_: e.matmul(
                                                          pxr[ri][:, 0:nbw],
                                                          xht[:, kc,
                                                              r * 128:(r + 1) * 128],
                                                          ccb[p][:, c0:c0 + nbw],
                                                          start=st_, stop=sp_),
                                                      reads=[("xht", kc, r),
                                                             ("ccb", p)],
                                                      writes=[("pxr", ri)] if (st_ or sp_) else (),
                                                      tag="A")
                                                kb.op("tensor",
                                                      lambda e, r=r, ri=ri,
                                                      kc=kc, p=p, nbw=nbw,
                                                      c0=c0, st_=st_,
                                                      sp_=sp_: e.matmul(
                                                          pps[ri][:, 0:nbw],
                                                          xht[:, kc,
                                                              r * 128:(r + 1) * 128],
                                                          scb[p][:, c0:c0 + nbw],
                                                          start=st_, stop=sp_),
                                                      reads=[("xht", kc, r),
                                                             ("scb", p)],
                                                      writes=[("pps", ri)] if (st_ or sp_) else (),
                                                      tag="A")
                                        pump(1)
                                    for ri in range(2):
                                        r = rg * 2 + ri
                                        kb.op("vector", lambda e, r=r, ri=ri,
                                              nb0=nb0, nbw=nbw: e.tensor_copy(
                                                  xrb[r][:, nb0:nb0 + nbw],
                                                  pxr[ri][:, 0:nbw]),
                                              reads=[("pxr", ri)],
                                              writes=[("xr", r)], tag="A")
                                        kb.op("vector", lambda e, r=r, ri=ri,
                                              nb0=nb0, nbw=nbw: e.tensor_copy(
                                                  psb[r][:, nb0:nb0 + nbw],
                                                  pps[ri][:, 0:nbw]),
                                              reads=[("pps", ri)],
                                              writes=[("ps", r)], tag="A")
                                    pump(3)
                            kb.fence("A")
                    kb.fence("A")

                    with contextlib.ExitStack() as ps:
                        corb = [ps.enter_context(
                            _sb(f"a_cor{r}", [128, 2048], BF16))
                            for r in range(4)]
                        with contextlib.ExitStack() as psp:
                            ptb = [psp.enter_context(
                                _sb(f"a_pt{r}", [128, KCQ, 128], BF16))
                                for r in range(4)]
                            with contextlib.ExitStack() as ps2:
                                tA = ps2.enter_context(
                                    _sb("a_tA", [128, 1088], BF16))
                                tB = ps2.enter_context(
                                    _sb("a_tB", [128, 1088], BF16))
                                pb = ps2.enter_context(
                                    _sb("a_pb", [128, FQP], BF16))
                                for r in range(4):
                                    for ch in range(2):
                                        c0 = ch * 1088
                                        kb.op("scalar", lambda e, r=r, c0=c0:
                                              e.activation(
                                                  tA[:, :],
                                                  xrb[r][:, c0:c0 + 1088],
                                                  AF.Square),
                                              reads=[("xr", r)],
                                              writes=[("tA",)], tag="A")
                                        kb.op("vector", lambda e, r=r, c0=c0:
                                              e.tensor_tensor(
                                                  tB[:, :],
                                                  psb[r][:, c0:c0 + 1088],
                                                  psb[r][:, c0:c0 + 1088],
                                                  ALU.mult),
                                              reads=[("ps", r)],
                                              writes=[("tB",)], tag="A")
                                        kb.op("vector", lambda e, c0=c0:
                                              e.tensor_tensor(
                                                  pb[:, c0:c0 + 1088],
                                                  tA[:, :], tB[:, :],
                                                  ALU.add),
                                              reads=[("tA",), ("tB",)],
                                              writes=[("pb", c0)], tag="A")
                                    for kq in range(KCQ):
                                        transpose128(
                                            pb[:, kq * 128:(kq + 1) * 128],
                                            ptb[r][:, kq, :],
                                            [("pb", 0), ("pb", 1088)],
                                            [("ptb", r, kq)], tag="A")
                                    pump(3)
                                kb.fence("A")

                            with contextlib.ExitStack() as ps2:
                                cib = ps2.enter_context(
                                    _sb("a_ci", [128, KCQ * 512], BF16))
                                pc = [ps2.enter_context(
                                    _pt(f"a_pc{i}", [128, 512], F32))
                                    for i in range(4)]
                                for n in range(4):
                                    kb.op("gpsimd", lambda e, n=n:
                                          e.dma_start(
                                              out=cib[:, :],
                                              in_=dram_ap(
                                                  CiC_d, n * 512,
                                                  [[D, 128],
                                                   [128 * D, KCQ],
                                                   [1, 512]])),
                                          writes=[("cib",)],
                                          dma=True, tag="A")
                                    for rg in range(2):
                                        par = rg % 2
                                        for kq in range(KCQ):
                                            for ri in range(2):
                                                r = rg * 2 + ri
                                                bank = par * 2 + ri
                                                st_ = (kq == 0)
                                                sp_ = (kq == KCQ - 1)
                                                kb.op("tensor",
                                                      lambda e, r=r, kq=kq,
                                                      bank=bank,
                                                      st_=st_, sp_=sp_:
                                                      e.matmul(
                                                          pc[bank][:, :],
                                                          ptb[r][:, kq, :],
                                                          cib[:, kq * 512:
                                                              (kq + 1) * 512],
                                                          start=st_,
                                                          stop=sp_),
                                                      reads=[("ptb", r, kq),
                                                             ("cib",)],
                                                      writes=[("pc", bank)] if (st_ or sp_) else (),
                                                      tag="A")
                                        for ri in range(2):
                                            r = rg * 2 + ri
                                            bank = par * 2 + ri
                                            kb.op("vector", lambda e, r=r,
                                                  n=n, bank=bank:
                                                  e.tensor_copy(
                                                      corb[r][:, n * 512:
                                                              (n + 1) * 512],
                                                      pc[bank][:, :]),
                                                  reads=[("pc", bank)],
                                                  writes=[("cor", r, n)],
                                                  tag="A")
                                        pump(3)
                                kb.fence("A")

                        with contextlib.ExitStack() as ps2:
                            t8 = ps2.enter_context(_sb("a_t8", [128, 8], BF16))
                            i8 = ps2.enter_context(_sb("a_i8", [128, 8], U32))
                            dvb = ps2.enter_context(_sb("a_dv", [128, 8], F32))
                            exb = ps2.enter_context(_sb("a_ex", [128, 8], F32))
                            den = ps2.enter_context(_sb("a_den", [128, 1], F32))
                            for r in range(4):
                                corres = [("cor", r, n) for n in range(4)]
                                kb.op("vector", lambda e, r=r: e.max(
                                    t8[:, :], corb[r][:, :]),
                                    reads=corres, writes=[("t8",)], tag="A")
                                kb.op("vector", lambda e, r=r: e.max_index(
                                    i8[:, :], t8[:, :], corb[r][:, :]),
                                    reads=corres + [("t8",)],
                                    writes=[("i8",)], tag="A")
                                kb.op("vector", lambda e: e.tensor_tensor(
                                    dvb[:, :], t8[:, :],
                                    t8[:, 0:1].broadcast_to((128, 8)),
                                    ALU.subtract),
                                    reads=[("t8",)], writes=[("dv",)],
                                    tag="A")
                                kb.op("scalar", lambda e: e.activation(
                                    exb[:, :], dvb[:, :], AF.Exp,
                                    accum_out=den[:, :]),
                                    reads=[("dv",)],
                                    writes=[("den",), ("exv",)], tag="A")
                                kb.op("vector", lambda e, r=r: e.reciprocal(
                                    cof8[:, r:r + 1], den[:, :]),
                                    reads=[("den",)], writes=[("cof", r)],
                                    tag="A")
                                kb.op("vector", lambda e, r=r: e.tensor_copy(
                                    d08[:, r:r + 1], i8[:, 0:1]),
                                    reads=[("i8",)], writes=[("d0", r)],
                                    tag="A")
                                pump(1)
                        kb.fence("A")

                    with contextlib.ExitStack() as ps2:
                        mi1 = ps2.enter_context(_sb("a_mi1", [128, 1088], I32))
                        mi2 = ps2.enter_context(_sb("a_mi2", [128, 1088], I32))
                        tf1 = ps2.enter_context(_sb("a_tf1", [128, 1088], F32))
                        tf2 = ps2.enter_context(_sb("a_tf2", [128, 1088], F32))
                        csb = ps2.enter_context(_sb("a_cs", [128, FQP], BF16))
                        snb = ps2.enter_context(_sb("a_sn", [128, FQP], BF16))
                        y1 = ps2.enter_context(_sb("a_y1", [128, 1088], BF16))
                        y2 = ps2.enter_context(_sb("a_y2", [128, 1088], BF16))
                        y3 = ps2.enter_context(_sb("a_y3", [128, FQP], BF16))

                        for r in range(4):
                            for ch in range(2):
                                c0 = ch * 1088
                                kb.op("vector", lambda e, c0=c0, r=r:
                                      e.tensor_tensor(
                                          mi1[:, :],
                                          iota_i[:, c0:c0 + 1088],
                                          d08[:, r:r + 1].broadcast_to(
                                              (128, 1088)), ALU.mult),
                                      reads=[("iota",), ("d0", r)],
                                      writes=[("mi1",)], tag="A")
                                kb.op("vector", lambda e: e.tensor_tensor(
                                    mi2[:, :], mi1[:, :],
                                    c1024[:, :].broadcast_to((128, 1088)),
                                    ALU.add),
                                    reads=[("mi1",), ("c1024",)],
                                    writes=[("mi2",)], tag="A")
                                for mi, tf, outb, mk, tk, ok_, sk in (
                                        (mi1, tf1, snb, ("mi1",), ("tf1",),
                                         ("tf2",), ("snb",)),
                                        (mi2, tf2, csb, ("mi2",), ("tf2",),
                                         ("tf1",), ("csb",))):
                                    oth = tf2 if tf is tf1 else tf1
                                    kb.op("vector", lambda e, mi=mi, mk=mk:
                                          e.tensor_single_scalar(
                                              mi[:, :], mi[:, :], 4095,
                                              ALU.bitwise_and),
                                          reads=[mk], writes=[mk], tag="A")
                                    kb.op("vector", lambda e, mi=mi, tf=tf:
                                          e.tensor_copy(tf[:, :], mi[:, :]),
                                          reads=[mk], writes=[tk], tag="A")
                                    kb.op("vector", lambda e, tf=tf:
                                          e.tensor_scalar(
                                              tf[:, :], tf[:, :],
                                              PHASE_SCALE, None, ALU.mult),
                                          reads=[tk], writes=[tk], tag="A")
                                    kb.op("vector", lambda e, tf=tf, oth=oth:
                                          e.tensor_scalar(
                                              oth[:, :], tf[:, :],
                                              float(np.pi), TWO_PI,
                                              ALU.is_gt, ALU.mult),
                                          reads=[tk], writes=[ok_], tag="A")
                                    kb.op("vector", lambda e, tf=tf, oth=oth:
                                          e.tensor_tensor(
                                              tf[:, :], tf[:, :], oth[:, :],
                                              ALU.subtract),
                                          reads=[tk, ok_], writes=[tk],
                                          tag="A")
                                    kb.op("scalar", lambda e, tf=tf,
                                          outb=outb, c0=c0: e.activation(
                                              outb[:, c0:c0 + 1088],
                                              tf[:, :], AF.Sin),
                                          reads=[tk], writes=[sk], tag="A")
                                pump(2)
                            for which in range(2):
                                a_op = ALU.add if which == 0 else ALU.subtract
                                m1 = csb if which == 0 else snb
                                m2 = snb if which == 0 else csb
                                yt = ytr if which == 0 else yti
                                ykey = "ytr" if which == 0 else "yti"
                                for ch in range(2):
                                    c0 = ch * 1088
                                    kb.op("vector", lambda e, r=r, c0=c0,
                                          m1=m1: e.tensor_tensor(
                                              y1[:, :],
                                              xrb[r][:, c0:c0 + 1088],
                                              m1[:, c0:c0 + 1088], ALU.mult),
                                          reads=[("xr", r), ("csb",),
                                                 ("snb",)],
                                          writes=[("y1",)], tag="A")
                                    kb.op("vector", lambda e, r=r, c0=c0,
                                          m2=m2: e.tensor_tensor(
                                              y2[:, :],
                                              psb[r][:, c0:c0 + 1088],
                                              m2[:, c0:c0 + 1088], ALU.mult),
                                          reads=[("ps", r), ("csb",),
                                                 ("snb",)],
                                          writes=[("y2",)], tag="A")
                                    kb.op("vector", lambda e, c0=c0,
                                          a_op=a_op: e.tensor_tensor(
                                              y3[:, c0:c0 + 1088], y1[:, :],
                                              y2[:, :], a_op),
                                          reads=[("y1",), ("y2",)],
                                          writes=[("y3", c0)], tag="A")
                                    kb.op("vector", lambda e, c0=c0, r=r:
                                          e.tensor_scalar(
                                              y3[:, c0:c0 + 1088],
                                              y3[:, c0:c0 + 1088],
                                              cof8[:, r:r + 1], None,
                                              ALU.mult),
                                          reads=[("y3", c0), ("cof", r)],
                                          writes=[("y3", c0)], tag="A")
                                for kq in range(KCQ):
                                    transpose128(
                                        y3[:, kq * 128:(kq + 1) * 128],
                                        yt[r][:, kq, :],
                                        [("y3", 0), ("y3", 1088)],
                                        [(ykey, r, kq)], tag="A")
                                pump(2)
                        kb.fence("A")

                    es_x.close()
                    with contextlib.ExitStack() as ps:
                        cb2 = [ps.enter_context(
                            _sb(f"a_cj{i}", [128, KCQ * 512], BF16))
                            for i in range(2)]
                        sb2 = [ps.enter_context(
                            _sb(f"a_sj{i}", [128, KCQ * 512], BF16))
                            for i in range(2)]
                        xsl = [ps.enter_context(
                            _sb(f"a_xs{i}", [128, 512], F32))
                            for i in range(2)]
                        o1b = [ps.enter_context(
                            _sb(f"a_o1{i}", [128, 512], F32))
                            for i in range(2)]
                        pr = [ps.enter_context(
                            _pt(f"a_pr{i}", [128, 512], F32))
                            for i in range(4)]
                        for n in range(8):
                            p = n % 2
                            kb.op("sync", lambda e, p=p, n=n:
                                  e.dma_start(
                                      out=cb2[p][:, :],
                                      in_=dram_ap(
                                          CiC_d, n * 512,
                                          [[D, 128], [128 * D, KCQ],
                                           [1, 512]])),
                                  writes=[("cj", p)], dma=True,
                                  tag="A")
                            kb.op("gpsimd", lambda e, p=p, n=n:
                                  e.dma_start(
                                      out=sb2[p][:, :],
                                      in_=dram_ap(
                                          CiS_d, n * 512,
                                          [[D, 128], [128 * D, KCQ],
                                           [1, 512]])),
                                  writes=[("sj", p)], dma=True,
                                  tag="A")
                            for rg in range(2):
                                par = (n * 2 + rg) % 2
                                for kq in range(KCQ):
                                    for ri in range(2):
                                        r = rg * 2 + ri
                                        bank = par * 2 + ri
                                        kb.op("tensor", lambda e, r=r, kq=kq,
                                              p=p, bank=bank, st_=(kq == 0):
                                              e.matmul(pr[bank][:, :],
                                                       ytr[r][:, kq, :],
                                                       cb2[p][:, kq * 512:
                                                              (kq + 1) * 512],
                                                       start=st_, stop=False),
                                              reads=[("ytr", r, kq),
                                                     ("cj", p)],
                                              writes=[("pr", bank)] if kq == 0 else (),
                                              tag="A")
                                        kb.op("tensor", lambda e, r=r, kq=kq,
                                              p=p, bank=bank,
                                              sp_=(kq == KCQ - 1):
                                              e.matmul(pr[bank][:, :],
                                                       yti[r][:, kq, :],
                                                       sb2[p][:, kq * 512:
                                                              (kq + 1) * 512],
                                                       start=False, stop=sp_),
                                              reads=[("yti", r, kq),
                                                     ("sj", p)],
                                              writes=[("pr", bank)] if kq == KCQ - 1 else (),
                                              tag="A")
                                for ri in range(2):
                                    r = rg * 2 + ri
                                    bank = par * 2 + ri
                                    bp = (n * 4 + r) % 2
                                    off = (row0 + r * 128) * D + n * 512
                                    kb.op("sync", lambda e, bp=bp, off=off:
                                          e.dma_start(
                                              out=xsl[bp][:, :],
                                              in_=dram_ap(x_in, off,
                                                          [[D, 128],
                                                           [1, 512]])),
                                          writes=[("xsl", bp)], dma=True,
                                          tag="A")
                                    kb.op("vector", lambda e, bank=bank,
                                          bp=bp: e.tensor_tensor(
                                              o1b[bp][:, :], pr[bank][:, :],
                                              xsl[bp][:, :], ALU.add),
                                          reads=[("pr", bank), ("xsl", bp)],
                                          writes=[("o1b", bp)], tag="A")
                                    kb.op("sync", lambda e, bp=bp, off=off:
                                          e.dma_start(
                                              out=dram_ap(out1_d, off,
                                                          [[D, 128],
                                                           [1, 512]]),
                                              in_=o1b[bp][:, :]),
                                          reads=[("o1b", bp)],
                                          writes=[("out1", off)], dma=True,
                                          tag="A")
                                pump(3)
                        kb.fence("A")
                    kb.fence("A")

            while not dq_done[0]:
                try:
                    next(dgen)
                except StopIteration:
                    dq_done[0] = True
        es_d.close()
        kb.barrier()

        out1_res = [("out1", (r * 128) * D + n * 512)
                    for r in range(8) for n in range(8)]

        with contextlib.ExitStack() as pf:
            x2t = pf.enter_context(_sb("f_x2t", [128, KCF, RPC], BF16))

            with contextlib.ExitStack() as ps:
                ot = ps.enter_context(_sb("f_ot", [128, D], F32))
                fsqc = ps.enter_context(_sb("f_sqc", [128, 512], F32))
                fsqp = ps.enter_context(_sb("f_sqp", [128, 8], F32))
                fxhb = ps.enter_context(_sb("f_xhb", [128, D], BF16))
                for rt in range(8):
                    rb = rt * 128 * D
                    kb.op("sync", lambda e, rb=rb: e.dma_start(
                        out=ot[:, :],
                        in_=dram_ap(out1_d, rb, [[D, 128], [1, D]])),
                        reads=out1_res, writes=[("f_ot",)], dma=True)
                    cb = 64 + 8 * (rt % 4)
                    kb.op("vector", lambda e, cb=cb: e.reduce_sum(
                        stats[:, cb:cb + 1], ot[:, :], axis=AX.X),
                        reads=[("f_ot",)], writes=[("st", cb)])
                    for q8 in range(8):
                        kb.op("scalar", lambda e, q8=q8: e.activation(
                            fsqc[:, :], ot[:, q8 * 512:(q8 + 1) * 512],
                            AF.Square),
                            reads=[("f_ot",)], writes=[("sqc",)])
                        kb.op("vector", lambda e, q8=q8: e.reduce_sum(
                            fsqp[:, q8:q8 + 1], fsqc[:, :], axis=AX.X),
                            reads=[("sqc",)], writes=[("sqp", q8)])
                    kb.op("vector", lambda e, cb=cb: e.reduce_sum(
                        stats[:, cb + 1:cb + 2], fsqp[:, :], axis=AX.X),
                        reads=[("sqp", q) for q in range(8)],
                        writes=[("st", cb + 1)])
                    kb.op("vector", lambda e, cb=cb: e.tensor_scalar(
                        stats[:, cb + 2:cb + 3], stats[:, cb:cb + 1],
                        1.0 / D, None, ALU.mult),
                        reads=[("st", cb)], writes=[("st", cb + 2)])
                    kb.op("vector", lambda e, cb=cb: e.tensor_scalar(
                        stats[:, cb + 3:cb + 4], stats[:, cb + 1:cb + 2],
                        1.0 / D, None, ALU.mult),
                        reads=[("st", cb + 1)], writes=[("st", cb + 3)])
                    kb.op("vector", lambda e, cb=cb: e.tensor_tensor(
                        stats[:, cb + 4:cb + 5], stats[:, cb + 2:cb + 3],
                        stats[:, cb + 2:cb + 3], ALU.mult),
                        reads=[("st", cb + 2)], writes=[("st", cb + 4)])
                    kb.op("vector", lambda e, cb=cb: e.tensor_tensor(
                        stats[:, cb + 5:cb + 6], stats[:, cb + 3:cb + 4],
                        stats[:, cb + 4:cb + 5], ALU.subtract),
                        reads=[("st", cb + 3), ("st", cb + 4)],
                        writes=[("st", cb + 5)])
                    kb.op("vector", lambda e, cb=cb: e.tensor_scalar(
                        stats[:, cb + 5:cb + 6], stats[:, cb + 5:cb + 6],
                        EPS, None, ALU.add),
                        reads=[("st", cb + 5)], writes=[("st", cb + 5)])
                    kb.op("scalar", lambda e, cb=cb: e.activation(
                        stats[:, cb + 6:cb + 7], stats[:, cb + 5:cb + 6],
                        AF.Ln),
                        reads=[("st", cb + 5)], writes=[("st", cb + 6)])
                    kb.op("scalar", lambda e, cb=cb: e.activation(
                        stats[:, cb + 7:cb + 8], stats[:, cb + 6:cb + 7],
                        AF.Exp, scale=-0.5),
                        reads=[("st", cb + 6)], writes=[("st", cb + 7)])
                    kb.op("vector", lambda e, cb=cb: e.scalar_tensor_tensor(
                        stats[:, cb + 4:cb + 5], stats[:, cb + 2:cb + 3],
                        -1.0, stats[:, cb + 7:cb + 8], ALU.mult, ALU.mult),
                        reads=[("st", cb + 2), ("st", cb + 7)],
                        writes=[("st", cb + 4)])
                    kb.op("scalar", lambda e, cb=cb: e.activation(
                        fxhb[:, :], ot[:, :], AF.Identity,
                        bias=stats[:, cb + 4:cb + 5],
                        scale=stats[:, cb + 7:cb + 8]),
                        reads=[("f_ot",), ("st", cb + 4), ("st", cb + 7)],
                        writes=[("f_xhb",)])
                    for kc in range(KCF):
                        transpose128(fxhb[:, kc * 128:(kc + 1) * 128],
                                     x2t[:, kc, rt * 128:(rt + 1) * 128],
                                     [("f_xhb",)], [("x2t", kc, rt)])
                kb.barrier()
            es_tp.close()

            x2t_kc = {kc: [("x2t", kc, rt) for rt in range(8)]
                      for kc in range(KCF)}

            ht = pf.enter_context(_sb("f_ht", [128, 43, RPC], BF16))

            def slab_ap(mat, hc):
                r, j = divmod(hc, HPC)
                dqa_l = dqa1_d if mat == 1 else dqa2_d
                base = r * DQ1_ELEMS + j * SLAB_W
                return dram_ap(dqa_l, base, [[4096, 128], [1, 4096]])

            def piece_ap(hc, nq):
                base = nq * DQ3_ELEMS + hc * PIECE_W
                return dram_ap(dqa3_d, base, [[512, 128], [1, 512]])

            for hph in range(2):
                hcs = list(range(hph * 43, hph * 43 + 43))

                with contextlib.ExitStack() as ps:
                    s1b = [ps.enter_context(
                        _sb(f"f_s1{i}", [128, KCF, 128], BF16))
                        for i in range(2)]
                    s2b = [ps.enter_context(
                        _sb(f"f_s2{i}", [128, KCF, 128], BF16))
                        for i in range(2)]
                    sg = ps.enter_context(_sb("f_sg", [128, RPC], F32))
                    pg = [ps.enter_context(_pt(f"f_pg{i}", [128, RPC], F32))
                          for i in range(2)]
                    pu = [ps.enter_context(_pt(f"f_pu{i}", [128, RPC], F32))
                          for i in range(2)]
                    for i, hc in enumerate(hcs):
                        par = i % 2
                        kb.op("sync", lambda e, par=par, hc=hc: e.dma_start(
                            out=s1b[par][:, :, :].rearrange(
                                "p a b -> p (a b)"),
                            in_=slab_ap(1, hc)),
                            reads=[("dqa1",)], writes=[("s1b", par)],
                            dma=True)
                        kb.op("sync", lambda e, par=par, hc=hc: e.dma_start(
                            out=s2b[par][:, :, :].rearrange(
                                "p a b -> p (a b)"),
                            in_=slab_ap(2, hc)),
                            reads=[("dqa2",)], writes=[("s2b", par)],
                            dma=True)
                        for kc in range(KCF):
                            for ns in range(2):
                                st_ = (kc == 0)
                                sp_ = (kc == KCF - 1)
                                kb.op("tensor", lambda e, par=par, kc=kc,
                                      ns=ns, st_=st_, sp_=sp_: e.matmul(
                                          pg[par][:, ns * 512:(ns + 1) * 512],
                                          s1b[par][:, kc, :],
                                          x2t[:, kc, ns * 512:(ns + 1) * 512],
                                          start=st_, stop=sp_),
                                      reads=[("s1b", par)] + x2t_kc[kc],
                                      writes=[("pg", par)] if (st_ or sp_) else ())
                                kb.op("tensor", lambda e, par=par, kc=kc,
                                      ns=ns, st_=st_, sp_=sp_: e.matmul(
                                          pu[par][:, ns * 512:(ns + 1) * 512],
                                          s2b[par][:, kc, :],
                                          x2t[:, kc, ns * 512:(ns + 1) * 512],
                                          start=st_, stop=sp_),
                                      reads=[("s2b", par)] + x2t_kc[kc],
                                      writes=[("pu", par)] if (st_ or sp_) else ())
                        kb.op("scalar", lambda e, par=par: e.activation(
                            sg[:, :], pg[par][:, :], AF.Silu),
                            reads=[("pg", par)], writes=[("f_sg",)])
                        kb.op("vector", lambda e, par=par, i=i:
                              e.tensor_tensor(ht[:, i, :], sg[:, :],
                                              pu[par][:, :], ALU.mult),
                              reads=[("f_sg",), ("pg", par), ("pu", par)],
                              writes=[("ht", i)])
                    kb.barrier()

                with contextlib.ExitStack() as ps:
                    pcb = [ps.enter_context(
                        _sb(f"f_pc{i}", [128, 4, 512], BF16))
                        for i in range(2)]
                    o1c = [ps.enter_context(_sb(f"f_o1c{i}", [128, 512], F32))
                           for i in range(2)]
                    ob = [ps.enter_context(_sb(f"f_ob{i}", [128, 512], F32))
                          for i in range(2)]
                    pd = [ps.enter_context(_pt(f"f_pd{i}", [128, 512], F32))
                          for i in range(8)]
                    nkb = (43 + 3) // 4
                    for n in range(8):
                        for mg in range(2):
                            for kbatch in range(nkb):
                                ks = list(range(kbatch * 4,
                                                min(43, kbatch * 4 + 4)))
                                bp = kbatch % 2
                                for ki, k in enumerate(ks):
                                    kb.op("gpsimd",
                                          lambda e, bp=bp, ki=ki, k=k, n=n,
                                          hph=hph: e.dma_start(
                                              out=pcb[bp][:, ki, :],
                                              in_=piece_ap(hph * 43 + k, n)),
                                          reads=[("dqa3",)],
                                          writes=[("pcb", bp, ki)], dma=True)
                                for ki, k in enumerate(ks):
                                    for m in range(4):
                                        bank = mg * 4 + m
                                        st_ = (kbatch == 0 and ki == 0)
                                        sp_ = (kbatch == nkb - 1 and
                                               ki == len(ks) - 1)
                                        mslice = (mg * 4 + m) * 128
                                        kb.op("tensor",
                                              lambda e, bp=bp, ki=ki, k=k,
                                              m=m, bank=bank, st_=st_,
                                              sp_=sp_, mslice=mslice:
                                              e.matmul(
                                                  pd[bank][:, :],
                                                  ht[:, k,
                                                     mslice:mslice + 128],
                                                  pcb[bp][:, ki, :],
                                                  start=st_, stop=sp_),
                                              reads=[("ht", k),
                                                     ("pcb", bp, ki)],
                                              writes=[("pd", bank)] if (st_ or sp_) else ())
                            srcd = out1_d if hph == 0 else part_d
                            for m in range(4):
                                bank = mg * 4 + m
                                bp = m % 2
                                off = (mg * 512 + m * 128) * D + n * 512
                                kb.op("sync", lambda e, bp=bp, off=off,
                                      srcd=srcd: e.dma_start(
                                          out=o1c[bp][:, :],
                                          in_=dram_ap(srcd, off,
                                                      [[D, 128], [1, 512]])),
                                      reads=out1_res if hph == 0 else [("part",)],
                                      writes=[("o1c", bp)], dma=True)
                                kb.op("vector", lambda e, bp=bp, bank=bank:
                                      e.tensor_tensor(
                                          ob[bp][:, :], pd[bank][:, :],
                                          o1c[bp][:, :], ALU.add),
                                      reads=[("pd", bank), ("o1c", bp)],
                                      writes=[("ob", bp)])
                                if hph == 0:
                                    kb.op("sync", lambda e, bp=bp, off=off:
                                          e.dma_start(
                                              out=dram_ap(part_d, off,
                                                          [[D, 128],
                                                           [1, 512]]),
                                              in_=ob[bp][:, :]),
                                          reads=[("ob", bp)],
                                          writes=[("part",)], dma=True)
                                else:
                                    kb.op("sync", lambda e, bp=bp, off=off:
                                          e.dma_start(
                                              out=dram_ap(out_d, off,
                                                          [[D, 128],
                                                           [1, 512]]),
                                              in_=ob[bp][:, :]),
                                          reads=[("ob", bp)],
                                          writes=[("outd", off)], dma=True)
                    kb.barrier()

        kb.barrier()

        with nc.Block() as block:
            kb.build(block)

    return nc


_NC_CACHE = {}


def kernel(L0_attn_x, L0_ffn_w1, L0_ffn_w2, L0_ffn_w3):
    x = np.ascontiguousarray(np.asarray(L0_attn_x, np.float32).reshape(8192, D))
    w1 = np.asarray(L0_ffn_w1, np.float32)
    w2 = np.asarray(L0_ffn_w2, np.float32)
    w3 = np.asarray(L0_ffn_w3, np.float32)

    w1p = np.zeros((HPAD, D), np.float32)
    w1p[:H] = w1
    w2p = np.zeros((HPAD, D), np.float32)
    w2p[:H] = w2
    w3p = np.zeros((D, HPAD), np.float32)
    w3p[:, :H] = w3

    Cc, Sc, CiCc, CiSc, ident, bcol = _host_consts()

    if "nc" not in _NC_CACHE:
        _NC_CACHE["nc"] = build_program()
    nc = _NC_CACHE["nc"]

    in_maps = []
    for c in range(NCORES):
        in_maps.append({
            "x_in": np.ascontiguousarray(x[c * RPC:(c + 1) * RPC]),
            "w1s": np.ascontiguousarray(w1p[c * 1408:(c + 1) * 1408]),
            "w2s": np.ascontiguousarray(w2p[c * 1408:(c + 1) * 1408]),
            "w3s": np.ascontiguousarray(w3p[c * 512:(c + 1) * 512]),
            "Cc": Cc, "Sc": Sc, "CiC": CiCc, "CiS": CiSc,
            "idn": ident, "bcol": bcol,
        })
    res = run_bass_kernel_spmd(nc, in_maps, core_ids=list(range(NCORES)))
    out = np.concatenate([np.asarray(r["out"]) for r in res.results], axis=0)
    return out.reshape(4, 2048, D)


# version 16
# speedup vs baseline: 1.2915x; 1.0025x over previous
import contextlib

import numpy as np
import ml_dtypes

import concourse.bass as bass
import concourse.mybir as mybir
from concourse.bass_utils import run_bass_kernel_spmd

F32 = mybir.dt.float32
BF16 = mybir.dt.bfloat16
I32 = mybir.dt.int32
U32 = mybir.dt.uint32
AF = mybir.ActivationFunctionType
ALU = mybir.AluOpType
AX = mybir.AxisListType
BF = ml_dtypes.bfloat16

NCORES = 8
D = 4096
RPC = 1024
HALF = 512
FQ = 2049
FQP = 2176
KCF = 32
KCQ = 17
NB_FWD = [(0, 512), (512, 512), (1024, 512), (1536, 512), (2048, 128)]
H = 11008
HPAD = 11264
NHC = 88
HC_REAL = 86
HPC = 11
EPS = 1e-5

NF4_CODE = np.array([-1.0, -0.6961928009986877, -0.5250730514526367,
                     -0.39491748809814453, -0.28444138169288635,
                     -0.18477343022823334, -0.09105003625154495, 0.0,
                     0.07958029955625534, 0.16093020141124725,
                     0.24611230194568634, 0.33791524171829224,
                     0.44070982933044434, 0.5626170039176941,
                     0.7229568362236023, 1.0], dtype=np.float32)
NF4_BOUND = ((NF4_CODE[:-1] + NF4_CODE[1:]) * 0.5).astype(np.float32)
NF4_E = ((NF4_CODE[1:] - NF4_CODE[:-1]) * 0.5).astype(np.float32)

SLAB_W = 128 * 4096
PIECE_W = 128 * 512
DQ1_ELEMS = HPC * SLAB_W
DQ3_ELEMS = NHC * PIECE_W
PHASE_SCALE = float(2 * np.pi / 4096)
TWO_PI = float(2 * np.pi)


def _host_consts():
    t = np.arange(D, dtype=np.float64)
    k = np.arange(FQP, dtype=np.float64)
    ang = 2 * np.pi * np.outer(t, k) / D
    C = np.cos(ang)
    S = np.sin(ang)
    C[:, FQ:] = 0.0
    S[:, FQ:] = 0.0
    w = np.full(FQP, 2.0)
    w[0] = 1.0
    w[2048] = 1.0
    w[FQ:] = 0.0
    angi = 2 * np.pi * np.outer(k, t) / D
    CiC = (w[:, None] * np.cos(angi)) / D
    CiS = (-w[:, None] * np.sin(angi)) / D
    Cc = np.ascontiguousarray(C.reshape(KCF, 128, FQP)).astype(BF)
    Sc = np.ascontiguousarray(S.reshape(KCF, 128, FQP)).astype(BF)
    CiCc = np.ascontiguousarray(CiC.reshape(KCQ, 128, D)).astype(BF)
    CiSc = np.ascontiguousarray(CiS.reshape(KCQ, 128, D)).astype(BF)
    ident = np.eye(128, dtype=BF)
    bcol = np.zeros((128, 16), np.float32)
    bcol[:, :15] = -NF4_BOUND[None, :]
    return Cc, Sc, CiCc, CiSc, ident, bcol


class KB:

    ENGS = ("sync", "scalar", "vector", "tensor", "gpsimd")
    DMA_POOL = 10

    def __init__(self, nc, es):
        self.nc = nc
        self.sem = {}
        self.cnt = {}
        names = ["sy", "sc", "ve", "te", "gp", "cc1", "cc2", "cc3"]
        for q in ("dA", "dB", "dG"):
            names += [f"{q}{i}" for i in range(self.DMA_POOL)]
        for name in names:
            self.sem[name] = es.enter_context(nc.semaphore("sem_" + name))
            self.cnt[name] = 0
        self.engsem = {"sync": "sy", "scalar": "sc", "vector": "ve",
                       "tensor": "te", "gpsimd": "gp"}
        self.dmaq = {"sync": "dA", "scalar": "dB", "gpsimd": "dG"}
        self.dseq = {"dA": 0, "dB": 0, "dG": 0}
        self.fake = {e: 0 for e in self.ENGS}
        self.thunks = {e: [] for e in self.ENGS}
        self.clk = {}
        self.semtop = {s: 0 for s in self.sem}
        self.lastw = {}
        self.readers = {}
        self.tagrec = {}
        self.fence_dep = {}

    @staticmethod
    def _join(a, b):
        if not b:
            return a
        out = dict(a)
        for k, v in b.items():
            if out.get(k, 0) < v:
                out[k] = v
        return out

    def _clock_of(self, s, v):
        c = self.clk.get((s, v))
        if c is not None:
            return c
        base = {s: v}
        for e, sn in self.engsem.items():
            if sn == s:
                base["F:" + e] = self.fake[e]
        return base

    def _emit(self, eng, fn, deps, inc_sem, inc_amt):
        cov = {}
        waits = []
        best = {}
        for s, v in deps:
            if best.get(s, 0) < v:
                best[s] = v
        order = sorted(best.items(), key=lambda kv: (kv[0][0] == "d", kv[0]))
        for s, v in order:
            if cov.get(s, 0) >= v:
                continue
            waits.append((s, v))
            cov = self._join(cov, self._clock_of(s, v))
        self.fake[eng] += 1
        cov["F:" + eng] = self.fake[eng]
        if inc_sem is not None:
            self.cnt[inc_sem] += inc_amt
            val = self.cnt[inc_sem]
            cov[inc_sem] = val
            prev = self.clk.get((inc_sem, self.semtop[inc_sem]), {})
            self.clk[(inc_sem, val)] = self._join(cov, prev)
            self.semtop[inc_sem] = val
            self.thunks[eng].append((waits, fn, inc_sem, inc_amt))
        else:
            self.thunks[eng].append((waits, fn, None, 0))

    def _record(self, rec, reads, writes, tag):
        for w in writes:
            self.lastw[w] = rec
            self.readers[w] = []
        for r in reads:
            self.readers.setdefault(r, []).append(rec)
        if tag is not None:
            tr = self.tagrec.setdefault(tag, {})
            s, v = rec
            if tr.get(s, 0) < v:
                tr[s] = v

    def op(self, eng, fn, reads=(), writes=(), dma=False, tag=None):
        deps = []
        fd = self.fence_dep.get((tag, eng))
        if fd is not None:
            deps.append(fd)
        for r in reads:
            lw = self.lastw.get(r)
            if lw:
                deps.append(lw)
        for w in writes:
            lw = self.lastw.get(w)
            if lw:
                deps.append(lw)
            deps.extend(self.readers.get(w, ()))
        if dma:
            q = self.dmaq[eng]
            n = self.dseq[q]
            self.dseq[q] = n + 1
            s = f"{q}{n % self.DMA_POOL}"
            if n >= self.DMA_POOL:
                deps.append((s, 16 * (n // self.DMA_POOL)))
            self._emit(eng, fn, deps, s, 16)
        else:
            s = self.engsem[eng]
            self._emit(eng, fn, deps, s, 1)
        self._record((s, self.cnt[s]), reads, writes, tag)

    def cc(self, fn, sem, reads=(), writes=()):
        deps = []
        for r in reads:
            lw = self.lastw.get(r)
            if lw:
                deps.append(lw)
        for w in writes:
            lw = self.lastw.get(w)
            if lw:
                deps.append(lw)
            deps.extend(self.readers.get(w, ()))
        self._emit("gpsimd", fn, deps, sem, 1)
        self._record((sem, self.cnt[sem]), reads, writes, None)

    def fence(self, tag):
        tr = dict(self.tagrec.get(tag, {}))
        for e in self.ENGS:
            deps = list(tr.items())
            fd = self.fence_dep.get((tag, e))
            if fd is not None:
                deps.append(fd)
            if e in ("vector", "gpsimd"):
                fn = (lambda en: en.engine_nop())
            else:
                fn = (lambda en: en.nop())
            s = self.engsem[e]
            self._emit(e, fn, deps, s, 1)
            self.fence_dep[(tag, e)] = (s, self.cnt[s])
            tr2 = self.tagrec.setdefault(tag, {})
            if tr2.get(s, 0) < self.cnt[s]:
                tr2[s] = self.cnt[s]

    def barrier(self):
        snap = [(s, v) for s, v in self.cnt.items() if v > 0]
        for e in self.ENGS:
            if e in ("vector", "gpsimd"):
                fn = (lambda en: en.engine_nop())
            else:
                fn = (lambda en: en.nop())
            s = self.engsem[e]
            self._emit(e, fn, snap, s, 1)
            rec = (s, self.cnt[s])
            for tag in set(t for (t, _) in self.fence_dep) | {"A", "D", None}:
                self.fence_dep[(tag, e)] = rec
            tr = self.tagrec.setdefault("A", {})

    def build(self, block):
        sems = self.sem
        for eng in self.ENGS:
            def make_body(thunks=self.thunks[eng]):
                def body(e):
                    for waits, fn, s, amt in thunks:
                        for ws, wv in waits:
                            e.wait_ge(sems[ws], wv)
                        if fn is not None:
                            ins = fn(e)
                            if s is not None:
                                ins.then_inc(sems[s], amt)
                return body
            getattr(block, eng)(make_body())


def build_program():
    nc = bass.Bass("TRN2", target_bir_lowering=False, debug=False)

    x_in = nc.dram_tensor("x_in", [RPC, D], F32, kind="ExternalInput")
    w1s = nc.dram_tensor("w1s", [HPC * 128, D], F32, kind="ExternalInput")
    w2s = nc.dram_tensor("w2s", [HPC * 128, D], F32, kind="ExternalInput")
    w3s = nc.dram_tensor("w3s", [512, HPAD], F32, kind="ExternalInput")
    Cc_d = nc.dram_tensor("Cc", [KCF, 128, FQP], BF16, kind="ExternalInput")
    Sc_d = nc.dram_tensor("Sc", [KCF, 128, FQP], BF16, kind="ExternalInput")
    CiC_d = nc.dram_tensor("CiC", [KCQ, 128, D], BF16, kind="ExternalInput")
    CiS_d = nc.dram_tensor("CiS", [KCQ, 128, D], BF16, kind="ExternalInput")
    idn_d = nc.dram_tensor("idn", [128, 128], BF16, kind="ExternalInput")
    bcol_d = nc.dram_tensor("bcol", [128, 16], F32, kind="ExternalInput")
    out_d = nc.dram_tensor("out", [RPC, D], F32, kind="ExternalOutput")

    out1_d = nc.dram_tensor("out1_s", [RPC, D], F32, kind="Internal")
    part_d = nc.dram_tensor("part_s", [RPC, D], F32, kind="Internal")
    dqm1_d = nc.dram_tensor("dqm1", [704, 8192], BF16, kind="Internal")
    dqm2_d = nc.dram_tensor("dqm2", [704, 8192], BF16, kind="Internal")
    dqm3_d = nc.dram_tensor("dqm3", [704, 8192], BF16, kind="Internal")
    dqa1_d = nc.dram_tensor("dqa1", [NCORES * 704, 8192], BF16,
                            kind="Internal", addr_space="Shared")
    dqa2_d = nc.dram_tensor("dqa2", [NCORES * 704, 8192], BF16,
                            kind="Internal", addr_space="Shared")
    dqa3_d = nc.dram_tensor("dqa3", [NCORES * 704, 8192], BF16,
                            kind="Internal", addr_space="Shared")

    def dram_ap(handle, offset, pattern):
        return bass.AP(handle, offset, pattern)

    _uid = [0]

    def _sb(name, shape, dt):
        _uid[0] += 1
        return nc.sbuf_tensor(f"{name}_{_uid[0]}", shape, dt)

    def _pt(name, shape, dt):
        _uid[0] += 1
        return nc.psum_tensor(f"{name}_{_uid[0]}", shape, dt)

    es_top = contextlib.ExitStack()
    with es_top:
        kb = KB(nc, es_top)

        idn = es_top.enter_context(_sb("idn_s", [128, 128], BF16))
        bcol = es_top.enter_context(_sb("bcol_s", [128, 16], F32))
        stats = es_top.enter_context(_sb("stats", [128, 96], F32))
        es_tp = contextlib.ExitStack()
        tpp = [es_tp.enter_context(_pt(f"tpp{i}", [128, 128], BF16))
               for i in range(2)]
        tpc = [0]

        kb.op("sync", lambda e: e.dma_start(out=idn[:, :], in_=idn_d.ap()),
              writes=[("idn",)], dma=True)
        kb.op("sync", lambda e: e.dma_start(out=bcol[:, :], in_=bcol_d.ap()),
              writes=[("bcol",)], dma=True)

        def transpose128(src_ap, dst_ap, reads, writes, tag=None):
            p = tpc[0] % 2
            tpc[0] += 1
            kb.op("tensor", lambda e, p=p, s=src_ap: e.transpose(
                tpp[p][:, :], s, idn[:, :]),
                reads=list(reads) + [("idn",)], writes=[("tpp", p)], tag=tag)
            kb.op("vector", lambda e, p=p, d=dst_ap: e.tensor_copy(
                d, tpp[p][:, :]),
                reads=[("tpp", p)], writes=list(writes), tag=tag)

        es_d = contextlib.ExitStack()
        wb2 = [es_d.enter_context(_sb(f"d_w{i}", [128, D], F32))
               for i in range(2)]
        sbf = [es_d.enter_context(_sb(f"d_s{i}", [128, 2048], BF16))
               for i in range(2)]
        accb = es_d.enter_context(_sb("d_acc", [128, 2048], F32))
        amc1 = es_d.enter_context(_sb("d_amc", [128, 64], F32))
        rcb = es_d.enter_context(_sb("d_rc", [128, 64], F32))
        dq2 = [es_d.enter_context(_sb(f"d_dq{i}", [128, D], BF16))
               for i in range(2)]
        slab1 = es_d.enter_context(_sb("d_slab", [128, D], BF16))
        tpd = [es_tp.enter_context(_pt(f"tpd{i}", [128, 128], BF16))
               for i in range(2)]
        tdc = [0]

        def transposeD(src_ap, dst_ap, reads, writes):
            p = tdc[0] % 2
            tdc[0] += 1
            kb.op("tensor", lambda e, p=p, s=src_ap: e.transpose(
                tpd[p][:, :], s, idn[:, :]),
                reads=list(reads) + [("idn",)], writes=[("tpd", p)], tag="D")
            kb.op("vector", lambda e, p=p, d=dst_ap: e.tensor_copy(
                d, tpd[p][:, :]),
                reads=[("tpd", p)], writes=list(writes), tag="D")

        def dq_math(src_ap, F, pm):
            nb = F // 64
            wbuf = wb2[pm]
            amc = amc1
            dqb = dq2[pm]
            kb.op("gpsimd", lambda e, a=src_ap, F=F, wbuf=wbuf:
                  e.dma_start(out=wbuf[:, 0:F], in_=a),
                  writes=[("wbuf", pm)], dma=True, tag="D")
            yield
            kb.op("vector", lambda e, F=F, nb=nb, wbuf=wbuf, amc=amc:
                  e.reduce_max(
                      amc[:, 0:nb],
                      wbuf[:, 0:F].rearrange("p (s k) -> p s k", k=64),
                      axis=AX.X, apply_absolute_value=True),
                  reads=[("wbuf", pm)], writes=[("am",)], tag="D")
            kb.op("vector", lambda e, nb=nb, amc=amc: e.tensor_scalar_max(
                amc[:, 0:nb], amc[:, 0:nb], 1e-30),
                reads=[("am",)], writes=[("amc", pm)], tag="D")
            kb.op("vector", lambda e, nb=nb, amc=amc: e.reciprocal(
                rcb[:, 0:nb], amc[:, 0:nb]),
                reads=[("amc", pm)], writes=[("rc",)], tag="D")
            kb.op("vector", lambda e, F=F, nb=nb, wbuf=wbuf: e.tensor_tensor(
                wbuf[:, 0:F].rearrange("p (s k) -> p s k", k=64),
                wbuf[:, 0:F].rearrange("p (s k) -> p s k", k=64),
                rcb[:, 0:nb].rearrange("p (s o) -> p s o", o=1)
                    .broadcast_to((128, nb, 64)),
                ALU.mult),
                reads=[("wbuf", pm), ("rc",)],
                writes=[("wbuf", pm), ("u", pm)],
                tag="D")
            yield
            for c0 in range(0, F, 2048):
                cw = min(2048, F - c0)
                cb = cw // 64
                for j in range(15):
                    p = j % 2
                    kb.op("scalar", lambda e, c0=c0, cw=cw, j=j, p=p,
                          wbuf=wbuf:
                          e.activation(sbf[p][:, 0:cw],
                                       wbuf[:, c0:c0 + cw], AF.Sign,
                                       bias=bcol[:, j:j + 1]),
                          reads=[("u", pm), ("wbuf", pm), ("bcol",)],
                          writes=[("sb", p)],
                          tag="D")
                    if j == 0:
                        kb.op("vector", lambda e, cw=cw, p=p:
                              e.tensor_scalar(accb[:, 0:cw],
                                              sbf[p][:, 0:cw],
                                              float(NF4_E[0]), None,
                                              ALU.mult),
                              reads=[("sb", p)], writes=[("acc",)],
                              tag="D")
                    else:
                        kb.op("vector", lambda e, cw=cw, j=j, p=p:
                              e.scalar_tensor_tensor(
                                  accb[:, 0:cw], sbf[p][:, 0:cw],
                                  float(NF4_E[j]), accb[:, 0:cw],
                                  ALU.mult, ALU.add),
                              reads=[("sb", p), ("acc",)],
                              writes=[("acc",)], tag="D")
                kb.op("vector", lambda e, c0=c0, cw=cw, cb=cb, amc=amc,
                      dqb=dqb: e.tensor_tensor(
                          dqb[:, c0:c0 + cw].rearrange(
                              "p (s k) -> p s k", k=64),
                          accb[:, 0:cw].rearrange(
                              "p (s k) -> p s k", k=64),
                          amc[:, c0 // 64:c0 // 64 + cb].rearrange(
                              "p (s o) -> p s o", o=1)
                              .broadcast_to((128, cb, 64)),
                          ALU.mult),
                      reads=[("acc",), ("amc", pm)],
                      writes=[("dq", pm, c0 // 2048)], tag="D")
                yield

        def d_gen():
            pm = 0
            for mat, wsrc, dqm in ((1, w1s, dqm1_d), (2, w2s, dqm2_d)):
                for j in range(HPC):
                    src = dram_ap(wsrc, j * 128 * D, [[D, 128], [1, D]])
                    yield from dq_math(src, D, pm)
                    dqb = dq2[pm]
                    slab = slab1
                    for kc in range(KCF):
                        transposeD(dqb[:, kc * 128:(kc + 1) * 128],
                                   slab[:, kc * 128:(kc + 1) * 128],
                                   [("dq", pm, (kc * 128) // 2048)],
                                   [("slab", kc)])
                        if kc in (7, 15, 23):
                            yield
                    res = (f"dqm{mat}", j)
                    kb.op("sync", lambda e, j=j, dqm=dqm, slab=slab:
                          e.dma_start(
                              out=dram_ap(dqm, j * SLAB_W,
                                          [[D, 128], [1, D]]),
                              in_=slab[:, :]),
                          reads=[("slab", kc) for kc in range(KCF)],
                          writes=[res], dma=True, tag="D")
                    pm = 1 - pm
                    yield
                sem = "cc1" if mat == 1 else "cc2"
                dqm_l, dqa_l = (dqm1_d, dqa1_d) if mat == 1 else (dqm2_d, dqa2_d)
                kb.cc(lambda e, dqm_l=dqm_l, dqa_l=dqa_l: e.collective_compute(
                    "AllGather", ALU.bypass,
                    replica_groups=[list(range(NCORES))],
                    ins=[dqm_l.ap()], outs=[dqa_l.ap()]),
                    sem=sem,
                    reads=[(f"dqm{mat}", j) for j in range(HPC)],
                    writes=[(f"dqa{mat}",)])
                yield
            for hcg in range(11):
                slab = slab1
                for dc in range(4):
                    src = dram_ap(w3s, dc * 128 * HPAD + hcg * 1024,
                                  [[HPAD, 128], [1, 1024]])
                    yield from dq_math(src, 1024, pm)
                    dqb = dq2[pm]
                    for j in range(8):
                        col = (j * 512 + dc * 128) // 128
                        transposeD(dqb[:, j * 128:(j + 1) * 128],
                                   slab[:, j * 512 + dc * 128:
                                        j * 512 + dc * 128 + 128],
                                   [("dq", pm, 0)], [("slab", col)])
                    pm = 1 - pm
                for j in range(8):
                    hc = hcg * 8 + j
                    res = ("dqm3", hc)
                    kb.op("sync", lambda e, hc=hc, j=j, slab=slab:
                          e.dma_start(
                              out=dram_ap(dqm3_d, hc * PIECE_W,
                                          [[512, 128], [1, 512]]),
                              in_=slab[:, j * 512:(j + 1) * 512]),
                          reads=[("slab", (j * 512 + dc * 128) // 128)
                                 for dc in range(4)],
                          writes=[res], dma=True, tag="D")
                yield
            kb.cc(lambda e: e.collective_compute(
                "AllGather", ALU.bypass,
                replica_groups=[list(range(NCORES))],
                ins=[dqm3_d.ap()], outs=[dqa3_d.ap()]),
                sem="cc3",
                reads=[("dqm3", hc) for hc in range(NHC)],
                writes=[("dqa3",)])

        dgen = d_gen()
        dq_done = [False]

        PUMP_INTERLEAVE = True

        def pump(k=1):
            if not PUMP_INTERLEAVE:
                return
            for _ in range(k):
                try:
                    next(dgen)
                except StopIteration:
                    dq_done[0] = True

        with contextlib.ExitStack() as pa:
            iota_i = pa.enter_context(_sb("iota_i", [128, FQP], I32))
            c1024 = pa.enter_context(_sb("c1024", [128, 1], I32))
            kb.op("gpsimd", lambda e: e.iota(
                iota_i[:, :], [[1, FQP]], channel_multiplier=0,
                allow_small_or_imprecise_dtypes=True),
                writes=[("iota",)], tag="A")
            kb.op("gpsimd", lambda e: e.memset(c1024[:, :], 1024),
                  writes=[("c1024",)], tag="A")

            for hf in range(2):
                row0 = hf * HALF
                with contextlib.ExitStack() as phh:
                    ytr = [phh.enter_context(
                        _sb(f"a_ytr{r}", [128, KCQ, 128], BF16))
                        for r in range(4)]
                    yti = [phh.enter_context(
                        _sb(f"a_yti{r}", [128, KCQ, 128], BF16))
                        for r in range(4)]
                    d08 = phh.enter_context(_sb("a_d08", [128, 4], I32))
                    cof8 = phh.enter_context(_sb("a_cof8", [128, 4], F32))
                    es_x = contextlib.ExitStack()
                    xrb = [es_x.enter_context(_sb(f"a_xr{r}", [128, FQP], BF16))
                           for r in range(4)]
                    psb = [es_x.enter_context(_sb(f"a_ps{r}", [128, FQP], BF16))
                           for r in range(4)]

                    with contextlib.ExitStack() as ps:
                        xht = ps.enter_context(
                            _sb("a_xht", [128, KCF, HALF], BF16))
                        with contextlib.ExitStack() as ps2:
                            xt = ps2.enter_context(_sb("a_xt", [128, D], F32))
                            sqc = ps2.enter_context(
                                _sb("a_sqc", [128, 512], F32))
                            sqp = ps2.enter_context(
                                _sb("a_sqp", [128, 8], F32))
                            xhb = ps2.enter_context(
                                _sb("a_xhb", [128, D], BF16))
                            for r in range(4):
                                rb = (row0 + r * 128) * D
                                kb.op("sync", lambda e, rb=rb: e.dma_start(
                                    out=xt[:, :],
                                    in_=dram_ap(x_in, rb, [[D, 128], [1, D]])),
                                    writes=[("xt",)], dma=True, tag="A")
                                cb = 8 * (hf * 4 + r)
                                kb.op("vector", lambda e, cb=cb: e.reduce_sum(
                                    stats[:, cb:cb + 1], xt[:, :], axis=AX.X),
                                    reads=[("xt",)], writes=[("st", cb)],
                                    tag="A")
                                for q8 in range(8):
                                    kb.op("scalar", lambda e, q8=q8:
                                          e.activation(
                                              sqc[:, :],
                                              xt[:, q8 * 512:(q8 + 1) * 512],
                                              AF.Square),
                                          reads=[("xt",)], writes=[("sqc",)],
                                          tag="A")
                                    kb.op("vector", lambda e, q8=q8:
                                          e.reduce_sum(
                                              sqp[:, q8:q8 + 1], sqc[:, :],
                                              axis=AX.X),
                                          reads=[("sqc",)],
                                          writes=[("sqp", q8)], tag="A")
                                kb.op("vector", lambda e, cb=cb: e.reduce_sum(
                                    stats[:, cb + 1:cb + 2], sqp[:, :],
                                    axis=AX.X),
                                    reads=[("sqp", q) for q in range(8)],
                                    writes=[("st", cb + 1)], tag="A")
                                kb.op("vector", lambda e, cb=cb:
                                      e.tensor_scalar(
                                          stats[:, cb + 2:cb + 3],
                                          stats[:, cb:cb + 1], 1.0 / D, None,
                                          ALU.mult),
                                      reads=[("st", cb)],
                                      writes=[("st", cb + 2)], tag="A")
                                kb.op("vector", lambda e, cb=cb:
                                      e.tensor_scalar(
                                          stats[:, cb + 3:cb + 4],
                                          stats[:, cb + 1:cb + 2], 1.0 / D,
                                          None, ALU.mult),
                                      reads=[("st", cb + 1)],
                                      writes=[("st", cb + 3)], tag="A")
                                kb.op("vector", lambda e, cb=cb:
                                      e.tensor_tensor(
                                          stats[:, cb + 4:cb + 5],
                                          stats[:, cb + 2:cb + 3],
                                          stats[:, cb + 2:cb + 3], ALU.mult),
                                      reads=[("st", cb + 2)],
                                      writes=[("st", cb + 4)], tag="A")
                                kb.op("vector", lambda e, cb=cb:
                                      e.tensor_tensor(
                                          stats[:, cb + 5:cb + 6],
                                          stats[:, cb + 3:cb + 4],
                                          stats[:, cb + 4:cb + 5],
                                          ALU.subtract),
                                      reads=[("st", cb + 3), ("st", cb + 4)],
                                      writes=[("st", cb + 5)], tag="A")
                                kb.op("vector", lambda e, cb=cb:
                                      e.tensor_scalar(
                                          stats[:, cb + 5:cb + 6],
                                          stats[:, cb + 5:cb + 6], EPS, None,
                                          ALU.add),
                                      reads=[("st", cb + 5)],
                                      writes=[("st", cb + 5)], tag="A")
                                kb.op("scalar", lambda e, cb=cb: e.activation(
                                    stats[:, cb + 6:cb + 7],
                                    stats[:, cb + 5:cb + 6], AF.Ln),
                                    reads=[("st", cb + 5)],
                                    writes=[("st", cb + 6)], tag="A")
                                kb.op("scalar", lambda e, cb=cb: e.activation(
                                    stats[:, cb + 7:cb + 8],
                                    stats[:, cb + 6:cb + 7], AF.Exp,
                                    scale=-0.5),
                                    reads=[("st", cb + 6)],
                                    writes=[("st", cb + 7)], tag="A")
                                kb.op("vector", lambda e, cb=cb:
                                      e.scalar_tensor_tensor(
                                          stats[:, cb + 4:cb + 5],
                                          stats[:, cb + 2:cb + 3], -1.0,
                                          stats[:, cb + 7:cb + 8],
                                          ALU.mult, ALU.mult),
                                      reads=[("st", cb + 2), ("st", cb + 7)],
                                      writes=[("st", cb + 4)], tag="A")
                                kb.op("scalar", lambda e, cb=cb: e.activation(
                                    xhb[:, :], xt[:, :], AF.Identity,
                                    bias=stats[:, cb + 4:cb + 5],
                                    scale=stats[:, cb + 7:cb + 8]),
                                    reads=[("xt",), ("st", cb + 4),
                                           ("st", cb + 7)],
                                    writes=[("xhb",)], tag="A")
                                for kc in range(KCF):
                                    transpose128(
                                        xhb[:, kc * 128:(kc + 1) * 128],
                                        xht[:, kc, r * 128:(r + 1) * 128],
                                        [("xhb",)], [("xht", kc, r)], tag="A")
                                pump(3)
                        kb.fence("A")

                        with contextlib.ExitStack() as pf:
                            KCB = 4
                            ccb = [pf.enter_context(
                                _sb(f"a_cc{i}", [128, KCB * 512], BF16))
                                for i in range(2)]
                            scb = [pf.enter_context(
                                _sb(f"a_sc{i}", [128, KCB * 512], BF16))
                                for i in range(2)]
                            pxr = [pf.enter_context(
                                _pt(f"a_pxr{i}", [128, 512], F32))
                                for i in range(2)]
                            pps = [pf.enter_context(
                                _pt(f"a_pps{i}", [128, 512], F32))
                                for i in range(2)]
                            for nbi, (nb0, nbw) in enumerate(NB_FWD):
                                for rg in range(2):
                                    for kcb in range(KCF // KCB):
                                        p = kcb % 2
                                        kc0 = kcb * KCB
                                        pat = [[FQP, 128],
                                               [128 * FQP, KCB], [1, nbw]]
                                        kb.op("sync", lambda e, p=p, kc0=kc0,
                                              nb0=nb0, nbw=nbw, pat=pat:
                                              e.dma_start(
                                                  out=ccb[p][:, 0:KCB * nbw],
                                                  in_=dram_ap(
                                                      Cc_d,
                                                      kc0 * 128 * FQP + nb0,
                                                      pat)),
                                              writes=[("ccb", p)], dma=True,
                                              tag="A")
                                        kb.op("gpsimd", lambda e, p=p,
                                              kc0=kc0, nb0=nb0, nbw=nbw,
                                              pat=pat: e.dma_start(
                                                  out=scb[p][:, 0:KCB * nbw],
                                                  in_=dram_ap(
                                                      Sc_d,
                                                      kc0 * 128 * FQP + nb0,
                                                      pat)),
                                              writes=[("scb", p)], dma=True,
                                              tag="A")
                                        for kci in range(KCB):
                                            kc = kc0 + kci
                                            st_ = (kc == 0)
                                            sp_ = (kc == KCF - 1)
                                            c0 = kci * nbw
                                            for ri in range(2):
                                                r = rg * 2 + ri
                                                kb.op("tensor",
                                                      lambda e, r=r, ri=ri,
                                                      kc=kc, p=p, nbw=nbw,
                                                      c0=c0, st_=st_,
                                                      sp_=sp_: e.matmul(
                                                          pxr[ri][:, 0:nbw],
                                                          xht[:, kc,
                                                              r * 128:(r + 1) * 128],
                                                          ccb[p][:, c0:c0 + nbw],
                                                          start=st_, stop=sp_),
                                                      reads=[("xht", kc, r),
                                                             ("ccb", p)],
                                                      writes=[("pxr", ri)] if (st_ or sp_) else (),
                                                      tag="A")
                                                kb.op("tensor",
                                                      lambda e, r=r, ri=ri,
                                                      kc=kc, p=p, nbw=nbw,
                                                      c0=c0, st_=st_,
                                                      sp_=sp_: e.matmul(
                                                          pps[ri][:, 0:nbw],
                                                          xht[:, kc,
                                                              r * 128:(r + 1) * 128],
                                                          scb[p][:, c0:c0 + nbw],
                                                          start=st_, stop=sp_),
                                                      reads=[("xht", kc, r),
                                                             ("scb", p)],
                                                      writes=[("pps", ri)] if (st_ or sp_) else (),
                                                      tag="A")
                                        pump(1)
                                    for ri in range(2):
                                        r = rg * 2 + ri
                                        kb.op("vector", lambda e, r=r, ri=ri,
                                              nb0=nb0, nbw=nbw: e.tensor_copy(
                                                  xrb[r][:, nb0:nb0 + nbw],
                                                  pxr[ri][:, 0:nbw]),
                                              reads=[("pxr", ri)],
                                              writes=[("xr", r)], tag="A")
                                        kb.op("vector", lambda e, r=r, ri=ri,
                                              nb0=nb0, nbw=nbw: e.tensor_copy(
                                                  psb[r][:, nb0:nb0 + nbw],
                                                  pps[ri][:, 0:nbw]),
                                              reads=[("pps", ri)],
                                              writes=[("ps", r)], tag="A")
                                    pump(3)
                            kb.fence("A")
                    kb.fence("A")

                    with contextlib.ExitStack() as ps:
                        corb = [ps.enter_context(
                            _sb(f"a_cor{r}", [128, 2048], BF16))
                            for r in range(4)]
                        with contextlib.ExitStack() as psp:
                            ptb = [psp.enter_context(
                                _sb(f"a_pt{r}", [128, KCQ, 128], BF16))
                                for r in range(4)]
                            with contextlib.ExitStack() as ps2:
                                tA = ps2.enter_context(
                                    _sb("a_tA", [128, 1088], BF16))
                                tB = ps2.enter_context(
                                    _sb("a_tB", [128, 1088], BF16))
                                pb = ps2.enter_context(
                                    _sb("a_pb", [128, FQP], BF16))
                                for r in range(4):
                                    for ch in range(2):
                                        c0 = ch * 1088
                                        kb.op("scalar", lambda e, r=r, c0=c0:
                                              e.activation(
                                                  tA[:, :],
                                                  xrb[r][:, c0:c0 + 1088],
                                                  AF.Square),
                                              reads=[("xr", r)],
                                              writes=[("tA",)], tag="A")
                                        kb.op("vector", lambda e, r=r, c0=c0:
                                              e.tensor_tensor(
                                                  tB[:, :],
                                                  psb[r][:, c0:c0 + 1088],
                                                  psb[r][:, c0:c0 + 1088],
                                                  ALU.mult),
                                              reads=[("ps", r)],
                                              writes=[("tB",)], tag="A")
                                        kb.op("vector", lambda e, c0=c0:
                                              e.tensor_tensor(
                                                  pb[:, c0:c0 + 1088],
                                                  tA[:, :], tB[:, :],
                                                  ALU.add),
                                              reads=[("tA",), ("tB",)],
                                              writes=[("pb", c0)], tag="A")
                                    for kq in range(KCQ):
                                        transpose128(
                                            pb[:, kq * 128:(kq + 1) * 128],
                                            ptb[r][:, kq, :],
                                            [("pb", 0), ("pb", 1088)],
                                            [("ptb", r, kq)], tag="A")
                                    pump(3)
                                kb.fence("A")

                            with contextlib.ExitStack() as ps2:
                                cib = ps2.enter_context(
                                    _sb("a_ci", [128, KCQ * 512], BF16))
                                pc = [ps2.enter_context(
                                    _pt(f"a_pc{i}", [128, 512], F32))
                                    for i in range(4)]
                                for n in range(4):
                                    kb.op("gpsimd", lambda e, n=n:
                                          e.dma_start(
                                              out=cib[:, :],
                                              in_=dram_ap(
                                                  CiC_d, n * 512,
                                                  [[D, 128],
                                                   [128 * D, KCQ],
                                                   [1, 512]])),
                                          writes=[("cib",)],
                                          dma=True, tag="A")
                                    for rg in range(2):
                                        par = rg % 2
                                        for kq in range(KCQ):
                                            for ri in range(2):
                                                r = rg * 2 + ri
                                                bank = par * 2 + ri
                                                st_ = (kq == 0)
                                                sp_ = (kq == KCQ - 1)
                                                kb.op("tensor",
                                                      lambda e, r=r, kq=kq,
                                                      bank=bank,
                                                      st_=st_, sp_=sp_:
                                                      e.matmul(
                                                          pc[bank][:, :],
                                                          ptb[r][:, kq, :],
                                                          cib[:, kq * 512:
                                                              (kq + 1) * 512],
                                                          start=st_,
                                                          stop=sp_),
                                                      reads=[("ptb", r, kq),
                                                             ("cib",)],
                                                      writes=[("pc", bank)] if (st_ or sp_) else (),
                                                      tag="A")
                                        for ri in range(2):
                                            r = rg * 2 + ri
                                            bank = par * 2 + ri
                                            kb.op("vector", lambda e, r=r,
                                                  n=n, bank=bank:
                                                  e.tensor_copy(
                                                      corb[r][:, n * 512:
                                                              (n + 1) * 512],
                                                      pc[bank][:, :]),
                                                  reads=[("pc", bank)],
                                                  writes=[("cor", r, n)],
                                                  tag="A")
                                        pump(3)
                                kb.fence("A")

                        with contextlib.ExitStack() as ps2:
                            t8 = ps2.enter_context(_sb("a_t8", [128, 8], BF16))
                            i8 = ps2.enter_context(_sb("a_i8", [128, 8], U32))
                            dvb = ps2.enter_context(_sb("a_dv", [128, 8], F32))
                            exb = ps2.enter_context(_sb("a_ex", [128, 8], F32))
                            den = ps2.enter_context(_sb("a_den", [128, 1], F32))
                            for r in range(4):
                                corres = [("cor", r, n) for n in range(4)]
                                kb.op("vector", lambda e, r=r: e.max(
                                    t8[:, :], corb[r][:, :]),
                                    reads=corres, writes=[("t8",)], tag="A")
                                kb.op("vector", lambda e, r=r: e.max_index(
                                    i8[:, :], t8[:, :], corb[r][:, :]),
                                    reads=corres + [("t8",)],
                                    writes=[("i8",)], tag="A")
                                kb.op("vector", lambda e: e.tensor_tensor(
                                    dvb[:, :], t8[:, :],
                                    t8[:, 0:1].broadcast_to((128, 8)),
                                    ALU.subtract),
                                    reads=[("t8",)], writes=[("dv",)],
                                    tag="A")
                                kb.op("scalar", lambda e: e.activation(
                                    exb[:, :], dvb[:, :], AF.Exp,
                                    accum_out=den[:, :]),
                                    reads=[("dv",)],
                                    writes=[("den",), ("exv",)], tag="A")
                                kb.op("vector", lambda e, r=r: e.reciprocal(
                                    cof8[:, r:r + 1], den[:, :]),
                                    reads=[("den",)], writes=[("cof", r)],
                                    tag="A")
                                kb.op("vector", lambda e, r=r: e.tensor_copy(
                                    d08[:, r:r + 1], i8[:, 0:1]),
                                    reads=[("i8",)], writes=[("d0", r)],
                                    tag="A")
                                pump(1)
                        kb.fence("A")

                    with contextlib.ExitStack() as ps2:
                        mi1 = ps2.enter_context(_sb("a_mi1", [128, 1088], I32))
                        mi2 = ps2.enter_context(_sb("a_mi2", [128, 1088], I32))
                        tf1 = ps2.enter_context(_sb("a_tf1", [128, 1088], F32))
                        tf2 = ps2.enter_context(_sb("a_tf2", [128, 1088], F32))
                        csb = ps2.enter_context(_sb("a_cs", [128, FQP], BF16))
                        snb = ps2.enter_context(_sb("a_sn", [128, FQP], BF16))
                        y1 = ps2.enter_context(_sb("a_y1", [128, 1088], BF16))
                        y2 = ps2.enter_context(_sb("a_y2", [128, 1088], BF16))
                        y3 = ps2.enter_context(_sb("a_y3", [128, FQP], BF16))

                        for r in range(4):
                            for ch in range(2):
                                c0 = ch * 1088
                                kb.op("vector", lambda e, c0=c0, r=r:
                                      e.tensor_tensor(
                                          mi1[:, :],
                                          iota_i[:, c0:c0 + 1088],
                                          d08[:, r:r + 1].broadcast_to(
                                              (128, 1088)), ALU.mult),
                                      reads=[("iota",), ("d0", r)],
                                      writes=[("mi1",)], tag="A")
                                kb.op("vector", lambda e: e.tensor_tensor(
                                    mi2[:, :], mi1[:, :],
                                    c1024[:, :].broadcast_to((128, 1088)),
                                    ALU.add),
                                    reads=[("mi1",), ("c1024",)],
                                    writes=[("mi2",)], tag="A")
                                for mi, tf, outb, mk, tk, ok_, sk in (
                                        (mi1, tf1, snb, ("mi1",), ("tf1",),
                                         ("tf2",), ("snb",)),
                                        (mi2, tf2, csb, ("mi2",), ("tf2",),
                                         ("tf1",), ("csb",))):
                                    oth = tf2 if tf is tf1 else tf1
                                    kb.op("vector", lambda e, mi=mi, mk=mk:
                                          e.tensor_single_scalar(
                                              mi[:, :], mi[:, :], 4095,
                                              ALU.bitwise_and),
                                          reads=[mk], writes=[mk], tag="A")
                                    kb.op("vector", lambda e, mi=mi, tf=tf:
                                          e.tensor_copy(tf[:, :], mi[:, :]),
                                          reads=[mk], writes=[tk], tag="A")
                                    kb.op("vector", lambda e, tf=tf:
                                          e.tensor_scalar(
                                              tf[:, :], tf[:, :],
                                              PHASE_SCALE, None, ALU.mult),
                                          reads=[tk], writes=[tk], tag="A")
                                    kb.op("vector", lambda e, tf=tf, oth=oth:
                                          e.tensor_scalar(
                                              oth[:, :], tf[:, :],
                                              float(np.pi), TWO_PI,
                                              ALU.is_gt, ALU.mult),
                                          reads=[tk], writes=[ok_], tag="A")
                                    kb.op("vector", lambda e, tf=tf, oth=oth:
                                          e.tensor_tensor(
                                              tf[:, :], tf[:, :], oth[:, :],
                                              ALU.subtract),
                                          reads=[tk, ok_], writes=[tk],
                                          tag="A")
                                    kb.op("scalar", lambda e, tf=tf,
                                          outb=outb, c0=c0: e.activation(
                                              outb[:, c0:c0 + 1088],
                                              tf[:, :], AF.Sin),
                                          reads=[tk], writes=[sk], tag="A")
                                pump(2)
                            for which in range(2):
                                a_op = ALU.add if which == 0 else ALU.subtract
                                m1 = csb if which == 0 else snb
                                m2 = snb if which == 0 else csb
                                yt = ytr if which == 0 else yti
                                ykey = "ytr" if which == 0 else "yti"
                                for ch in range(2):
                                    c0 = ch * 1088
                                    kb.op("vector", lambda e, r=r, c0=c0,
                                          m1=m1: e.tensor_tensor(
                                              y1[:, :],
                                              xrb[r][:, c0:c0 + 1088],
                                              m1[:, c0:c0 + 1088], ALU.mult),
                                          reads=[("xr", r), ("csb",),
                                                 ("snb",)],
                                          writes=[("y1",)], tag="A")
                                    kb.op("vector", lambda e, r=r, c0=c0,
                                          m2=m2: e.tensor_tensor(
                                              y2[:, :],
                                              psb[r][:, c0:c0 + 1088],
                                              m2[:, c0:c0 + 1088], ALU.mult),
                                          reads=[("ps", r), ("csb",),
                                                 ("snb",)],
                                          writes=[("y2",)], tag="A")
                                    kb.op("vector", lambda e, c0=c0,
                                          a_op=a_op: e.tensor_tensor(
                                              y3[:, c0:c0 + 1088], y1[:, :],
                                              y2[:, :], a_op),
                                          reads=[("y1",), ("y2",)],
                                          writes=[("y3", c0)], tag="A")
                                    kb.op("vector", lambda e, c0=c0, r=r:
                                          e.tensor_scalar(
                                              y3[:, c0:c0 + 1088],
                                              y3[:, c0:c0 + 1088],
                                              cof8[:, r:r + 1], None,
                                              ALU.mult),
                                          reads=[("y3", c0), ("cof", r)],
                                          writes=[("y3", c0)], tag="A")
                                for kq in range(KCQ):
                                    transpose128(
                                        y3[:, kq * 128:(kq + 1) * 128],
                                        yt[r][:, kq, :],
                                        [("y3", 0), ("y3", 1088)],
                                        [(ykey, r, kq)], tag="A")
                                pump(2)
                        kb.fence("A")

                    es_x.close()
                    with contextlib.ExitStack() as ps:
                        cb2 = [ps.enter_context(
                            _sb(f"a_cj{i}", [128, KCQ * 512], BF16))
                            for i in range(2)]
                        sb2 = [ps.enter_context(
                            _sb(f"a_sj{i}", [128, KCQ * 512], BF16))
                            for i in range(2)]
                        xsl = [ps.enter_context(
                            _sb(f"a_xs{i}", [128, 512], F32))
                            for i in range(2)]
                        o1b = [ps.enter_context(
                            _sb(f"a_o1{i}", [128, 512], F32))
                            for i in range(2)]
                        pr = [ps.enter_context(
                            _pt(f"a_pr{i}", [128, 512], F32))
                            for i in range(4)]
                        for n in range(8):
                            p = n % 2
                            kb.op("sync", lambda e, p=p, n=n:
                                  e.dma_start(
                                      out=cb2[p][:, :],
                                      in_=dram_ap(
                                          CiC_d, n * 512,
                                          [[D, 128], [128 * D, KCQ],
                                           [1, 512]])),
                                  writes=[("cj", p)], dma=True,
                                  tag="A")
                            kb.op("gpsimd", lambda e, p=p, n=n:
                                  e.dma_start(
                                      out=sb2[p][:, :],
                                      in_=dram_ap(
                                          CiS_d, n * 512,
                                          [[D, 128], [128 * D, KCQ],
                                           [1, 512]])),
                                  writes=[("sj", p)], dma=True,
                                  tag="A")
                            for rg in range(2):
                                par = (n * 2 + rg) % 2
                                for kq in range(KCQ):
                                    for ri in range(2):
                                        r = rg * 2 + ri
                                        bank = par * 2 + ri
                                        kb.op("tensor", lambda e, r=r, kq=kq,
                                              p=p, bank=bank, st_=(kq == 0):
                                              e.matmul(pr[bank][:, :],
                                                       ytr[r][:, kq, :],
                                                       cb2[p][:, kq * 512:
                                                              (kq + 1) * 512],
                                                       start=st_, stop=False),
                                              reads=[("ytr", r, kq),
                                                     ("cj", p)],
                                              writes=[("pr", bank)] if kq == 0 else (),
                                              tag="A")
                                        kb.op("tensor", lambda e, r=r, kq=kq,
                                              p=p, bank=bank,
                                              sp_=(kq == KCQ - 1):
                                              e.matmul(pr[bank][:, :],
                                                       yti[r][:, kq, :],
                                                       sb2[p][:, kq * 512:
                                                              (kq + 1) * 512],
                                                       start=False, stop=sp_),
                                              reads=[("yti", r, kq),
                                                     ("sj", p)],
                                              writes=[("pr", bank)] if kq == KCQ - 1 else (),
                                              tag="A")
                                for ri in range(2):
                                    r = rg * 2 + ri
                                    bank = par * 2 + ri
                                    bp = (n * 4 + r) % 2
                                    off = (row0 + r * 128) * D + n * 512
                                    kb.op("sync", lambda e, bp=bp, off=off:
                                          e.dma_start(
                                              out=xsl[bp][:, :],
                                              in_=dram_ap(x_in, off,
                                                          [[D, 128],
                                                           [1, 512]])),
                                          writes=[("xsl", bp)], dma=True,
                                          tag="A")
                                    kb.op("vector", lambda e, bank=bank,
                                          bp=bp: e.tensor_tensor(
                                              o1b[bp][:, :], pr[bank][:, :],
                                              xsl[bp][:, :], ALU.add),
                                          reads=[("pr", bank), ("xsl", bp)],
                                          writes=[("o1b", bp)], tag="A")
                                    kb.op("sync", lambda e, bp=bp, off=off:
                                          e.dma_start(
                                              out=dram_ap(out1_d, off,
                                                          [[D, 128],
                                                           [1, 512]]),
                                              in_=o1b[bp][:, :]),
                                          reads=[("o1b", bp)],
                                          writes=[("out1", off)], dma=True,
                                          tag="A")
                                pump(3)
                        kb.fence("A")
                    kb.fence("A")

            while not dq_done[0]:
                try:
                    next(dgen)
                except StopIteration:
                    dq_done[0] = True
        es_d.close()
        kb.barrier()

        out1_res = [("out1", (r * 128) * D + n * 512)
                    for r in range(8) for n in range(8)]

        with contextlib.ExitStack() as pf:
            x2t = pf.enter_context(_sb("f_x2t", [128, KCF, RPC], BF16))

            with contextlib.ExitStack() as ps:
                ot = ps.enter_context(_sb("f_ot", [128, D], F32))
                fsqc = ps.enter_context(_sb("f_sqc", [128, 512], F32))
                fsqp = ps.enter_context(_sb("f_sqp", [128, 8], F32))
                fxhb = ps.enter_context(_sb("f_xhb", [128, D], BF16))
                for rt in range(8):
                    rb = rt * 128 * D
                    kb.op("sync", lambda e, rb=rb: e.dma_start(
                        out=ot[:, :],
                        in_=dram_ap(out1_d, rb, [[D, 128], [1, D]])),
                        reads=out1_res, writes=[("f_ot",)], dma=True)
                    cb = 64 + 8 * (rt % 4)
                    kb.op("vector", lambda e, cb=cb: e.reduce_sum(
                        stats[:, cb:cb + 1], ot[:, :], axis=AX.X),
                        reads=[("f_ot",)], writes=[("st", cb)])
                    for q8 in range(8):
                        kb.op("scalar", lambda e, q8=q8: e.activation(
                            fsqc[:, :], ot[:, q8 * 512:(q8 + 1) * 512],
                            AF.Square),
                            reads=[("f_ot",)], writes=[("sqc",)])
                        kb.op("vector", lambda e, q8=q8: e.reduce_sum(
                            fsqp[:, q8:q8 + 1], fsqc[:, :], axis=AX.X),
                            reads=[("sqc",)], writes=[("sqp", q8)])
                    kb.op("vector", lambda e, cb=cb: e.reduce_sum(
                        stats[:, cb + 1:cb + 2], fsqp[:, :], axis=AX.X),
                        reads=[("sqp", q) for q in range(8)],
                        writes=[("st", cb + 1)])
                    kb.op("vector", lambda e, cb=cb: e.tensor_scalar(
                        stats[:, cb + 2:cb + 3], stats[:, cb:cb + 1],
                        1.0 / D, None, ALU.mult),
                        reads=[("st", cb)], writes=[("st", cb + 2)])
                    kb.op("vector", lambda e, cb=cb: e.tensor_scalar(
                        stats[:, cb + 3:cb + 4], stats[:, cb + 1:cb + 2],
                        1.0 / D, None, ALU.mult),
                        reads=[("st", cb + 1)], writes=[("st", cb + 3)])
                    kb.op("vector", lambda e, cb=cb: e.tensor_tensor(
                        stats[:, cb + 4:cb + 5], stats[:, cb + 2:cb + 3],
                        stats[:, cb + 2:cb + 3], ALU.mult),
                        reads=[("st", cb + 2)], writes=[("st", cb + 4)])
                    kb.op("vector", lambda e, cb=cb: e.tensor_tensor(
                        stats[:, cb + 5:cb + 6], stats[:, cb + 3:cb + 4],
                        stats[:, cb + 4:cb + 5], ALU.subtract),
                        reads=[("st", cb + 3), ("st", cb + 4)],
                        writes=[("st", cb + 5)])
                    kb.op("vector", lambda e, cb=cb: e.tensor_scalar(
                        stats[:, cb + 5:cb + 6], stats[:, cb + 5:cb + 6],
                        EPS, None, ALU.add),
                        reads=[("st", cb + 5)], writes=[("st", cb + 5)])
                    kb.op("scalar", lambda e, cb=cb: e.activation(
                        stats[:, cb + 6:cb + 7], stats[:, cb + 5:cb + 6],
                        AF.Ln),
                        reads=[("st", cb + 5)], writes=[("st", cb + 6)])
                    kb.op("scalar", lambda e, cb=cb: e.activation(
                        stats[:, cb + 7:cb + 8], stats[:, cb + 6:cb + 7],
                        AF.Exp, scale=-0.5),
                        reads=[("st", cb + 6)], writes=[("st", cb + 7)])
                    kb.op("vector", lambda e, cb=cb: e.scalar_tensor_tensor(
                        stats[:, cb + 4:cb + 5], stats[:, cb + 2:cb + 3],
                        -1.0, stats[:, cb + 7:cb + 8], ALU.mult, ALU.mult),
                        reads=[("st", cb + 2), ("st", cb + 7)],
                        writes=[("st", cb + 4)])
                    kb.op("scalar", lambda e, cb=cb: e.activation(
                        fxhb[:, :], ot[:, :], AF.Identity,
                        bias=stats[:, cb + 4:cb + 5],
                        scale=stats[:, cb + 7:cb + 8]),
                        reads=[("f_ot",), ("st", cb + 4), ("st", cb + 7)],
                        writes=[("f_xhb",)])
                    for kc in range(KCF):
                        transpose128(fxhb[:, kc * 128:(kc + 1) * 128],
                                     x2t[:, kc, rt * 128:(rt + 1) * 128],
                                     [("f_xhb",)], [("x2t", kc, rt)])
                kb.barrier()
            es_tp.close()

            x2t_kc = {kc: [("x2t", kc, rt) for rt in range(8)]
                      for kc in range(KCF)}

            ht = pf.enter_context(_sb("f_ht", [128, 43, RPC], BF16))

            def slab_ap(mat, hc):
                r, j = divmod(hc, HPC)
                dqa_l = dqa1_d if mat == 1 else dqa2_d
                base = r * DQ1_ELEMS + j * SLAB_W
                return dram_ap(dqa_l, base, [[4096, 128], [1, 4096]])

            def piece_ap(hc, nq):
                base = nq * DQ3_ELEMS + hc * PIECE_W
                return dram_ap(dqa3_d, base, [[512, 128], [1, 512]])

            for hph in range(2):
                hcs = list(range(hph * 43, hph * 43 + 43))

                with contextlib.ExitStack() as ps:
                    s1b = [ps.enter_context(
                        _sb(f"f_s1{i}", [128, KCF, 128], BF16))
                        for i in range(2)]
                    s2b = [ps.enter_context(
                        _sb(f"f_s2{i}", [128, KCF, 128], BF16))
                        for i in range(2)]
                    sg = ps.enter_context(_sb("f_sg", [128, RPC], F32))
                    pg = [ps.enter_context(_pt(f"f_pg{i}", [128, RPC], F32))
                          for i in range(2)]
                    pu = [ps.enter_context(_pt(f"f_pu{i}", [128, RPC], F32))
                          for i in range(2)]
                    for i, hc in enumerate(hcs):
                        par = i % 2
                        kb.op("sync", lambda e, par=par, hc=hc: e.dma_start(
                            out=s1b[par][:, :, :].rearrange(
                                "p a b -> p (a b)"),
                            in_=slab_ap(1, hc)),
                            reads=[("dqa1",)], writes=[("s1b", par)],
                            dma=True)
                        kb.op("sync", lambda e, par=par, hc=hc: e.dma_start(
                            out=s2b[par][:, :, :].rearrange(
                                "p a b -> p (a b)"),
                            in_=slab_ap(2, hc)),
                            reads=[("dqa2",)], writes=[("s2b", par)],
                            dma=True)
                        for kc in range(KCF):
                            for ns in range(2):
                                st_ = (kc == 0)
                                sp_ = (kc == KCF - 1)
                                kb.op("tensor", lambda e, par=par, kc=kc,
                                      ns=ns, st_=st_, sp_=sp_: e.matmul(
                                          pg[par][:, ns * 512:(ns + 1) * 512],
                                          s1b[par][:, kc, :],
                                          x2t[:, kc, ns * 512:(ns + 1) * 512],
                                          start=st_, stop=sp_),
                                      reads=[("s1b", par)] + x2t_kc[kc],
                                      writes=[("pg", par)] if (st_ or sp_) else ())
                                kb.op("tensor", lambda e, par=par, kc=kc,
                                      ns=ns, st_=st_, sp_=sp_: e.matmul(
                                          pu[par][:, ns * 512:(ns + 1) * 512],
                                          s2b[par][:, kc, :],
                                          x2t[:, kc, ns * 512:(ns + 1) * 512],
                                          start=st_, stop=sp_),
                                      reads=[("s2b", par)] + x2t_kc[kc],
                                      writes=[("pu", par)] if (st_ or sp_) else ())
                        kb.op("scalar", lambda e, par=par: e.activation(
                            sg[:, :], pg[par][:, :], AF.Silu),
                            reads=[("pg", par)], writes=[("f_sg",)])
                        kb.op("vector", lambda e, par=par, i=i:
                              e.tensor_tensor(ht[:, i, :], sg[:, :],
                                              pu[par][:, :], ALU.mult),
                              reads=[("f_sg",), ("pg", par), ("pu", par)],
                              writes=[("ht", i)])
                    kb.barrier()

                with contextlib.ExitStack() as ps:
                    pcb = [ps.enter_context(
                        _sb(f"f_pc{i}", [128, 4, 512], BF16))
                        for i in range(2)]
                    o1c = [ps.enter_context(_sb(f"f_o1c{i}", [128, 512], F32))
                           for i in range(2)]
                    ob = [ps.enter_context(_sb(f"f_ob{i}", [128, 512], F32))
                          for i in range(2)]
                    pd = [ps.enter_context(_pt(f"f_pd{i}", [128, 512], F32))
                          for i in range(8)]
                    nkb = (43 + 3) // 4
                    for n in range(8):
                        for mg in range(2):
                            for kbatch in range(nkb):
                                ks = list(range(kbatch * 4,
                                                min(43, kbatch * 4 + 4)))
                                bp = kbatch % 2
                                for ki, k in enumerate(ks):
                                    kb.op("gpsimd",
                                          lambda e, bp=bp, ki=ki, k=k, n=n,
                                          hph=hph: e.dma_start(
                                              out=pcb[bp][:, ki, :],
                                              in_=piece_ap(hph * 43 + k, n)),
                                          reads=[("dqa3",)],
                                          writes=[("pcb", bp, ki)], dma=True)
                                for ki, k in enumerate(ks):
                                    for m in range(4):
                                        bank = mg * 4 + m
                                        st_ = (kbatch == 0 and ki == 0)
                                        sp_ = (kbatch == nkb - 1 and
                                               ki == len(ks) - 1)
                                        mslice = (mg * 4 + m) * 128
                                        kb.op("tensor",
                                              lambda e, bp=bp, ki=ki, k=k,
                                              m=m, bank=bank, st_=st_,
                                              sp_=sp_, mslice=mslice:
                                              e.matmul(
                                                  pd[bank][:, :],
                                                  ht[:, k,
                                                     mslice:mslice + 128],
                                                  pcb[bp][:, ki, :],
                                                  start=st_, stop=sp_),
                                              reads=[("ht", k),
                                                     ("pcb", bp, ki)],
                                              writes=[("pd", bank)] if (st_ or sp_) else ())
                            srcd = out1_d if hph == 0 else part_d
                            for m in range(4):
                                bank = mg * 4 + m
                                bp = m % 2
                                off = (mg * 512 + m * 128) * D + n * 512
                                kb.op("sync", lambda e, bp=bp, off=off,
                                      srcd=srcd: e.dma_start(
                                          out=o1c[bp][:, :],
                                          in_=dram_ap(srcd, off,
                                                      [[D, 128], [1, 512]])),
                                      reads=out1_res if hph == 0 else [("part",)],
                                      writes=[("o1c", bp)], dma=True)
                                kb.op("vector", lambda e, bp=bp, bank=bank:
                                      e.tensor_tensor(
                                          ob[bp][:, :], pd[bank][:, :],
                                          o1c[bp][:, :], ALU.add),
                                      reads=[("pd", bank), ("o1c", bp)],
                                      writes=[("ob", bp)])
                                if hph == 0:
                                    kb.op("sync", lambda e, bp=bp, off=off:
                                          e.dma_start(
                                              out=dram_ap(part_d, off,
                                                          [[D, 128],
                                                           [1, 512]]),
                                              in_=ob[bp][:, :]),
                                          reads=[("ob", bp)],
                                          writes=[("part",)], dma=True)
                                else:
                                    kb.op("sync", lambda e, bp=bp, off=off:
                                          e.dma_start(
                                              out=dram_ap(out_d, off,
                                                          [[D, 128],
                                                           [1, 512]]),
                                              in_=ob[bp][:, :]),
                                          reads=[("ob", bp)],
                                          writes=[("outd", off)], dma=True)
                    kb.barrier()

        kb.barrier()

        with nc.Block() as block:
            kb.build(block)

    return nc


_NC_CACHE = {}


def kernel(L0_attn_x, L0_ffn_w1, L0_ffn_w2, L0_ffn_w3):
    x = np.ascontiguousarray(np.asarray(L0_attn_x, np.float32).reshape(8192, D))
    w1 = np.asarray(L0_ffn_w1, np.float32)
    w2 = np.asarray(L0_ffn_w2, np.float32)
    w3 = np.asarray(L0_ffn_w3, np.float32)

    w1p = np.zeros((HPAD, D), np.float32)
    w1p[:H] = w1
    w2p = np.zeros((HPAD, D), np.float32)
    w2p[:H] = w2
    w3p = np.zeros((D, HPAD), np.float32)
    w3p[:, :H] = w3

    Cc, Sc, CiCc, CiSc, ident, bcol = _host_consts()

    if "nc" not in _NC_CACHE:
        _NC_CACHE["nc"] = build_program()
    nc = _NC_CACHE["nc"]

    in_maps = []
    for c in range(NCORES):
        in_maps.append({
            "x_in": np.ascontiguousarray(x[c * RPC:(c + 1) * RPC]),
            "w1s": np.ascontiguousarray(w1p[c * 1408:(c + 1) * 1408]),
            "w2s": np.ascontiguousarray(w2p[c * 1408:(c + 1) * 1408]),
            "w3s": np.ascontiguousarray(w3p[c * 512:(c + 1) * 512]),
            "Cc": Cc, "Sc": Sc, "CiC": CiCc, "CiS": CiSc,
            "idn": ident, "bcol": bcol,
        })
    res = run_bass_kernel_spmd(nc, in_maps, core_ids=list(range(NCORES)))
    out = np.concatenate([np.asarray(r["out"]) for r in res.results], axis=0)
    return out.reshape(4, 2048, D)


# version 17
# speedup vs baseline: 1.3112x; 1.0152x over previous
import contextlib

import numpy as np
import ml_dtypes

import concourse.bass as bass
import concourse.mybir as mybir
from concourse.bass_utils import run_bass_kernel_spmd

F32 = mybir.dt.float32
BF16 = mybir.dt.bfloat16
I32 = mybir.dt.int32
U32 = mybir.dt.uint32
AF = mybir.ActivationFunctionType
ALU = mybir.AluOpType
AX = mybir.AxisListType
BF = ml_dtypes.bfloat16

NCORES = 8
D = 4096
RPC = 1024
HALF = 512
FQ = 2049
FQP = 2176
KCF = 32
KCQ = 17
NB_FWD = [(0, 512), (512, 512), (1024, 512), (1536, 512), (2048, 128)]
H = 11008
HPAD = 11264
NHC = 88
HC_REAL = 86
HPC = 11
EPS = 1e-5

NF4_CODE = np.array([-1.0, -0.6961928009986877, -0.5250730514526367,
                     -0.39491748809814453, -0.28444138169288635,
                     -0.18477343022823334, -0.09105003625154495, 0.0,
                     0.07958029955625534, 0.16093020141124725,
                     0.24611230194568634, 0.33791524171829224,
                     0.44070982933044434, 0.5626170039176941,
                     0.7229568362236023, 1.0], dtype=np.float32)
NF4_BOUND = ((NF4_CODE[:-1] + NF4_CODE[1:]) * 0.5).astype(np.float32)
NF4_E = ((NF4_CODE[1:] - NF4_CODE[:-1]) * 0.5).astype(np.float32)

SLAB_W = 128 * 4096
PIECE_W = 128 * 512
DQ1_ELEMS = HPC * SLAB_W
DQ3_ELEMS = NHC * PIECE_W
PHASE_SCALE = float(2 * np.pi / 4096)
TWO_PI = float(2 * np.pi)


def _host_consts():
    t = np.arange(D, dtype=np.float64)
    k = np.arange(FQP, dtype=np.float64)
    ang = 2 * np.pi * np.outer(t, k) / D
    C = np.cos(ang)
    S = np.sin(ang)
    C[:, FQ:] = 0.0
    S[:, FQ:] = 0.0
    w = np.full(FQP, 2.0)
    w[0] = 1.0
    w[2048] = 1.0
    w[FQ:] = 0.0
    angi = 2 * np.pi * np.outer(k, t) / D
    CiC = (w[:, None] * np.cos(angi)) / D
    CiS = (-w[:, None] * np.sin(angi)) / D
    Cc = np.ascontiguousarray(C.reshape(KCF, 128, FQP)).astype(BF)
    Sc = np.ascontiguousarray(S.reshape(KCF, 128, FQP)).astype(BF)
    CiCc = np.ascontiguousarray(CiC.reshape(KCQ, 128, D)).astype(BF)
    CiSc = np.ascontiguousarray(CiS.reshape(KCQ, 128, D)).astype(BF)
    ident = np.eye(128, dtype=BF)
    bcol = np.zeros((128, 16), np.float32)
    bcol[:, :15] = -NF4_BOUND[None, :]
    return Cc, Sc, CiCc, CiSc, ident, bcol


class KB:

    ENGS = ("sync", "scalar", "vector", "tensor", "gpsimd")
    DMA_POOL = 10

    def __init__(self, nc, es):
        self.nc = nc
        self.sem = {}
        self.cnt = {}
        names = ["sy", "sc", "ve", "te", "gp", "cc1", "cc2", "cc3"]
        for q in ("dA", "dB", "dG"):
            names += [f"{q}{i}" for i in range(self.DMA_POOL)]
        for name in names:
            self.sem[name] = es.enter_context(nc.semaphore("sem_" + name))
            self.cnt[name] = 0
        self.engsem = {"sync": "sy", "scalar": "sc", "vector": "ve",
                       "tensor": "te", "gpsimd": "gp"}
        self.dmaq = {"sync": "dA", "scalar": "dB", "gpsimd": "dG"}
        self.dseq = {"dA": 0, "dB": 0, "dG": 0}
        self.fake = {e: 0 for e in self.ENGS}
        self.thunks = {e: [] for e in self.ENGS}
        self.clk = {}
        self.semtop = {s: 0 for s in self.sem}
        self.lastw = {}
        self.readers = {}
        self.tagrec = {}
        self.fence_dep = {}

    @staticmethod
    def _join(a, b):
        if not b:
            return a
        out = dict(a)
        for k, v in b.items():
            if out.get(k, 0) < v:
                out[k] = v
        return out

    def _clock_of(self, s, v):
        c = self.clk.get((s, v))
        if c is not None:
            return c
        base = {s: v}
        for e, sn in self.engsem.items():
            if sn == s:
                base["F:" + e] = self.fake[e]
        return base

    def _emit(self, eng, fn, deps, inc_sem, inc_amt):
        cov = {}
        waits = []
        best = {}
        for s, v in deps:
            if best.get(s, 0) < v:
                best[s] = v
        order = sorted(best.items(), key=lambda kv: (kv[0][0] == "d", kv[0]))
        for s, v in order:
            if cov.get(s, 0) >= v:
                continue
            waits.append((s, v))
            cov = self._join(cov, self._clock_of(s, v))
        self.fake[eng] += 1
        cov["F:" + eng] = self.fake[eng]
        if inc_sem is not None:
            self.cnt[inc_sem] += inc_amt
            val = self.cnt[inc_sem]
            cov[inc_sem] = val
            prev = self.clk.get((inc_sem, self.semtop[inc_sem]), {})
            self.clk[(inc_sem, val)] = self._join(cov, prev)
            self.semtop[inc_sem] = val
            self.thunks[eng].append((waits, fn, inc_sem, inc_amt))
        else:
            self.thunks[eng].append((waits, fn, None, 0))

    def _record(self, rec, reads, writes, tag):
        for w in writes:
            self.lastw[w] = rec
            self.readers[w] = []
        for r in reads:
            self.readers.setdefault(r, []).append(rec)
        if tag is not None:
            tr = self.tagrec.setdefault(tag, {})
            s, v = rec
            if tr.get(s, 0) < v:
                tr[s] = v

    def op(self, eng, fn, reads=(), writes=(), dma=False, tag=None):
        deps = []
        fd = self.fence_dep.get((tag, eng))
        if fd is not None:
            deps.append(fd)
        for r in reads:
            lw = self.lastw.get(r)
            if lw:
                deps.append(lw)
        for w in writes:
            lw = self.lastw.get(w)
            if lw:
                deps.append(lw)
            deps.extend(self.readers.get(w, ()))
        if dma:
            q = self.dmaq[eng]
            n = self.dseq[q]
            self.dseq[q] = n + 1
            s = f"{q}{n % self.DMA_POOL}"
            if n >= self.DMA_POOL:
                deps.append((s, 16 * (n // self.DMA_POOL)))
            self._emit(eng, fn, deps, s, 16)
        else:
            s = self.engsem[eng]
            self._emit(eng, fn, deps, s, 1)
        self._record((s, self.cnt[s]), reads, writes, tag)

    def cc(self, fn, sem, reads=(), writes=()):
        deps = []
        for r in reads:
            lw = self.lastw.get(r)
            if lw:
                deps.append(lw)
        for w in writes:
            lw = self.lastw.get(w)
            if lw:
                deps.append(lw)
            deps.extend(self.readers.get(w, ()))
        self._emit("gpsimd", fn, deps, sem, 1)
        self._record((sem, self.cnt[sem]), reads, writes, None)

    def fence(self, tag):
        tr = dict(self.tagrec.get(tag, {}))
        for e in self.ENGS:
            deps = list(tr.items())
            fd = self.fence_dep.get((tag, e))
            if fd is not None:
                deps.append(fd)
            if e in ("vector", "gpsimd"):
                fn = (lambda en: en.engine_nop())
            else:
                fn = (lambda en: en.nop())
            s = self.engsem[e]
            self._emit(e, fn, deps, s, 1)
            self.fence_dep[(tag, e)] = (s, self.cnt[s])
            tr2 = self.tagrec.setdefault(tag, {})
            if tr2.get(s, 0) < self.cnt[s]:
                tr2[s] = self.cnt[s]

    def barrier(self):
        snap = [(s, v) for s, v in self.cnt.items() if v > 0]
        for e in self.ENGS:
            if e in ("vector", "gpsimd"):
                fn = (lambda en: en.engine_nop())
            else:
                fn = (lambda en: en.nop())
            s = self.engsem[e]
            self._emit(e, fn, snap, s, 1)
            rec = (s, self.cnt[s])
            for tag in set(t for (t, _) in self.fence_dep) | {"A", "D", None}:
                self.fence_dep[(tag, e)] = rec
            tr = self.tagrec.setdefault("A", {})

    def build(self, block):
        sems = self.sem
        for eng in self.ENGS:
            def make_body(thunks=self.thunks[eng]):
                def body(e):
                    for waits, fn, s, amt in thunks:
                        for ws, wv in waits:
                            e.wait_ge(sems[ws], wv)
                        if fn is not None:
                            ins = fn(e)
                            if s is not None:
                                ins.then_inc(sems[s], amt)
                return body
            getattr(block, eng)(make_body())


def build_program():
    nc = bass.Bass("TRN2", target_bir_lowering=False, debug=False)

    x_in = nc.dram_tensor("x_in", [RPC, D], F32, kind="ExternalInput")
    w1s = nc.dram_tensor("w1s", [HPC * 128, D], F32, kind="ExternalInput")
    w2s = nc.dram_tensor("w2s", [HPC * 128, D], F32, kind="ExternalInput")
    w3s = nc.dram_tensor("w3s", [512, HPAD], F32, kind="ExternalInput")
    Cc_d = nc.dram_tensor("Cc", [KCF, 128, FQP], BF16, kind="ExternalInput")
    Sc_d = nc.dram_tensor("Sc", [KCF, 128, FQP], BF16, kind="ExternalInput")
    CiC_d = nc.dram_tensor("CiC", [KCQ, 128, D], BF16, kind="ExternalInput")
    CiS_d = nc.dram_tensor("CiS", [KCQ, 128, D], BF16, kind="ExternalInput")
    idn_d = nc.dram_tensor("idn", [128, 128], BF16, kind="ExternalInput")
    bcol_d = nc.dram_tensor("bcol", [128, 16], F32, kind="ExternalInput")
    out_d = nc.dram_tensor("out", [RPC, D], F32, kind="ExternalOutput")

    out1_d = nc.dram_tensor("out1_s", [RPC, D], F32, kind="Internal")
    part_d = nc.dram_tensor("part_s", [RPC, D], F32, kind="Internal")
    dqm1_d = nc.dram_tensor("dqm1", [704, 8192], BF16, kind="Internal")
    dqm2_d = nc.dram_tensor("dqm2", [704, 8192], BF16, kind="Internal")
    dqm3_d = nc.dram_tensor("dqm3", [704, 8192], BF16, kind="Internal")
    dqa1_d = nc.dram_tensor("dqa1", [NCORES * 704, 8192], BF16,
                            kind="Internal", addr_space="Shared")
    dqa2_d = nc.dram_tensor("dqa2", [NCORES * 704, 8192], BF16,
                            kind="Internal", addr_space="Shared")
    dqa3_d = nc.dram_tensor("dqa3", [NCORES * 704, 8192], BF16,
                            kind="Internal", addr_space="Shared")

    def dram_ap(handle, offset, pattern):
        return bass.AP(handle, offset, pattern)

    _uid = [0]

    def _sb(name, shape, dt):
        _uid[0] += 1
        return nc.sbuf_tensor(f"{name}_{_uid[0]}", shape, dt)

    def _pt(name, shape, dt):
        _uid[0] += 1
        return nc.psum_tensor(f"{name}_{_uid[0]}", shape, dt)

    es_top = contextlib.ExitStack()
    with es_top:
        kb = KB(nc, es_top)

        idn = es_top.enter_context(_sb("idn_s", [128, 128], BF16))
        bcol = es_top.enter_context(_sb("bcol_s", [128, 16], F32))
        stats = es_top.enter_context(_sb("stats", [128, 96], F32))
        es_tp = contextlib.ExitStack()
        tpp = [es_tp.enter_context(_pt(f"tpp{i}", [128, 128], BF16))
               for i in range(2)]
        tpc = [0]

        kb.op("sync", lambda e: e.dma_start(out=idn[:, :], in_=idn_d.ap()),
              writes=[("idn",)], dma=True)
        kb.op("sync", lambda e: e.dma_start(out=bcol[:, :], in_=bcol_d.ap()),
              writes=[("bcol",)], dma=True)

        def transpose128(src_ap, dst_ap, reads, writes, tag=None):
            p = tpc[0] % 2
            tpc[0] += 1
            kb.op("tensor", lambda e, p=p, s=src_ap: e.transpose(
                tpp[p][:, :], s, idn[:, :]),
                reads=list(reads) + [("idn",)], writes=[("tpp", p)], tag=tag)
            kb.op("vector", lambda e, p=p, d=dst_ap: e.tensor_copy(
                d, tpp[p][:, :]),
                reads=[("tpp", p)], writes=list(writes), tag=tag)

        es_d = contextlib.ExitStack()
        wb2 = [es_d.enter_context(_sb(f"d_w{i}", [128, D], F32))
               for i in range(2)]
        F16 = mybir.dt.float16
        sbf = [es_d.enter_context(_sb(f"d_s{i}", [128, 2048], F16))
               for i in range(2)]
        accb = es_d.enter_context(_sb("d_acc", [128, 2048], F16))
        amc1 = es_d.enter_context(_sb("d_amc", [128, 64], F32))
        rcb = es_d.enter_context(_sb("d_rc", [128, 64], F32))
        dq2 = [es_d.enter_context(_sb(f"d_dq{i}", [128, D], BF16))
               for i in range(2)]
        slab1 = es_d.enter_context(_sb("d_slab", [128, D], BF16))
        tpd = [es_tp.enter_context(_pt(f"tpd{i}", [128, 128], BF16))
               for i in range(2)]
        tdc = [0]

        def transposeD(src_ap, dst_ap, reads, writes):
            p = tdc[0] % 2
            tdc[0] += 1
            kb.op("tensor", lambda e, p=p, s=src_ap: e.transpose(
                tpd[p][:, :], s, idn[:, :]),
                reads=list(reads) + [("idn",)], writes=[("tpd", p)], tag="D")
            kb.op("vector", lambda e, p=p, d=dst_ap: e.tensor_copy(
                d, tpd[p][:, :]),
                reads=[("tpd", p)], writes=list(writes), tag="D")

        def dq_math(src_ap, F, pm):
            nb = F // 64
            wbuf = wb2[pm]
            amc = amc1
            dqb = dq2[pm]
            kb.op("gpsimd", lambda e, a=src_ap, F=F, wbuf=wbuf:
                  e.dma_start(out=wbuf[:, 0:F], in_=a),
                  writes=[("wbuf", pm)], dma=True, tag="D")
            yield
            kb.op("vector", lambda e, F=F, nb=nb, wbuf=wbuf, amc=amc:
                  e.reduce_max(
                      amc[:, 0:nb],
                      wbuf[:, 0:F].rearrange("p (s k) -> p s k", k=64),
                      axis=AX.X, apply_absolute_value=True),
                  reads=[("wbuf", pm)], writes=[("am",)], tag="D")
            kb.op("vector", lambda e, nb=nb, amc=amc: e.tensor_scalar_max(
                amc[:, 0:nb], amc[:, 0:nb], 1e-30),
                reads=[("am",)], writes=[("amc", pm)], tag="D")
            kb.op("vector", lambda e, nb=nb, amc=amc: e.reciprocal(
                rcb[:, 0:nb], amc[:, 0:nb]),
                reads=[("amc", pm)], writes=[("rc",)], tag="D")
            kb.op("vector", lambda e, F=F, nb=nb, wbuf=wbuf: e.tensor_tensor(
                wbuf[:, 0:F].rearrange("p (s k) -> p s k", k=64),
                wbuf[:, 0:F].rearrange("p (s k) -> p s k", k=64),
                rcb[:, 0:nb].rearrange("p (s o) -> p s o", o=1)
                    .broadcast_to((128, nb, 64)),
                ALU.mult),
                reads=[("wbuf", pm), ("rc",)],
                writes=[("wbuf", pm), ("u", pm)],
                tag="D")
            yield
            for c0 in range(0, F, 2048):
                cw = min(2048, F - c0)
                cb = cw // 64
                for j in range(15):
                    p = j % 2
                    kb.op("scalar", lambda e, c0=c0, cw=cw, j=j, p=p,
                          wbuf=wbuf:
                          e.activation(sbf[p][:, 0:cw],
                                       wbuf[:, c0:c0 + cw], AF.Sign,
                                       bias=bcol[:, j:j + 1]),
                          reads=[("u", pm), ("wbuf", pm), ("bcol",)],
                          writes=[("sb", p)],
                          tag="D")
                    if j == 0:
                        kb.op("vector", lambda e, cw=cw, p=p:
                              e.tensor_scalar(accb[:, 0:cw],
                                              sbf[p][:, 0:cw],
                                              float(NF4_E[0]), None,
                                              ALU.mult),
                              reads=[("sb", p)], writes=[("acc",)],
                              tag="D")
                    else:
                        kb.op("vector", lambda e, cw=cw, j=j, p=p:
                              e.scalar_tensor_tensor(
                                  accb[:, 0:cw], sbf[p][:, 0:cw],
                                  float(NF4_E[j]), accb[:, 0:cw],
                                  ALU.mult, ALU.add),
                              reads=[("sb", p), ("acc",)],
                              writes=[("acc",)], tag="D")
                kb.op("vector", lambda e, c0=c0, cw=cw, cb=cb, amc=amc,
                      dqb=dqb: e.tensor_tensor(
                          dqb[:, c0:c0 + cw].rearrange(
                              "p (s k) -> p s k", k=64),
                          accb[:, 0:cw].rearrange(
                              "p (s k) -> p s k", k=64),
                          amc[:, c0 // 64:c0 // 64 + cb].rearrange(
                              "p (s o) -> p s o", o=1)
                              .broadcast_to((128, cb, 64)),
                          ALU.mult),
                      reads=[("acc",), ("amc", pm)],
                      writes=[("dq", pm, c0 // 2048)], tag="D")
                yield

        def d_gen():
            pm = 0
            for mat, wsrc, dqm in ((1, w1s, dqm1_d), (2, w2s, dqm2_d)):
                for j in range(HPC):
                    src = dram_ap(wsrc, j * 128 * D, [[D, 128], [1, D]])
                    yield from dq_math(src, D, pm)
                    dqb = dq2[pm]
                    slab = slab1
                    for kc in range(KCF):
                        transposeD(dqb[:, kc * 128:(kc + 1) * 128],
                                   slab[:, kc * 128:(kc + 1) * 128],
                                   [("dq", pm, (kc * 128) // 2048)],
                                   [("slab", kc)])
                        if kc in (7, 15, 23):
                            yield
                    res = (f"dqm{mat}", j)
                    kb.op("sync", lambda e, j=j, dqm=dqm, slab=slab:
                          e.dma_start(
                              out=dram_ap(dqm, j * SLAB_W,
                                          [[D, 128], [1, D]]),
                              in_=slab[:, :]),
                          reads=[("slab", kc) for kc in range(KCF)],
                          writes=[res], dma=True, tag="D")
                    pm = 1 - pm
                    yield
                sem = "cc1" if mat == 1 else "cc2"
                dqm_l, dqa_l = (dqm1_d, dqa1_d) if mat == 1 else (dqm2_d, dqa2_d)
                kb.cc(lambda e, dqm_l=dqm_l, dqa_l=dqa_l: e.collective_compute(
                    "AllGather", ALU.bypass,
                    replica_groups=[list(range(NCORES))],
                    ins=[dqm_l.ap()], outs=[dqa_l.ap()]),
                    sem=sem,
                    reads=[(f"dqm{mat}", j) for j in range(HPC)],
                    writes=[(f"dqa{mat}",)])
                yield
            for hcg in range(11):
                slab = slab1
                for dc in range(4):
                    src = dram_ap(w3s, dc * 128 * HPAD + hcg * 1024,
                                  [[HPAD, 128], [1, 1024]])
                    yield from dq_math(src, 1024, pm)
                    dqb = dq2[pm]
                    for j in range(8):
                        col = (j * 512 + dc * 128) // 128
                        transposeD(dqb[:, j * 128:(j + 1) * 128],
                                   slab[:, j * 512 + dc * 128:
                                        j * 512 + dc * 128 + 128],
                                   [("dq", pm, 0)], [("slab", col)])
                    pm = 1 - pm
                for j in range(8):
                    hc = hcg * 8 + j
                    res = ("dqm3", hc)
                    kb.op("sync", lambda e, hc=hc, j=j, slab=slab:
                          e.dma_start(
                              out=dram_ap(dqm3_d, hc * PIECE_W,
                                          [[512, 128], [1, 512]]),
                              in_=slab[:, j * 512:(j + 1) * 512]),
                          reads=[("slab", (j * 512 + dc * 128) // 128)
                                 for dc in range(4)],
                          writes=[res], dma=True, tag="D")
                yield
            kb.cc(lambda e: e.collective_compute(
                "AllGather", ALU.bypass,
                replica_groups=[list(range(NCORES))],
                ins=[dqm3_d.ap()], outs=[dqa3_d.ap()]),
                sem="cc3",
                reads=[("dqm3", hc) for hc in range(NHC)],
                writes=[("dqa3",)])

        dgen = d_gen()
        dq_done = [False]

        PUMP_INTERLEAVE = True

        def pump(k=1):
            if not PUMP_INTERLEAVE:
                return
            for _ in range(k):
                try:
                    next(dgen)
                except StopIteration:
                    dq_done[0] = True

        with contextlib.ExitStack() as pa:
            iota_i = pa.enter_context(_sb("iota_i", [128, FQP], I32))
            c1024 = pa.enter_context(_sb("c1024", [128, 1], I32))
            kb.op("gpsimd", lambda e: e.iota(
                iota_i[:, :], [[1, FQP]], channel_multiplier=0,
                allow_small_or_imprecise_dtypes=True),
                writes=[("iota",)], tag="A")
            kb.op("gpsimd", lambda e: e.memset(c1024[:, :], 1024),
                  writes=[("c1024",)], tag="A")

            for hf in range(2):
                row0 = hf * HALF
                with contextlib.ExitStack() as phh:
                    ytr = [phh.enter_context(
                        _sb(f"a_ytr{r}", [128, KCQ, 128], BF16))
                        for r in range(4)]
                    yti = [phh.enter_context(
                        _sb(f"a_yti{r}", [128, KCQ, 128], BF16))
                        for r in range(4)]
                    d08 = phh.enter_context(_sb("a_d08", [128, 4], I32))
                    cof8 = phh.enter_context(_sb("a_cof8", [128, 4], F32))
                    es_x = contextlib.ExitStack()
                    xrb = [es_x.enter_context(_sb(f"a_xr{r}", [128, FQP], BF16))
                           for r in range(4)]
                    psb = [es_x.enter_context(_sb(f"a_ps{r}", [128, FQP], BF16))
                           for r in range(4)]

                    with contextlib.ExitStack() as ps:
                        xht = ps.enter_context(
                            _sb("a_xht", [128, KCF, HALF], BF16))
                        with contextlib.ExitStack() as ps2:
                            xt = ps2.enter_context(_sb("a_xt", [128, D], F32))
                            sqc = ps2.enter_context(
                                _sb("a_sqc", [128, 512], F32))
                            sqp = ps2.enter_context(
                                _sb("a_sqp", [128, 8], F32))
                            xhb = ps2.enter_context(
                                _sb("a_xhb", [128, D], BF16))
                            for r in range(4):
                                rb = (row0 + r * 128) * D
                                kb.op("sync", lambda e, rb=rb: e.dma_start(
                                    out=xt[:, :],
                                    in_=dram_ap(x_in, rb, [[D, 128], [1, D]])),
                                    writes=[("xt",)], dma=True, tag="A")
                                cb = 8 * (hf * 4 + r)
                                kb.op("vector", lambda e, cb=cb: e.reduce_sum(
                                    stats[:, cb:cb + 1], xt[:, :], axis=AX.X),
                                    reads=[("xt",)], writes=[("st", cb)],
                                    tag="A")
                                for q8 in range(8):
                                    kb.op("scalar", lambda e, q8=q8:
                                          e.activation(
                                              sqc[:, :],
                                              xt[:, q8 * 512:(q8 + 1) * 512],
                                              AF.Square),
                                          reads=[("xt",)], writes=[("sqc",)],
                                          tag="A")
                                    kb.op("vector", lambda e, q8=q8:
                                          e.reduce_sum(
                                              sqp[:, q8:q8 + 1], sqc[:, :],
                                              axis=AX.X),
                                          reads=[("sqc",)],
                                          writes=[("sqp", q8)], tag="A")
                                kb.op("vector", lambda e, cb=cb: e.reduce_sum(
                                    stats[:, cb + 1:cb + 2], sqp[:, :],
                                    axis=AX.X),
                                    reads=[("sqp", q) for q in range(8)],
                                    writes=[("st", cb + 1)], tag="A")
                                kb.op("vector", lambda e, cb=cb:
                                      e.tensor_scalar(
                                          stats[:, cb + 2:cb + 3],
                                          stats[:, cb:cb + 1], 1.0 / D, None,
                                          ALU.mult),
                                      reads=[("st", cb)],
                                      writes=[("st", cb + 2)], tag="A")
                                kb.op("vector", lambda e, cb=cb:
                                      e.tensor_scalar(
                                          stats[:, cb + 3:cb + 4],
                                          stats[:, cb + 1:cb + 2], 1.0 / D,
                                          None, ALU.mult),
                                      reads=[("st", cb + 1)],
                                      writes=[("st", cb + 3)], tag="A")
                                kb.op("vector", lambda e, cb=cb:
                                      e.tensor_tensor(
                                          stats[:, cb + 4:cb + 5],
                                          stats[:, cb + 2:cb + 3],
                                          stats[:, cb + 2:cb + 3], ALU.mult),
                                      reads=[("st", cb + 2)],
                                      writes=[("st", cb + 4)], tag="A")
                                kb.op("vector", lambda e, cb=cb:
                                      e.tensor_tensor(
                                          stats[:, cb + 5:cb + 6],
                                          stats[:, cb + 3:cb + 4],
                                          stats[:, cb + 4:cb + 5],
                                          ALU.subtract),
                                      reads=[("st", cb + 3), ("st", cb + 4)],
                                      writes=[("st", cb + 5)], tag="A")
                                kb.op("vector", lambda e, cb=cb:
                                      e.tensor_scalar(
                                          stats[:, cb + 5:cb + 6],
                                          stats[:, cb + 5:cb + 6], EPS, None,
                                          ALU.add),
                                      reads=[("st", cb + 5)],
                                      writes=[("st", cb + 5)], tag="A")
                                kb.op("scalar", lambda e, cb=cb: e.activation(
                                    stats[:, cb + 6:cb + 7],
                                    stats[:, cb + 5:cb + 6], AF.Ln),
                                    reads=[("st", cb + 5)],
                                    writes=[("st", cb + 6)], tag="A")
                                kb.op("scalar", lambda e, cb=cb: e.activation(
                                    stats[:, cb + 7:cb + 8],
                                    stats[:, cb + 6:cb + 7], AF.Exp,
                                    scale=-0.5),
                                    reads=[("st", cb + 6)],
                                    writes=[("st", cb + 7)], tag="A")
                                kb.op("vector", lambda e, cb=cb:
                                      e.scalar_tensor_tensor(
                                          stats[:, cb + 4:cb + 5],
                                          stats[:, cb + 2:cb + 3], -1.0,
                                          stats[:, cb + 7:cb + 8],
                                          ALU.mult, ALU.mult),
                                      reads=[("st", cb + 2), ("st", cb + 7)],
                                      writes=[("st", cb + 4)], tag="A")
                                kb.op("scalar", lambda e, cb=cb: e.activation(
                                    xhb[:, :], xt[:, :], AF.Identity,
                                    bias=stats[:, cb + 4:cb + 5],
                                    scale=stats[:, cb + 7:cb + 8]),
                                    reads=[("xt",), ("st", cb + 4),
                                           ("st", cb + 7)],
                                    writes=[("xhb",)], tag="A")
                                for kc in range(KCF):
                                    transpose128(
                                        xhb[:, kc * 128:(kc + 1) * 128],
                                        xht[:, kc, r * 128:(r + 1) * 128],
                                        [("xhb",)], [("xht", kc, r)], tag="A")
                                pump(3)
                        kb.fence("A")

                        with contextlib.ExitStack() as pf:
                            KCB = 4
                            ccb = [pf.enter_context(
                                _sb(f"a_cc{i}", [128, KCB * 512], BF16))
                                for i in range(2)]
                            scb = [pf.enter_context(
                                _sb(f"a_sc{i}", [128, KCB * 512], BF16))
                                for i in range(2)]
                            pxr = [pf.enter_context(
                                _pt(f"a_pxr{i}", [128, 512], F32))
                                for i in range(2)]
                            pps = [pf.enter_context(
                                _pt(f"a_pps{i}", [128, 512], F32))
                                for i in range(2)]
                            for nbi, (nb0, nbw) in enumerate(NB_FWD):
                                for rg in range(2):
                                    for kcb in range(KCF // KCB):
                                        p = kcb % 2
                                        kc0 = kcb * KCB
                                        pat = [[FQP, 128],
                                               [128 * FQP, KCB], [1, nbw]]
                                        kb.op("sync", lambda e, p=p, kc0=kc0,
                                              nb0=nb0, nbw=nbw, pat=pat:
                                              e.dma_start(
                                                  out=ccb[p][:, 0:KCB * nbw],
                                                  in_=dram_ap(
                                                      Cc_d,
                                                      kc0 * 128 * FQP + nb0,
                                                      pat)),
                                              writes=[("ccb", p)], dma=True,
                                              tag="A")
                                        kb.op("gpsimd", lambda e, p=p,
                                              kc0=kc0, nb0=nb0, nbw=nbw,
                                              pat=pat: e.dma_start(
                                                  out=scb[p][:, 0:KCB * nbw],
                                                  in_=dram_ap(
                                                      Sc_d,
                                                      kc0 * 128 * FQP + nb0,
                                                      pat)),
                                              writes=[("scb", p)], dma=True,
                                              tag="A")
                                        for kci in range(KCB):
                                            kc = kc0 + kci
                                            st_ = (kc == 0)
                                            sp_ = (kc == KCF - 1)
                                            c0 = kci * nbw
                                            for ri in range(2):
                                                r = rg * 2 + ri
                                                kb.op("tensor",
                                                      lambda e, r=r, ri=ri,
                                                      kc=kc, p=p, nbw=nbw,
                                                      c0=c0, st_=st_,
                                                      sp_=sp_: e.matmul(
                                                          pxr[ri][:, 0:nbw],
                                                          xht[:, kc,
                                                              r * 128:(r + 1) * 128],
                                                          ccb[p][:, c0:c0 + nbw],
                                                          start=st_, stop=sp_),
                                                      reads=[("xht", kc, r),
                                                             ("ccb", p)],
                                                      writes=[("pxr", ri)] if (st_ or sp_) else (),
                                                      tag="A")
                                                kb.op("tensor",
                                                      lambda e, r=r, ri=ri,
                                                      kc=kc, p=p, nbw=nbw,
                                                      c0=c0, st_=st_,
                                                      sp_=sp_: e.matmul(
                                                          pps[ri][:, 0:nbw],
                                                          xht[:, kc,
                                                              r * 128:(r + 1) * 128],
                                                          scb[p][:, c0:c0 + nbw],
                                                          start=st_, stop=sp_),
                                                      reads=[("xht", kc, r),
                                                             ("scb", p)],
                                                      writes=[("pps", ri)] if (st_ or sp_) else (),
                                                      tag="A")
                                        pump(1)
                                    for ri in range(2):
                                        r = rg * 2 + ri
                                        kb.op("vector", lambda e, r=r, ri=ri,
                                              nb0=nb0, nbw=nbw: e.tensor_copy(
                                                  xrb[r][:, nb0:nb0 + nbw],
                                                  pxr[ri][:, 0:nbw]),
                                              reads=[("pxr", ri)],
                                              writes=[("xr", r)], tag="A")
                                        kb.op("vector", lambda e, r=r, ri=ri,
                                              nb0=nb0, nbw=nbw: e.tensor_copy(
                                                  psb[r][:, nb0:nb0 + nbw],
                                                  pps[ri][:, 0:nbw]),
                                              reads=[("pps", ri)],
                                              writes=[("ps", r)], tag="A")
                                    pump(3)
                            kb.fence("A")
                    kb.fence("A")

                    with contextlib.ExitStack() as ps:
                        corb = [ps.enter_context(
                            _sb(f"a_cor{r}", [128, 2048], BF16))
                            for r in range(4)]
                        with contextlib.ExitStack() as psp:
                            ptb = [psp.enter_context(
                                _sb(f"a_pt{r}", [128, KCQ, 128], BF16))
                                for r in range(4)]
                            with contextlib.ExitStack() as ps2:
                                tA = ps2.enter_context(
                                    _sb("a_tA", [128, 1088], BF16))
                                tB = ps2.enter_context(
                                    _sb("a_tB", [128, 1088], BF16))
                                pb = ps2.enter_context(
                                    _sb("a_pb", [128, FQP], BF16))
                                for r in range(4):
                                    for ch in range(2):
                                        c0 = ch * 1088
                                        kb.op("scalar", lambda e, r=r, c0=c0:
                                              e.activation(
                                                  tA[:, :],
                                                  xrb[r][:, c0:c0 + 1088],
                                                  AF.Square),
                                              reads=[("xr", r)],
                                              writes=[("tA",)], tag="A")
                                        kb.op("vector", lambda e, r=r, c0=c0:
                                              e.tensor_tensor(
                                                  tB[:, :],
                                                  psb[r][:, c0:c0 + 1088],
                                                  psb[r][:, c0:c0 + 1088],
                                                  ALU.mult),
                                              reads=[("ps", r)],
                                              writes=[("tB",)], tag="A")
                                        kb.op("vector", lambda e, c0=c0:
                                              e.tensor_tensor(
                                                  pb[:, c0:c0 + 1088],
                                                  tA[:, :], tB[:, :],
                                                  ALU.add),
                                              reads=[("tA",), ("tB",)],
                                              writes=[("pb", c0)], tag="A")
                                    for kq in range(KCQ):
                                        transpose128(
                                            pb[:, kq * 128:(kq + 1) * 128],
                                            ptb[r][:, kq, :],
                                            [("pb", 0), ("pb", 1088)],
                                            [("ptb", r, kq)], tag="A")
                                    pump(3)
                                kb.fence("A")

                            with contextlib.ExitStack() as ps2:
                                cib = ps2.enter_context(
                                    _sb("a_ci", [128, KCQ * 512], BF16))
                                pc = [ps2.enter_context(
                                    _pt(f"a_pc{i}", [128, 512], F32))
                                    for i in range(4)]
                                for n in range(4):
                                    kb.op("gpsimd", lambda e, n=n:
                                          e.dma_start(
                                              out=cib[:, :],
                                              in_=dram_ap(
                                                  CiC_d, n * 512,
                                                  [[D, 128],
                                                   [128 * D, KCQ],
                                                   [1, 512]])),
                                          writes=[("cib",)],
                                          dma=True, tag="A")
                                    for rg in range(2):
                                        par = rg % 2
                                        for kq in range(KCQ):
                                            for ri in range(2):
                                                r = rg * 2 + ri
                                                bank = par * 2 + ri
                                                st_ = (kq == 0)
                                                sp_ = (kq == KCQ - 1)
                                                kb.op("tensor",
                                                      lambda e, r=r, kq=kq,
                                                      bank=bank,
                                                      st_=st_, sp_=sp_:
                                                      e.matmul(
                                                          pc[bank][:, :],
                                                          ptb[r][:, kq, :],
                                                          cib[:, kq * 512:
                                                              (kq + 1) * 512],
                                                          start=st_,
                                                          stop=sp_),
                                                      reads=[("ptb", r, kq),
                                                             ("cib",)],
                                                      writes=[("pc", bank)] if (st_ or sp_) else (),
                                                      tag="A")
                                        for ri in range(2):
                                            r = rg * 2 + ri
                                            bank = par * 2 + ri
                                            kb.op("vector", lambda e, r=r,
                                                  n=n, bank=bank:
                                                  e.tensor_copy(
                                                      corb[r][:, n * 512:
                                                              (n + 1) * 512],
                                                      pc[bank][:, :]),
                                                  reads=[("pc", bank)],
                                                  writes=[("cor", r, n)],
                                                  tag="A")
                                        pump(3)
                                kb.fence("A")

                        with contextlib.ExitStack() as ps2:
                            t8 = ps2.enter_context(_sb("a_t8", [128, 8], BF16))
                            i8 = ps2.enter_context(_sb("a_i8", [128, 8], U32))
                            dvb = ps2.enter_context(_sb("a_dv", [128, 8], F32))
                            exb = ps2.enter_context(_sb("a_ex", [128, 8], F32))
                            den = ps2.enter_context(_sb("a_den", [128, 1], F32))
                            for r in range(4):
                                corres = [("cor", r, n) for n in range(4)]
                                kb.op("vector", lambda e, r=r: e.max(
                                    t8[:, :], corb[r][:, :]),
                                    reads=corres, writes=[("t8",)], tag="A")
                                kb.op("vector", lambda e, r=r: e.max_index(
                                    i8[:, :], t8[:, :], corb[r][:, :]),
                                    reads=corres + [("t8",)],
                                    writes=[("i8",)], tag="A")
                                kb.op("vector", lambda e: e.tensor_tensor(
                                    dvb[:, :], t8[:, :],
                                    t8[:, 0:1].broadcast_to((128, 8)),
                                    ALU.subtract),
                                    reads=[("t8",)], writes=[("dv",)],
                                    tag="A")
                                kb.op("scalar", lambda e: e.activation(
                                    exb[:, :], dvb[:, :], AF.Exp,
                                    accum_out=den[:, :]),
                                    reads=[("dv",)],
                                    writes=[("den",), ("exv",)], tag="A")
                                kb.op("vector", lambda e, r=r: e.reciprocal(
                                    cof8[:, r:r + 1], den[:, :]),
                                    reads=[("den",)], writes=[("cof", r)],
                                    tag="A")
                                kb.op("vector", lambda e, r=r: e.tensor_copy(
                                    d08[:, r:r + 1], i8[:, 0:1]),
                                    reads=[("i8",)], writes=[("d0", r)],
                                    tag="A")
                                pump(1)
                        kb.fence("A")

                    with contextlib.ExitStack() as ps2:
                        mi1 = ps2.enter_context(_sb("a_mi1", [128, 1088], I32))
                        mi2 = ps2.enter_context(_sb("a_mi2", [128, 1088], I32))
                        tf1 = ps2.enter_context(_sb("a_tf1", [128, 1088], F32))
                        tf2 = ps2.enter_context(_sb("a_tf2", [128, 1088], F32))
                        csb = ps2.enter_context(_sb("a_cs", [128, FQP], BF16))
                        snb = ps2.enter_context(_sb("a_sn", [128, FQP], BF16))
                        y1 = ps2.enter_context(_sb("a_y1", [128, 1088], BF16))
                        y2 = ps2.enter_context(_sb("a_y2", [128, 1088], BF16))
                        y3 = ps2.enter_context(_sb("a_y3", [128, FQP], BF16))

                        for r in range(4):
                            for ch in range(2):
                                c0 = ch * 1088
                                kb.op("vector", lambda e, c0=c0, r=r:
                                      e.tensor_tensor(
                                          mi1[:, :],
                                          iota_i[:, c0:c0 + 1088],
                                          d08[:, r:r + 1].broadcast_to(
                                              (128, 1088)), ALU.mult),
                                      reads=[("iota",), ("d0", r)],
                                      writes=[("mi1",)], tag="A")
                                kb.op("vector", lambda e: e.tensor_tensor(
                                    mi2[:, :], mi1[:, :],
                                    c1024[:, :].broadcast_to((128, 1088)),
                                    ALU.add),
                                    reads=[("mi1",), ("c1024",)],
                                    writes=[("mi2",)], tag="A")
                                for mi, tf, outb, mk, tk, ok_, sk in (
                                        (mi1, tf1, snb, ("mi1",), ("tf1",),
                                         ("tf2",), ("snb",)),
                                        (mi2, tf2, csb, ("mi2",), ("tf2",),
                                         ("tf1",), ("csb",))):
                                    oth = tf2 if tf is tf1 else tf1
                                    kb.op("vector", lambda e, mi=mi, mk=mk:
                                          e.tensor_single_scalar(
                                              mi[:, :], mi[:, :], 4095,
                                              ALU.bitwise_and),
                                          reads=[mk], writes=[mk], tag="A")
                                    kb.op("vector", lambda e, mi=mi, tf=tf:
                                          e.tensor_copy(tf[:, :], mi[:, :]),
                                          reads=[mk], writes=[tk], tag="A")
                                    kb.op("vector", lambda e, tf=tf:
                                          e.tensor_scalar(
                                              tf[:, :], tf[:, :],
                                              PHASE_SCALE, None, ALU.mult),
                                          reads=[tk], writes=[tk], tag="A")
                                    kb.op("vector", lambda e, tf=tf, oth=oth:
                                          e.tensor_scalar(
                                              oth[:, :], tf[:, :],
                                              float(np.pi), TWO_PI,
                                              ALU.is_gt, ALU.mult),
                                          reads=[tk], writes=[ok_], tag="A")
                                    kb.op("vector", lambda e, tf=tf, oth=oth:
                                          e.tensor_tensor(
                                              tf[:, :], tf[:, :], oth[:, :],
                                              ALU.subtract),
                                          reads=[tk, ok_], writes=[tk],
                                          tag="A")
                                    kb.op("scalar", lambda e, tf=tf,
                                          outb=outb, c0=c0: e.activation(
                                              outb[:, c0:c0 + 1088],
                                              tf[:, :], AF.Sin),
                                          reads=[tk], writes=[sk], tag="A")
                                pump(2)
                            for which in range(2):
                                a_op = ALU.add if which == 0 else ALU.subtract
                                m1 = csb if which == 0 else snb
                                m2 = snb if which == 0 else csb
                                yt = ytr if which == 0 else yti
                                ykey = "ytr" if which == 0 else "yti"
                                for ch in range(2):
                                    c0 = ch * 1088
                                    kb.op("vector", lambda e, r=r, c0=c0,
                                          m1=m1: e.tensor_tensor(
                                              y1[:, :],
                                              xrb[r][:, c0:c0 + 1088],
                                              m1[:, c0:c0 + 1088], ALU.mult),
                                          reads=[("xr", r), ("csb",),
                                                 ("snb",)],
                                          writes=[("y1",)], tag="A")
                                    kb.op("vector", lambda e, r=r, c0=c0,
                                          m2=m2: e.tensor_tensor(
                                              y2[:, :],
                                              psb[r][:, c0:c0 + 1088],
                                              m2[:, c0:c0 + 1088], ALU.mult),
                                          reads=[("ps", r), ("csb",),
                                                 ("snb",)],
                                          writes=[("y2",)], tag="A")
                                    kb.op("vector", lambda e, c0=c0,
                                          a_op=a_op: e.tensor_tensor(
                                              y3[:, c0:c0 + 1088], y1[:, :],
                                              y2[:, :], a_op),
                                          reads=[("y1",), ("y2",)],
                                          writes=[("y3", c0)], tag="A")
                                    kb.op("vector", lambda e, c0=c0, r=r:
                                          e.tensor_scalar(
                                              y3[:, c0:c0 + 1088],
                                              y3[:, c0:c0 + 1088],
                                              cof8[:, r:r + 1], None,
                                              ALU.mult),
                                          reads=[("y3", c0), ("cof", r)],
                                          writes=[("y3", c0)], tag="A")
                                for kq in range(KCQ):
                                    transpose128(
                                        y3[:, kq * 128:(kq + 1) * 128],
                                        yt[r][:, kq, :],
                                        [("y3", 0), ("y3", 1088)],
                                        [(ykey, r, kq)], tag="A")
                                pump(2)
                        kb.fence("A")

                    es_x.close()
                    with contextlib.ExitStack() as ps:
                        cb2 = [ps.enter_context(
                            _sb(f"a_cj{i}", [128, KCQ * 512], BF16))
                            for i in range(2)]
                        sb2 = [ps.enter_context(
                            _sb(f"a_sj{i}", [128, KCQ * 512], BF16))
                            for i in range(2)]
                        xsl = [ps.enter_context(
                            _sb(f"a_xs{i}", [128, 512], F32))
                            for i in range(2)]
                        o1b = [ps.enter_context(
                            _sb(f"a_o1{i}", [128, 512], F32))
                            for i in range(2)]
                        pr = [ps.enter_context(
                            _pt(f"a_pr{i}", [128, 512], F32))
                            for i in range(4)]
                        for n in range(8):
                            p = n % 2
                            kb.op("sync", lambda e, p=p, n=n:
                                  e.dma_start(
                                      out=cb2[p][:, :],
                                      in_=dram_ap(
                                          CiC_d, n * 512,
                                          [[D, 128], [128 * D, KCQ],
                                           [1, 512]])),
                                  writes=[("cj", p)], dma=True,
                                  tag="A")
                            kb.op("gpsimd", lambda e, p=p, n=n:
                                  e.dma_start(
                                      out=sb2[p][:, :],
                                      in_=dram_ap(
                                          CiS_d, n * 512,
                                          [[D, 128], [128 * D, KCQ],
                                           [1, 512]])),
                                  writes=[("sj", p)], dma=True,
                                  tag="A")
                            for rg in range(2):
                                par = (n * 2 + rg) % 2
                                for kq in range(KCQ):
                                    for ri in range(2):
                                        r = rg * 2 + ri
                                        bank = par * 2 + ri
                                        kb.op("tensor", lambda e, r=r, kq=kq,
                                              p=p, bank=bank, st_=(kq == 0):
                                              e.matmul(pr[bank][:, :],
                                                       ytr[r][:, kq, :],
                                                       cb2[p][:, kq * 512:
                                                              (kq + 1) * 512],
                                                       start=st_, stop=False),
                                              reads=[("ytr", r, kq),
                                                     ("cj", p)],
                                              writes=[("pr", bank)] if kq == 0 else (),
                                              tag="A")
                                        kb.op("tensor", lambda e, r=r, kq=kq,
                                              p=p, bank=bank,
                                              sp_=(kq == KCQ - 1):
                                              e.matmul(pr[bank][:, :],
                                                       yti[r][:, kq, :],
                                                       sb2[p][:, kq * 512:
                                                              (kq + 1) * 512],
                                                       start=False, stop=sp_),
                                              reads=[("yti", r, kq),
                                                     ("sj", p)],
                                              writes=[("pr", bank)] if kq == KCQ - 1 else (),
                                              tag="A")
                                for ri in range(2):
                                    r = rg * 2 + ri
                                    bank = par * 2 + ri
                                    bp = (n * 4 + r) % 2
                                    off = (row0 + r * 128) * D + n * 512
                                    kb.op("sync", lambda e, bp=bp, off=off:
                                          e.dma_start(
                                              out=xsl[bp][:, :],
                                              in_=dram_ap(x_in, off,
                                                          [[D, 128],
                                                           [1, 512]])),
                                          writes=[("xsl", bp)], dma=True,
                                          tag="A")
                                    kb.op("vector", lambda e, bank=bank,
                                          bp=bp: e.tensor_tensor(
                                              o1b[bp][:, :], pr[bank][:, :],
                                              xsl[bp][:, :], ALU.add),
                                          reads=[("pr", bank), ("xsl", bp)],
                                          writes=[("o1b", bp)], tag="A")
                                    kb.op("sync", lambda e, bp=bp, off=off:
                                          e.dma_start(
                                              out=dram_ap(out1_d, off,
                                                          [[D, 128],
                                                           [1, 512]]),
                                              in_=o1b[bp][:, :]),
                                          reads=[("o1b", bp)],
                                          writes=[("out1", off)], dma=True,
                                          tag="A")
                                pump(3)
                        kb.fence("A")
                    kb.fence("A")

            while not dq_done[0]:
                try:
                    next(dgen)
                except StopIteration:
                    dq_done[0] = True
        es_d.close()
        kb.barrier()

        out1_res = [("out1", (r * 128) * D + n * 512)
                    for r in range(8) for n in range(8)]

        with contextlib.ExitStack() as pf:
            x2t = pf.enter_context(_sb("f_x2t", [128, KCF, RPC], BF16))

            with contextlib.ExitStack() as ps:
                ot = ps.enter_context(_sb("f_ot", [128, D], F32))
                fsqc = ps.enter_context(_sb("f_sqc", [128, 512], F32))
                fsqp = ps.enter_context(_sb("f_sqp", [128, 8], F32))
                fxhb = ps.enter_context(_sb("f_xhb", [128, D], BF16))
                for rt in range(8):
                    rb = rt * 128 * D
                    kb.op("sync", lambda e, rb=rb: e.dma_start(
                        out=ot[:, :],
                        in_=dram_ap(out1_d, rb, [[D, 128], [1, D]])),
                        reads=out1_res, writes=[("f_ot",)], dma=True)
                    cb = 64 + 8 * (rt % 4)
                    kb.op("vector", lambda e, cb=cb: e.reduce_sum(
                        stats[:, cb:cb + 1], ot[:, :], axis=AX.X),
                        reads=[("f_ot",)], writes=[("st", cb)])
                    for q8 in range(8):
                        kb.op("scalar", lambda e, q8=q8: e.activation(
                            fsqc[:, :], ot[:, q8 * 512:(q8 + 1) * 512],
                            AF.Square),
                            reads=[("f_ot",)], writes=[("sqc",)])
                        kb.op("vector", lambda e, q8=q8: e.reduce_sum(
                            fsqp[:, q8:q8 + 1], fsqc[:, :], axis=AX.X),
                            reads=[("sqc",)], writes=[("sqp", q8)])
                    kb.op("vector", lambda e, cb=cb: e.reduce_sum(
                        stats[:, cb + 1:cb + 2], fsqp[:, :], axis=AX.X),
                        reads=[("sqp", q) for q in range(8)],
                        writes=[("st", cb + 1)])
                    kb.op("vector", lambda e, cb=cb: e.tensor_scalar(
                        stats[:, cb + 2:cb + 3], stats[:, cb:cb + 1],
                        1.0 / D, None, ALU.mult),
                        reads=[("st", cb)], writes=[("st", cb + 2)])
                    kb.op("vector", lambda e, cb=cb: e.tensor_scalar(
                        stats[:, cb + 3:cb + 4], stats[:, cb + 1:cb + 2],
                        1.0 / D, None, ALU.mult),
                        reads=[("st", cb + 1)], writes=[("st", cb + 3)])
                    kb.op("vector", lambda e, cb=cb: e.tensor_tensor(
                        stats[:, cb + 4:cb + 5], stats[:, cb + 2:cb + 3],
                        stats[:, cb + 2:cb + 3], ALU.mult),
                        reads=[("st", cb + 2)], writes=[("st", cb + 4)])
                    kb.op("vector", lambda e, cb=cb: e.tensor_tensor(
                        stats[:, cb + 5:cb + 6], stats[:, cb + 3:cb + 4],
                        stats[:, cb + 4:cb + 5], ALU.subtract),
                        reads=[("st", cb + 3), ("st", cb + 4)],
                        writes=[("st", cb + 5)])
                    kb.op("vector", lambda e, cb=cb: e.tensor_scalar(
                        stats[:, cb + 5:cb + 6], stats[:, cb + 5:cb + 6],
                        EPS, None, ALU.add),
                        reads=[("st", cb + 5)], writes=[("st", cb + 5)])
                    kb.op("scalar", lambda e, cb=cb: e.activation(
                        stats[:, cb + 6:cb + 7], stats[:, cb + 5:cb + 6],
                        AF.Ln),
                        reads=[("st", cb + 5)], writes=[("st", cb + 6)])
                    kb.op("scalar", lambda e, cb=cb: e.activation(
                        stats[:, cb + 7:cb + 8], stats[:, cb + 6:cb + 7],
                        AF.Exp, scale=-0.5),
                        reads=[("st", cb + 6)], writes=[("st", cb + 7)])
                    kb.op("vector", lambda e, cb=cb: e.scalar_tensor_tensor(
                        stats[:, cb + 4:cb + 5], stats[:, cb + 2:cb + 3],
                        -1.0, stats[:, cb + 7:cb + 8], ALU.mult, ALU.mult),
                        reads=[("st", cb + 2), ("st", cb + 7)],
                        writes=[("st", cb + 4)])
                    kb.op("scalar", lambda e, cb=cb: e.activation(
                        fxhb[:, :], ot[:, :], AF.Identity,
                        bias=stats[:, cb + 4:cb + 5],
                        scale=stats[:, cb + 7:cb + 8]),
                        reads=[("f_ot",), ("st", cb + 4), ("st", cb + 7)],
                        writes=[("f_xhb",)])
                    for kc in range(KCF):
                        transpose128(fxhb[:, kc * 128:(kc + 1) * 128],
                                     x2t[:, kc, rt * 128:(rt + 1) * 128],
                                     [("f_xhb",)], [("x2t", kc, rt)])
                kb.barrier()
            es_tp.close()

            x2t_kc = {kc: [("x2t", kc, rt) for rt in range(8)]
                      for kc in range(KCF)}

            ht = pf.enter_context(_sb("f_ht", [128, 43, RPC], BF16))

            def slab_ap(mat, hc):
                r, j = divmod(hc, HPC)
                dqa_l = dqa1_d if mat == 1 else dqa2_d
                base = r * DQ1_ELEMS + j * SLAB_W
                return dram_ap(dqa_l, base, [[4096, 128], [1, 4096]])

            def piece_ap(hc, nq):
                base = nq * DQ3_ELEMS + hc * PIECE_W
                return dram_ap(dqa3_d, base, [[512, 128], [1, 512]])

            for hph in range(2):
                hcs = list(range(hph * 43, hph * 43 + 43))

                with contextlib.ExitStack() as ps:
                    s1b = [ps.enter_context(
                        _sb(f"f_s1{i}", [128, KCF, 128], BF16))
                        for i in range(2)]
                    s2b = [ps.enter_context(
                        _sb(f"f_s2{i}", [128, KCF, 128], BF16))
                        for i in range(2)]
                    sg = ps.enter_context(_sb("f_sg", [128, RPC], F32))
                    pg = [ps.enter_context(_pt(f"f_pg{i}", [128, RPC], F32))
                          for i in range(2)]
                    pu = [ps.enter_context(_pt(f"f_pu{i}", [128, RPC], F32))
                          for i in range(2)]
                    for i, hc in enumerate(hcs):
                        par = i % 2
                        kb.op("sync", lambda e, par=par, hc=hc: e.dma_start(
                            out=s1b[par][:, :, :].rearrange(
                                "p a b -> p (a b)"),
                            in_=slab_ap(1, hc)),
                            reads=[("dqa1",)], writes=[("s1b", par)],
                            dma=True)
                        kb.op("sync", lambda e, par=par, hc=hc: e.dma_start(
                            out=s2b[par][:, :, :].rearrange(
                                "p a b -> p (a b)"),
                            in_=slab_ap(2, hc)),
                            reads=[("dqa2",)], writes=[("s2b", par)],
                            dma=True)
                        for kc in range(KCF):
                            for ns in range(2):
                                st_ = (kc == 0)
                                sp_ = (kc == KCF - 1)
                                kb.op("tensor", lambda e, par=par, kc=kc,
                                      ns=ns, st_=st_, sp_=sp_: e.matmul(
                                          pg[par][:, ns * 512:(ns + 1) * 512],
                                          s1b[par][:, kc, :],
                                          x2t[:, kc, ns * 512:(ns + 1) * 512],
                                          start=st_, stop=sp_),
                                      reads=[("s1b", par)] + x2t_kc[kc],
                                      writes=[("pg", par)] if (st_ or sp_) else ())
                                kb.op("tensor", lambda e, par=par, kc=kc,
                                      ns=ns, st_=st_, sp_=sp_: e.matmul(
                                          pu[par][:, ns * 512:(ns + 1) * 512],
                                          s2b[par][:, kc, :],
                                          x2t[:, kc, ns * 512:(ns + 1) * 512],
                                          start=st_, stop=sp_),
                                      reads=[("s2b", par)] + x2t_kc[kc],
                                      writes=[("pu", par)] if (st_ or sp_) else ())
                        kb.op("scalar", lambda e, par=par: e.activation(
                            sg[:, :], pg[par][:, :], AF.Silu),
                            reads=[("pg", par)], writes=[("f_sg",)])
                        kb.op("vector", lambda e, par=par, i=i:
                              e.tensor_tensor(ht[:, i, :], sg[:, :],
                                              pu[par][:, :], ALU.mult),
                              reads=[("f_sg",), ("pg", par), ("pu", par)],
                              writes=[("ht", i)])
                    kb.barrier()

                with contextlib.ExitStack() as ps:
                    pcb = [ps.enter_context(
                        _sb(f"f_pc{i}", [128, 4, 512], BF16))
                        for i in range(2)]
                    o1c = [ps.enter_context(_sb(f"f_o1c{i}", [128, 512], F32))
                           for i in range(2)]
                    ob = [ps.enter_context(_sb(f"f_ob{i}", [128, 512], F32))
                          for i in range(2)]
                    pd = [ps.enter_context(_pt(f"f_pd{i}", [128, 512], F32))
                          for i in range(8)]
                    nkb = (43 + 3) // 4
                    for n in range(8):
                        for mg in range(2):
                            for kbatch in range(nkb):
                                ks = list(range(kbatch * 4,
                                                min(43, kbatch * 4 + 4)))
                                bp = kbatch % 2
                                for ki, k in enumerate(ks):
                                    kb.op("gpsimd",
                                          lambda e, bp=bp, ki=ki, k=k, n=n,
                                          hph=hph: e.dma_start(
                                              out=pcb[bp][:, ki, :],
                                              in_=piece_ap(hph * 43 + k, n)),
                                          reads=[("dqa3",)],
                                          writes=[("pcb", bp, ki)], dma=True)
                                for ki, k in enumerate(ks):
                                    for m in range(4):
                                        bank = mg * 4 + m
                                        st_ = (kbatch == 0 and ki == 0)
                                        sp_ = (kbatch == nkb - 1 and
                                               ki == len(ks) - 1)
                                        mslice = (mg * 4 + m) * 128
                                        kb.op("tensor",
                                              lambda e, bp=bp, ki=ki, k=k,
                                              m=m, bank=bank, st_=st_,
                                              sp_=sp_, mslice=mslice:
                                              e.matmul(
                                                  pd[bank][:, :],
                                                  ht[:, k,
                                                     mslice:mslice + 128],
                                                  pcb[bp][:, ki, :],
                                                  start=st_, stop=sp_),
                                              reads=[("ht", k),
                                                     ("pcb", bp, ki)],
                                              writes=[("pd", bank)] if (st_ or sp_) else ())
                            srcd = out1_d if hph == 0 else part_d
                            for m in range(4):
                                bank = mg * 4 + m
                                bp = m % 2
                                off = (mg * 512 + m * 128) * D + n * 512
                                kb.op("sync", lambda e, bp=bp, off=off,
                                      srcd=srcd: e.dma_start(
                                          out=o1c[bp][:, :],
                                          in_=dram_ap(srcd, off,
                                                      [[D, 128], [1, 512]])),
                                      reads=out1_res if hph == 0 else [("part",)],
                                      writes=[("o1c", bp)], dma=True)
                                kb.op("vector", lambda e, bp=bp, bank=bank:
                                      e.tensor_tensor(
                                          ob[bp][:, :], pd[bank][:, :],
                                          o1c[bp][:, :], ALU.add),
                                      reads=[("pd", bank), ("o1c", bp)],
                                      writes=[("ob", bp)])
                                if hph == 0:
                                    kb.op("sync", lambda e, bp=bp, off=off:
                                          e.dma_start(
                                              out=dram_ap(part_d, off,
                                                          [[D, 128],
                                                           [1, 512]]),
                                              in_=ob[bp][:, :]),
                                          reads=[("ob", bp)],
                                          writes=[("part",)], dma=True)
                                else:
                                    kb.op("sync", lambda e, bp=bp, off=off:
                                          e.dma_start(
                                              out=dram_ap(out_d, off,
                                                          [[D, 128],
                                                           [1, 512]]),
                                              in_=ob[bp][:, :]),
                                          reads=[("ob", bp)],
                                          writes=[("outd", off)], dma=True)
                    kb.barrier()

        kb.barrier()

        with nc.Block() as block:
            kb.build(block)

    return nc


_NC_CACHE = {}


def kernel(L0_attn_x, L0_ffn_w1, L0_ffn_w2, L0_ffn_w3):
    x = np.ascontiguousarray(np.asarray(L0_attn_x, np.float32).reshape(8192, D))
    w1 = np.asarray(L0_ffn_w1, np.float32)
    w2 = np.asarray(L0_ffn_w2, np.float32)
    w3 = np.asarray(L0_ffn_w3, np.float32)

    w1p = np.zeros((HPAD, D), np.float32)
    w1p[:H] = w1
    w2p = np.zeros((HPAD, D), np.float32)
    w2p[:H] = w2
    w3p = np.zeros((D, HPAD), np.float32)
    w3p[:, :H] = w3

    Cc, Sc, CiCc, CiSc, ident, bcol = _host_consts()

    if "nc" not in _NC_CACHE:
        _NC_CACHE["nc"] = build_program()
    nc = _NC_CACHE["nc"]

    in_maps = []
    for c in range(NCORES):
        in_maps.append({
            "x_in": np.ascontiguousarray(x[c * RPC:(c + 1) * RPC]),
            "w1s": np.ascontiguousarray(w1p[c * 1408:(c + 1) * 1408]),
            "w2s": np.ascontiguousarray(w2p[c * 1408:(c + 1) * 1408]),
            "w3s": np.ascontiguousarray(w3p[c * 512:(c + 1) * 512]),
            "Cc": Cc, "Sc": Sc, "CiC": CiCc, "CiS": CiSc,
            "idn": ident, "bcol": bcol,
        })
    res = run_bass_kernel_spmd(nc, in_maps, core_ids=list(range(NCORES)))
    out = np.concatenate([np.asarray(r["out"]) for r in res.results], axis=0)
    return out.reshape(4, 2048, D)
